# version 02
# baseline (speedup 1.0000x reference)
import math
import numpy as np
import ml_dtypes
import concourse.bass as bass
import concourse.mybir as mybir
from concourse.bass_utils import run_bass_kernel_spmd

F32 = mybir.dt.float32
BF16 = mybir.dt.bfloat16
I32 = mybir.dt.int32
AF = mybir.ActivationFunctionType
ALU = mybir.AluOpType
AX = mybir.AxisListType
ts = bass.ts
BF = ml_dtypes.bfloat16

NCORES = 8
D = 1024
B = 4
JD = D // 128


class Sem:
    def __init__(self, nc, name):
        self.h = nc.semaphore(name).__enter__()
        self.val = 0

    def inc(self, inst, n=1):
        inst.then_inc(self.h, n)
        self.val += n
        return self.val

    def wait(self, eng, val=None):
        eng.wait_ge(self.h, self.val if val is None else val)


def sb(nc, name, shape, dtype):
    return nc.sbuf_tensor(name, shape, dtype).__enter__()


def ps(nc, name, shape, dtype=F32):
    return nc.psum_tensor(name, shape, dtype).__enter__()


def din(nc, name, shape, dtype=F32):
    return nc.dram_tensor(name, shape, dtype, kind="ExternalInput").ap()


def dout(nc, name, shape, dtype=F32):
    return nc.dram_tensor(name, shape, dtype, kind="ExternalOutput").ap()


def dint(nc, name, shape, dtype=F32):
    return nc.dram_tensor(name, shape, dtype, kind="Internal").ap()



def build_prog_A(TLOC, phases=4):
    TL = B * TLOC
    TLB = TL + 4
    NT = (TLB + 127) // 128
    NTC = TL // 512 if TL >= 512 else 1
    TCW = min(512, TL)

    nc = bass.Bass("TRN2", target_bir_lowering=False, debug=False)
    xa = din(nc, "xa", [TLB, D])
    mixes = din(nc, "mixes", [6, 128, JD])
    wts = din(nc, "wts", [6, D, D])
    identb = din(nc, "identb", [128, 128], BF16)
    identf = din(nc, "identf", [128, 128], F32)
    xfm_o = dout(nc, "xfm", [JD, 128, TL])
    rg_o = dout(nc, "rg", [2, JD, 128, TL], BF16)
    sp_o = dout(nc, "sp", [2, JD, 128, TL])
    kva_s = dint(nc, "kva_s", [3, JD, 128, TL], BF16)
    dec_s = dint(nc, "dec_s", [JD, 128, TL])

    xt = sb(nc, "xt", [128, D], F32)
    xnt = sb(nc, "xnt", [128, D], F32)
    xnb = sb(nc, "xnb", [128, D], BF16)
    lnm1 = sb(nc, "lnm1", [128, 1], F32)
    lnm2 = sb(nc, "lnm2", [128, 1], F32)
    lnsq = sb(nc, "lnsq", [128, D], F32)
    idb = sb(nc, "idb", [128, 128], BF16)
    idf = sb(nc, "idf", [128, 128], F32)
    mix_sb = sb(nc, "mix_sb", [128, 6 * JD], F32)
    xne = sb(nc, "xne", [128, JD * TLB], BF16)
    dbf = sb(nc, "dbf", [128, JD * TL], BF16)
    zbf = sb(nc, "zbf", [128, JD * TL], BF16)
    wtf = sb(nc, "wtf", [128, D], F32)
    wtb = sb(nc, "wtb", [128, 8 * D], BF16)
    acts = sb(nc, "acts", [128, TL], F32)
    actb = sb(nc, "actb", [128, TL], BF16)
    t_a = sb(nc, "t_a", [128, TCW], F32)
    t_r = sb(nc, "t_r", [128, TCW], F32)
    xev = sb(nc, "xev", [128, 128], F32)
    kj = sb(nc, "kj", [128, TL], BF16)
    vj = sb(nc, "vj", [128, TL], BF16)
    aj = sb(nc, "aj", [128, TL], BF16)
    dj = sb(nc, "dj", [128, TL], F32)
    uj = sb(nc, "uj", [128, TL], F32)
    slj = sb(nc, "slj", [128, TL], F32)
    pcj = sb(nc, "pcj", [128, TL], F32)
    ptr = ps(nc, "ptr", [128, 128], BF16)
    ptrf = ps(nc, "ptrf", [128, 128], F32)
    pp = ps(nc, "pp", [128, TCW], F32)
    if True:
        Ld = Sem(nc, "Ld"); St = Sem(nc, "St")
        T = Sem(nc, "T"); V = Sem(nc, "V"); S = Sem(nc, "S")
        v, s, t, g, sy = nc.vector, nc.scalar, nc.tensor, nc.gpsimd, nc.sync

        sy.dma_start(out=idb[:], in_=identb).then_inc(Ld.h, 16)
        sy.dma_start(out=idf[:], in_=identf).then_inc(Ld.h, 16)
        for m in range(6):
            sy.dma_start(out=mix_sb[:, m * JD:(m + 1) * JD], in_=mixes[m]).then_inc(Ld.h, 16)
        Ld.val += 16 * 8
        onesf = acts

        for i in range(NT):
            r0 = 128 * i
            rows = min(128, TLB - r0)
            if rows < 128:
                ei = V.inc(v.memset(xt[:], 0.0))
                V.wait(sy, ei)
            T.wait(sy)
            V.wait(sy)
            sy.dma_start(out=xt[0:rows, :],
                         in_=xa[r0:r0 + rows, :]).then_inc(Ld.h, 16)
            Ld.val += 16
            Ld.wait(v)
            v.tensor_reduce(lnm1[:], xt[:], AX.X, ALU.add)
            v.drain()
            v.tensor_scalar_mul(lnm1[:], lnm1[:], 1.0 / D)
            v.drain()
            v.tensor_scalar(xnt[:], xt[:], lnm1[:], None, ALU.subtract)
            v.drain()
            v.tensor_mul(lnsq[:], xnt[:], xnt[:])
            v.drain()
            v.tensor_reduce(lnm2[:], lnsq[:], AX.X, ALU.add)
            v.drain()
            ei = V.inc(v.tensor_scalar(lnm2[:], lnm2[:], 1.0 / D, 1e-5, ALU.mult, ALU.add))
            V.wait(s, ei)
            ei = S.inc(s.activation(lnm2[:], lnm2[:], AF.Sqrt))
            S.wait(v, ei)
            v.reciprocal(lnm2[:], lnm2[:])
            v.drain()
            v.tensor_scalar(xnt[:], xnt[:], lnm2[:], None, ALU.mult)
            v.drain()
            ei = V.inc(v.tensor_copy(xnb[:], xnt[:]))
            V.wait(t, ei)
            for j in range(JD):
                ei = T.inc(t.transpose(ptr[:], xnb[:, 128 * j:128 * (j + 1)], idb[:]))
                T.wait(v, ei)
                ei = V.inc(v.tensor_copy(xne[:, j * TLB + r0: j * TLB + r0 + rows],
                                         ptr[0:128, 0:rows]))
                V.wait(t, ei)
                c0 = max(r0, 4)
                n_real = r0 + rows - c0
                if n_real <= 0:
                    continue
                ei = T.inc(t.transpose(ptrf[:], xt[:, 128 * j:128 * (j + 1)], idf[:]))
                T.wait(v, ei)
                ei = V.inc(v.tensor_copy(xev[:, 0:rows], ptrf[:, 0:rows]))
                V.wait(g, ei)
                g.dma_start(out=xfm_o[j, :, c0 - 4: c0 - 4 + n_real],
                            in_=xev[:, c0 - r0: c0 - r0 + n_real]).then_inc(St.h, 16)
                St.val += 16
                V.wait(t, ei)
                St.wait(v)

        if phases < 2:
            St.wait(g)
            return nc
        St.wait(v)
        v.drain()
        for j in range(JD):
            v.tensor_tensor(dbf[:, j * TL:(j + 1) * TL],
                            xne[:, j * TLB: j * TLB + TL],
                            xne[:, j * TLB + 4: j * TLB + 4 + TL], ALU.subtract)
        ei = V.inc(v.drain())

        import os as _os
        WNAMES = ["r", "w", "k", "v", "a", "g"][:int(_os.environ.get("NMIX", "6"))]
        for m, wn in enumerate(WNAMES):
            T.wait(sy)
            for k in range(JD):
                if k > 0:
                    V.wait(sy)
                sy.dma_start(out=wtf[:],
                             in_=wts[m, 128 * k:128 * (k + 1), :]).then_inc(Ld.h, 16)
                Ld.val += 16
                Ld.wait(v)
                ei = V.inc(v.tensor_copy(wtb[:, k * D:(k + 1) * D], wtf[:]))
            T.wait(v)
            for j in range(JD):
                v.scalar_tensor_tensor(zbf[:, j * TL:(j + 1) * TL],
                                       dbf[:, j * TL:(j + 1) * TL],
                                       mix_sb[:, m * JD + j: m * JD + j + 1],
                                       xne[:, j * TLB + 4: j * TLB + 4 + TL],
                                       ALU.mult, ALU.add)
            ei = V.inc(v.drain())
            V.wait(t, ei)
            for m8 in range(JD):
                for tc in range(NTC):
                    c0 = tc * TCW
                    for k in range(JD):
                        mm = t.matmul(pp[:, 0:TCW],
                                      wtb[:, k * D + 128 * m8: k * D + 128 * (m8 + 1)],
                                      zbf[:, k * TL + c0: k * TL + c0 + TCW],
                                      start=(k == 0), stop=(k == JD - 1))
                    ei = T.inc(mm)
                    if wn == "r":
                        T.wait(s, ei)
                        ei = S.inc(s.activation(actb[:, c0:c0 + TCW], pp[:, 0:TCW], AF.Sigmoid))
                        S.wait(t, ei)
                        S.wait(g, ei)
                    elif wn == "k":
                        T.wait(s, ei)
                        ei = S.inc(s.activation(actb[:, c0:c0 + TCW], pp[:, 0:TCW], AF.Tanh))
                        S.wait(t, ei)
                        S.wait(g, ei)
                    elif wn in ("a", "g"):
                        T.wait(s, ei)
                        ei = S.inc(s.activation(actb[:, c0:c0 + TCW], pp[:, 0:TCW], AF.Sigmoid))
                        S.wait(t, ei)
                        S.wait(g, ei)
                    elif wn == "v":
                        T.wait(v, ei)
                        ei = V.inc(v.tensor_copy(actb[:, c0:c0 + TCW], pp[:, 0:TCW]))
                        V.wait(t, ei)
                        V.wait(g, ei)
                    elif wn == "w":
                        T.wait(s, ei)
                        ei_abs = S.inc(s.activation(t_a[:], pp[:, 0:TCW], AF.Abs))
                        s.drain()
                        s.activation(t_a[:], t_a[:], AF.Exp, scale=-1.0)
                        s.drain()
                        ei_ln = S.inc(s.activation(t_a[:], t_a[:], AF.Ln, bias=1.0))
                        S.wait(v, ei_abs)
                        v.tensor_relu(t_r[:], pp[:, 0:TCW])
                        v.drain()
                        S.wait(v, ei_ln)
                        ei_add = V.inc(v.tensor_add(t_a[:], t_a[:], t_r[:]))
                        V.wait(s, ei_add)
                        V.wait(t, ei_add)
                        ei = S.inc(s.activation(acts[:, c0:c0 + TCW], t_a[:], AF.Exp, scale=-1.0))
                        S.wait(t, ei)
                        S.wait(g, ei)
                if wn == "r":
                    g.dma_start(out=rg_o[0, m8], in_=actb[:, 0:TL]).then_inc(St.h, 16)
                elif wn == "g":
                    g.dma_start(out=rg_o[1, m8], in_=actb[:, 0:TL]).then_inc(St.h, 16)
                elif wn == "k":
                    g.dma_start(out=kva_s[0, m8], in_=actb[:, 0:TL]).then_inc(St.h, 16)
                elif wn == "v":
                    g.dma_start(out=kva_s[1, m8], in_=actb[:, 0:TL]).then_inc(St.h, 16)
                elif wn == "a":
                    g.dma_start(out=kva_s[2, m8], in_=actb[:, 0:TL]).then_inc(St.h, 16)
                elif wn == "w":
                    g.dma_start(out=dec_s[m8], in_=acts[:, 0:TL]).then_inc(St.h, 16)
                St.val += 16
                St.wait(s)
                St.wait(v)

        if phases < 4:
            St.wait(g)
            return nc
        St.wait(v)
        v.memset(onesf[:], 1.0)
        v.drain()
        for j in range(JD):
            St.wait(sy)
            V.wait(sy)
            sy.dma_start(out=kj[:], in_=kva_s[0, j]).then_inc(Ld.h, 16)
            sy.dma_start(out=vj[:], in_=kva_s[1, j]).then_inc(Ld.h, 16)
            sy.dma_start(out=aj[:], in_=kva_s[2, j]).then_inc(Ld.h, 16)
            sy.dma_start(out=dj[:], in_=dec_s[j]).then_inc(Ld.h, 16)
            Ld.val += 64
            Ld.wait(v)
            v.tensor_mul(uj[:], aj[:], kj[:])
            v.drain()
            v.tensor_mul(uj[:], uj[:], vj[:])
            v.drain()
            for b in range(B):
                v.tensor_tensor_scan(slj[:, b:TL:B], dj[:, b:TL:B], uj[:, b:TL:B],
                                     0.0, ALU.mult, ALU.add)
                v.tensor_tensor_scan(pcj[:, b:TL:B], dj[:, b:TL:B], onesf[:, b:TL:B],
                                     1.0, ALU.mult, ALU.mult)
            ei = V.inc(v.drain())
            V.wait(g, ei)
            g.dma_start(out=sp_o[0, j], in_=slj[:]).then_inc(St.h, 16)
            g.dma_start(out=sp_o[1, j], in_=pcj[:]).then_inc(St.h, 16)
            St.val += 32
            St.wait(v)
        St.wait(g)
    return nc



def host_prep_A(x_full, mix_list, w_list, core, TLOC):
    TG_ = x_full.shape[1]
    t0 = TLOC * core
    xs = x_full[:, t0:t0 + TLOC, :]
    xs = np.ascontiguousarray(xs.transpose(1, 0, 2).reshape(B * TLOC, D))
    if t0 == 0:
        bnd = np.zeros((B, D), np.float32)
    else:
        bnd = np.ascontiguousarray(x_full[:, t0 - 1, :])
    xa = np.concatenate([bnd, xs], axis=0)
    mixes = np.stack([m.reshape(JD, 128).T for m in mix_list])
    wts = np.stack([np.ascontiguousarray(w.T) for w in w_list])
    return {
        "xa": xa.astype(np.float32),
        "mixes": mixes.astype(np.float32),
        "wts": wts.astype(np.float32),
        "identb": np.eye(128, dtype=BF),
        "identf": np.eye(128, dtype=np.float32),
    }


def kernel(**inputs):
    raise NotImplementedError("assembling")



def build_prog_B(TLOC):
    TL = B * TLOC
    NTC = max(1, TL // 512)
    TCW = min(512, TL)
    GN_EPS = 0.00064

    nc = bass.Bass("TRN2", target_bir_lowering=False, debug=False)
    xfm_i = din(nc, "xfm_i", [JD, 128, TL])
    rg_i = din(nc, "rg_i", [2, JD, 128, TL], BF16)
    sp_i = din(nc, "sp_i", [2, JD, 128, TL])
    pref = din(nc, "pref", [7, 2, 128, JD * B])
    gnwb = din(nc, "gnwb", [2, 128, JD])
    woutT = din(nc, "woutT", [D, D])
    bout = din(nc, "bout", [1, D])
    gsel = din(nc, "gsel", [128, 2], BF16)
    sel16 = din(nc, "sel16", [JD, 16, 128], BF16)
    onesr = din(nc, "onesr", [1, TL])
    xout_o = dout(nc, "xout", [JD, 128, TL])

    prf = sb(nc, "prf", [128, 7 * 2 * JD * B], F32)
    sin_sb = sb(nc, "sin_sb", [128, JD * B], F32)
    tmp_sb = sb(nc, "tmp_sb", [128, JD * B], F32)
    gnw_sb = sb(nc, "gnw_sb", [128, 2 * JD], F32)
    gsel_sb = sb(nc, "gsel_sb", [128, 2], BF16)
    sel16_sb = sb(nc, "sel16_sb", [16, JD * 128], BF16)
    ones_bf = sb(nc, "ones_bf", [1, TL], BF16)
    onesf_sb = sb(nc, "onesf_sb", [1, TL], F32)
    bout_bf = sb(nc, "bout_bf", [1, D], BF16)
    bout_f = sb(nc, "bout_f", [1, D], F32)
    slj = sb(nc, "slj", [128, TL], F32)
    pcj = sb(nc, "pcj", [128, TL], F32)
    rj = sb(nc, "rj", [128, TL], BF16)
    gj = sb(nc, "gj", [128, TL], BF16)
    mxb = sb(nc, "mxb", [128, JD * TL], BF16)
    sqb = sb(nc, "sqb", [128, TCW], BF16)
    gst = sb(nc, "gst", [16, 2 * TL], F32)
    pg_sb = sb(nc, "pg_sb", [2, 2 * 512], F32)
    g16 = sb(nc, "g16", [16, 2 * TL], BF16)
    wtf = sb(nc, "wtf", [128, D], F32)
    wtb = sb(nc, "wtb", [128, 8 * D], BF16)
    xfm = sb(nc, "xfm", [128, TL], F32)
    t_x = sb(nc, "t_x", [128, TCW], F32)
    t_y = sb(nc, "t_y", [128, TCW], F32)
    pg = ps(nc, "pg", [2, TCW], F32)
    pg2 = ps(nc, "pg2", [2, TCW], F32)
    pbc = ps(nc, "pbc", [128, TCW], F32)
    pbc2 = ps(nc, "pbc2", [128, TCW], F32)
    pp = ps(nc, "pp", [128, TCW], F32)

    Ld = Sem(nc, "Ld"); Ld2 = Sem(nc, "Ld2"); St = Sem(nc, "St"); Sq = Sem(nc, "Sq")
    T = Sem(nc, "T"); V = Sem(nc, "V"); S = Sem(nc, "S")
    v, s, t, g, sy = nc.vector, nc.scalar, nc.tensor, nc.gpsimd, nc.sync

    for k in range(7):
        for q in range(2):
            sy.dma_start(out=prf[:, (k * 2 + q) * JD * B:(k * 2 + q + 1) * JD * B],
                         in_=pref[k, q]).then_inc(Ld.h, 16)
    sy.dma_start(out=gnw_sb[:, 0:JD], in_=gnwb[0]).then_inc(Ld.h, 16)
    sy.dma_start(out=gnw_sb[:, JD:2 * JD], in_=gnwb[1]).then_inc(Ld.h, 16)
    sy.dma_start(out=gsel_sb[:], in_=gsel).then_inc(Ld.h, 16)
    for j in range(JD):
        sy.dma_start(out=sel16_sb[0:16, j * 128:(j + 1) * 128],
                     in_=sel16[j]).then_inc(Ld.h, 16)
    sy.dma_start(out=onesf_sb[:], in_=onesr).then_inc(Ld.h, 16)
    sy.dma_start(out=bout_f[:], in_=bout).then_inc(Ld.h, 16)
    Ld.val += 16 * (14 + 2 + 1 + JD + 1 + 1)
    Ld.wait(v)
    v.tensor_copy(ones_bf[:], onesf_sb[:])
    v.drain()
    v.tensor_copy(bout_bf[:], bout_f[:])
    v.drain()
    v.memset(sin_sb[:], 0.0)
    v.drain()
    for k in range(7):
        v.tensor_mul(tmp_sb[:], sin_sb[:], prf[:, (k * 2) * JD * B:(k * 2 + 1) * JD * B])
        v.drain()
        v.tensor_add(sin_sb[:], tmp_sb[:], prf[:, (k * 2 + 1) * JD * B:(k * 2 + 2) * JD * B])
        v.drain()

    for j in range(JD):
        St.wait(sy); V.wait(sy); S.wait(sy)
        Ld2.wait(sy)
        sy.dma_start(out=slj[:], in_=sp_i[0, j]).then_inc(Ld2.h, 16)
        sy.dma_start(out=pcj[:], in_=sp_i[1, j]).then_inc(Ld2.h, 16)
        sy.dma_start(out=rj[:], in_=rg_i[0, j]).then_inc(Ld2.h, 16)
        Ld2.val += 48
        Ld2.wait(v)
        for b in range(B):
            v.scalar_tensor_tensor(slj[:, b:TL:B], pcj[:, b:TL:B],
                                   sin_sb[:, B * j + b: B * j + b + 1],
                                   slj[:, b:TL:B], ALU.mult, ALU.add)
        v.drain()
        ei = V.inc(v.tensor_mul(mxb[:, j * TL:(j + 1) * TL], rj[:], slj[:]))
        v.drain()
        for tc in range(NTC):
            c0 = tc * TCW
            V.wait(s, ei)
            eis = S.inc(s.activation(sqb[:], mxb[:, j * TL + c0: j * TL + c0 + TCW],
                                     AF.Square))
            V.wait(t, ei)
            S.wait(t, eis)
            mm = t.matmul(pg[0:2, 0:TCW], gsel_sb[:], mxb[:, j * TL + c0: j * TL + c0 + TCW],
                          start=True, stop=True)
            mm2 = t.matmul(pg2[0:2, 0:TCW], gsel_sb[:], sqb[:], start=True, stop=True)
            eit = T.inc(mm2)
            T.wait(v, eit)
            v.tensor_copy(pg_sb[0:2, 0:TCW], pg[0:2, 0:TCW])
            v.drain()
            ei2 = V.inc(v.tensor_copy(pg_sb[0:2, TCW:2 * TCW], pg2[0:2, 0:TCW]))
            V.wait(t, ei2)
            V.wait(g, ei2)
            g.dma_start(out=gst[2 * j:2 * j + 2, c0:c0 + TCW], in_=pg_sb[0:2, 0:TCW]).then_inc(Sq.h, 16)
            g.dma_start(out=gst[2 * j:2 * j + 2, TL + c0:TL + c0 + TCW], in_=pg_sb[0:2, TCW:2 * TCW]).then_inc(Sq.h, 16)
            Sq.val += 32
            Sq.wait(v)
            Sq.wait(s)

    Sq.wait(v)
    NG = 64.0
    v.tensor_scalar_mul(gst[0:16, 0:TL], gst[0:16, 0:TL], 1.0 / NG)
    v.drain()
    v.tensor_scalar_mul(gst[0:16, TL:2 * TL], gst[0:16, TL:2 * TL], 1.0 / NG)
    v.drain()
    v.tensor_mul(g16[0:16, 0:TL], gst[0:16, 0:TL], gst[0:16, 0:TL])
    v.drain()
    v.tensor_sub(gst[0:16, TL:2 * TL], gst[0:16, TL:2 * TL], g16[0:16, 0:TL])
    v.drain()
    v.tensor_scalar_add(gst[0:16, TL:2 * TL], gst[0:16, TL:2 * TL], GN_EPS)
    ei = V.inc(v.drain())
    V.wait(s, ei)
    ei = S.inc(s.activation(gst[0:16, TL:2 * TL], gst[0:16, TL:2 * TL], AF.Sqrt))
    S.wait(v, ei)
    v.reciprocal(gst[0:16, TL:2 * TL], gst[0:16, TL:2 * TL])
    v.drain()
    v.tensor_copy(g16[0:16, :], gst[0:16, :])
    ei = V.inc(v.drain())
    V.wait(t, ei)

    for j in range(JD):
        V.wait(sy)
        Ld2.wait(sy)
        sy.dma_start(out=gj[:], in_=rg_i[1, j]).then_inc(Ld2.h, 16)
        Ld2.val += 16
        for tc in range(NTC):
            c0 = tc * TCW
            sl = slice(j * TL + c0, j * TL + c0 + TCW)
            mm = t.matmul(pbc[:, 0:TCW], sel16_sb[0:16, j * 128:(j + 1) * 128],
                          g16[0:16, c0:c0 + TCW], start=True, stop=True)
            mm2 = t.matmul(pbc2[:, 0:TCW], sel16_sb[0:16, j * 128:(j + 1) * 128],
                           g16[0:16, TL + c0:TL + c0 + TCW], start=True, stop=True)
            eit = T.inc(mm2)
            T.wait(v, eit)
            Ld2.wait(v)
            v.tensor_sub(t_x[:], mxb[:, sl], pbc[:, 0:TCW])
            v.drain()
            v.tensor_mul(t_x[:], t_x[:], pbc2[:, 0:TCW])
            v.drain()
            v.tensor_scalar(t_x[:], t_x[:], gnw_sb[:, j:j + 1],
                            gnw_sb[:, JD + j:JD + j + 1], ALU.mult, ALU.add)
            v.drain()
            ei = V.inc(v.tensor_mul(mxb[:, sl], t_x[:], gj[:, c0:c0 + TCW]))
            V.wait(t, ei)

    for k in range(JD):
        if k > 0:
            V.wait(sy)
        T.wait(sy)
        Ld2.wait(sy)
        sy.dma_start(out=wtf[:], in_=woutT[128 * k:128 * (k + 1), :]).then_inc(Ld2.h, 16)
        Ld2.val += 16
        Ld2.wait(v)
        V.inc(v.tensor_copy(wtb[:, k * D:(k + 1) * D], wtf[:]))
    ei = V.inc(v.drain())
    V.wait(t, ei)
    for m8 in range(JD):
        V.wait(sy)
        Ld2.wait(sy)
        sy.dma_start(out=xfm[:], in_=xfm_i[m8]).then_inc(Ld2.h, 16)
        Ld2.val += 16
        for tc in range(NTC):
            c0 = tc * TCW
            for k in range(JD):
                mm = t.matmul(pp[:, 0:TCW],
                              wtb[:, k * D + 128 * m8: k * D + 128 * (m8 + 1)],
                              mxb[:, k * TL + c0: k * TL + c0 + TCW],
                              start=(k == 0), stop=False)
            mm = t.matmul(pp[:, 0:TCW], bout_bf[:, 128 * m8:128 * (m8 + 1)],
                          ones_bf[:, c0:c0 + TCW], start=False, stop=True)
            eit = T.inc(mm)
            T.wait(v, eit)
            Ld2.wait(v)
            ei = V.inc(v.tensor_add(t_y[:], pp[:, 0:TCW],
                                    xfm[:, c0:c0 + TCW]))
            V.wait(t, ei)
            V.wait(g, ei)
            g.wait_ge(St.h, St.val)
            g.dma_start(out=xout_o[m8, :, c0:c0 + TCW], in_=t_y[:]).then_inc(St.h, 16)
            St.val += 16
            St.wait(v)
    St.wait(g)
    return nc


def host_prep_B(outA_list, gn_w, gn_b, W_out, b_out, core, TLOC):
    TL = B * TLOC
    myA = outA_list[core]
    pref = np.zeros((7, 2, 128, JD * B), np.float32)
    pref[:, 0, :, :] = 1.0
    for k in range(core):
        sp = outA_list[k]["sp"]
        pf = sp[1][:, :, TL - B:TL]
        sf = sp[0][:, :, TL - B:TL]
        pref[k, 0] = pf.transpose(1, 0, 2).reshape(128, JD * B)
        pref[k, 1] = sf.transpose(1, 0, 2).reshape(128, JD * B)
    gnwb = np.stack([gn_w.reshape(JD, 128).T, gn_b.reshape(JD, 128).T])
    gsel = np.zeros((128, 2), BF)
    gsel[0:64, 0] = 1; gsel[64:128, 1] = 1
    sel16 = np.zeros((JD, 16, 128), BF)
    for j in range(JD):
        sel16[j, 2 * j, 0:64] = 1
        sel16[j, 2 * j + 1, 64:128] = 1
    return {
        "xfm_i": myA["xfm"], "rg_i": myA["rg"], "sp_i": myA["sp"],
        "pref": pref,
        "gnwb": gnwb.astype(np.float32),
        "woutT": np.ascontiguousarray(W_out.T).astype(np.float32),
        "bout": b_out.reshape(1, D).astype(np.float32),
        "gsel": gsel, "sel16": sel16,
        "onesr": np.ones((1, TL), np.float32),
    }



def build_prog_C(TLOC):
    TL = B * TLOC
    NTC = max(1, TL // 512)
    TCW = min(512, TL)
    E = 8
    H = 4 * D
    HC = H // 128
    HQ = HC // 4

    nc = bass.Bass("TRN2", target_bir_lowering=False, debug=False)
    x_i = din(nc, "x_i", [JD, 128, TL])
    gate_w = din(nc, "gate_w", [D, E])
    gate_b = din(nc, "gate_b", [1, E])
    w1_i = din(nc, "w1", [E, D, H])
    b1_i = din(nc, "b1", [E, 1, H])
    w2_i = din(nc, "w2", [E, H, D])
    b2_i = din(nc, "b2", [E, 1, D])
    identf = din(nc, "identf", [128, 128], F32)
    xout_o = dout(nc, "xout", [JD, 128, TL])

    uni = sb(nc, "uni", [128, JD * TCW], F32)
    xjs = uni
    xnb = sb(nc, "xnb", [128, JD * TL], BF16)
    xb_c = sb(nc, "xb_c", [128, TCW], BF16)
    sq_c = sb(nc, "sq_c", [128, TCW], BF16)
    onescol = sb(nc, "onescol", [128, 1], BF16)
    ones_row = sb(nc, "ones_row", [1, TL], BF16)
    st_f = sb(nc, "st_f", [1, 4 * TCW], F32)
    st_b = sb(nc, "st_b", [1, 2 * TCW], BF16)
    gw_sb = sb(nc, "gw_sb", [128, JD * E], BF16)
    gb_sb = sb(nc, "gb_sb", [1, E], BF16)
    gb_f = sb(nc, "gb_f", [1, E], F32)
    idf = sb(nc, "idf", [128, 128], F32)
    lg = sb(nc, "lg", [8, TCW], F32)
    lgt = sb(nc, "lgt", [128, E], F32)
    gm1 = sb(nc, "gm1", [128, 1], F32)
    gm8 = sb(nc, "gm8", [128, E], F32)
    wmap = sb(nc, "wmap", [128, E], F32)
    wfm = sb(nc, "wfm", [8, NTC * TCW], F32)
    wfmb = sb(nc, "wfmb", [8, NTC * TCW], BF16)
    wrow = sb(nc, "wrow", [1, TCW], F32)
    wrow_b = sb(nc, "wrow_b", [1, TCW], BF16)
    wbc = sb(nc, "wbc", [128, TCW], BF16)
    w1b = sb(nc, "w1b", [128, JD * 1024], BF16)
    w2b = sb(nc, "w2b", [128, HQ * D], BF16)
    wtf = sb(nc, "wtf", [128, 1024], F32)
    b1b = sb(nc, "b1b", [1, H], BF16)
    b2b = sb(nc, "b2b", [8, D], BF16)
    b2f = sb(nc, "b2f", [8, D], F32)
    hbuf = uni.bitcast(BF16)
    htmp = sb(nc, "htmp", [128, TCW], BF16)
    oacc = sb(nc, "oacc", [128, JD * TL], F32)
    t_y = sb(nc, "t_y", [128, TCW], F32)
    pA = ps(nc, "pA", [128, TCW], F32)
    pB = ps(nc, "pB", [128, TCW], F32)
    pC = ps(nc, "pC", [128, TCW], F32)
    pD = ps(nc, "pD", [128, TCW], F32)
    pb1 = pA; pb2 = pB; pbc = pC; pbc2 = pD
    pl = pA; ptt = pB; pth = pC
    ph = pB; po = pC

    Ld = Sem(nc, "Ld"); Lw = Sem(nc, "Lw"); St = Sem(nc, "St")
    T = Sem(nc, "T"); V = Sem(nc, "V"); S = Sem(nc, "S")
    v, s, t, g, sy = nc.vector, nc.scalar, nc.tensor, nc.gpsimd, nc.sync

    sy.dma_start(out=idf[:], in_=identf).then_inc(Ld.h, 16)
    Ld.val += 16
    for j in range(JD):
        sy.dma_start(out=wtf[:, 0:E], in_=gate_w[128 * j:128 * (j + 1), :]).then_inc(Ld.h, 16)
        Ld.val += 16
        Ld.wait(v)
        V.inc(v.tensor_copy(gw_sb[:, j * E:(j + 1) * E], wtf[:, 0:E]))
        V.wait(sy)
    sy.dma_start(out=gb_f[:], in_=gate_b).then_inc(Ld.h, 16)
    Ld.val += 16
    Ld.wait(v)
    v.memset(onescol[:], 1.0)
    v.memset(ones_row[:], 1.0)
    v.tensor_copy(gb_sb[:], gb_f[:])
    v.memset(oacc[:], 0.0)
    ei = V.inc(v.drain())
    V.wait(t, ei)

    for tc in range(NTC):
        c0 = tc * TCW
        V.wait(sy); S.wait(sy); T.wait(sy)
        Ld.wait(sy)
        for j in range(JD):
            sy.dma_start(out=xjs[:, j * TCW:(j + 1) * TCW],
                         in_=x_i[j][:, c0:c0 + TCW]).then_inc(Ld.h, 16)
        Ld.val += 16 * JD
        Ld.wait(v); Ld.wait(s)
        for j in range(JD):
            xsl = xjs[:, j * TCW:(j + 1) * TCW]
            ei = V.inc(v.tensor_copy(xb_c[:], xsl))
            eis = S.inc(s.activation(sq_c[:], xsl, AF.Square))
            V.wait(t, ei); S.wait(t, eis)
            mm1 = t.matmul(pb1[0:1, 0:TCW], onescol[:], xb_c[:], start=(j == 0), stop=(j == JD - 1))
            mm2 = t.matmul(pb2[0:1, 0:TCW], onescol[:], sq_c[:], start=(j == 0), stop=(j == JD - 1))
            eit = T.inc(mm2)
            T.wait(v, eit)
            T.wait(s, eit)
        v.tensor_scalar_mul(st_f[:, 0:TCW], pb1[0:1, 0:TCW], 1.0 / D)
        v.drain()
        v.tensor_scalar_mul(st_f[:, TCW:2 * TCW], pb2[0:1, 0:TCW], 1.0 / D)
        v.drain()
        v.tensor_mul(st_f[:, 3 * TCW:4 * TCW], st_f[:, 0:TCW], st_f[:, 0:TCW])
        v.drain()
        v.tensor_sub(st_f[:, TCW:2 * TCW], st_f[:, TCW:2 * TCW], st_f[:, 3 * TCW:4 * TCW])
        v.drain()
        v.tensor_scalar_add(st_f[:, TCW:2 * TCW], st_f[:, TCW:2 * TCW], 1e-5)
        ei = V.inc(v.drain())
        V.wait(s, ei)
        ei = S.inc(s.activation(st_f[:, 2 * TCW:3 * TCW], st_f[:, TCW:2 * TCW], AF.Sqrt))
        S.wait(v, ei)
        v.reciprocal(st_f[:, 2 * TCW:3 * TCW], st_f[:, 2 * TCW:3 * TCW])
        v.drain()
        v.tensor_copy(st_b[:, 0:TCW], st_f[:, 0:TCW])
        v.drain()
        ei = V.inc(v.tensor_copy(st_b[:, TCW:2 * TCW], st_f[:, 2 * TCW:3 * TCW]))
        V.wait(t, ei)
        mm1 = t.matmul(pbc[:, 0:TCW], onescol[0:1, 0:1].bitcast(BF16) if False else ones_row[:, 0:128],
                       st_b[:, 0:TCW], start=True, stop=True) if False else None
        mm1 = t.matmul(pbc[:, 0:TCW], ones_row[:, 0:128], st_b[:, 0:TCW], start=True, stop=True)
        mm2 = t.matmul(pbc2[:, 0:TCW], ones_row[:, 0:128], st_b[:, TCW:2 * TCW], start=True, stop=True)
        eit = T.inc(mm2)
        T.wait(v, eit)
        for j in range(JD):
            v.tensor_sub(t_y[:], xjs[:, j * TCW:(j + 1) * TCW], pbc[:, 0:TCW])
            v.drain()
            v.tensor_mul(xnb[:, j * TL + c0: j * TL + c0 + TCW], t_y[:], pbc2[:, 0:TCW])
            v.drain()
        ei = V.inc(v.drain())
        V.wait(t, ei)
        for j in range(JD):
            mm = t.matmul(pl[0:8, 0:TCW], gw_sb[:, j * E:(j + 1) * E],
                          xnb[:, j * TL + c0: j * TL + c0 + TCW],
                          start=(j == 0), stop=False)
        mm = t.matmul(pl[0:8, 0:TCW], gb_sb[:], ones_row[:, 0:TCW], start=False, stop=True)
        eit = T.inc(mm)
        T.wait(v, eit)
        ei = V.inc(v.tensor_copy(lg[0:8, 0:TCW], pl[0:8, 0:TCW]))
        V.wait(t, ei)
        for sbt in range(TCW // 128):
            mm = t.matmul(ptt[:, 0:E], lg[0:8, 128 * sbt:128 * (sbt + 1)], idf[0:8, 0:E],
                          start=True, stop=True)
            eit = T.inc(mm)
            T.wait(v, eit)
            v.tensor_copy(lgt[:], ptt[:, 0:E])
            v.drain()
            v.reduce_max(gm1[:], lgt[:], AX.X)
            v.drain()
            ei = V.inc(v.tensor_scalar(lgt[:], lgt[:], gm1[:], None, ALU.subtract))
            V.wait(s, ei)
            ei = S.inc(s.activation(lgt[:], lgt[:], AF.Exp))
            S.wait(v, ei)
            v.tensor_reduce(gm1[:], lgt[:], AX.X, ALU.add)
            v.drain()
            v.reciprocal(gm1[:], gm1[:])
            v.drain()
            v.tensor_scalar(lgt[:], lgt[:], gm1[:], None, ALU.mult)
            v.drain()
            v.max(out=gm8[:], in_=lgt[:])
            v.drain()
            v.tensor_scalar(wmap[:], lgt[:], gm8[:, 1:2], None, ALU.is_ge)
            v.drain()
            v.tensor_mul(wmap[:], wmap[:], lgt[:])
            v.drain()
            v.tensor_reduce(gm1[:], wmap[:], AX.X, ALU.add)
            v.drain()
            v.reciprocal(gm1[:], gm1[:])
            v.drain()
            ei = V.inc(v.tensor_scalar(wmap[:], wmap[:], gm1[:], None, ALU.mult))
            V.wait(t, ei)
            mm = t.transpose(pth[0:128, 0:8][0:128, 0:8] if False else pth[:, :],
                             wmap[:], idf[:]) if False else None
            mm = t.transpose(pth[:, :], wmap[:, 0:E].broadcast_to([128, E]) if False else wmap[:],
                             idf[:]) if False else None
            mm = t.transpose(pth[0:8, 0:128], wmap[:], idf[:])
            eit = T.inc(mm)
            T.wait(v, eit)
            ei = V.inc(v.tensor_copy(wfm[0:8, c0 + 128 * sbt: c0 + 128 * (sbt + 1)],
                                     pth[0:8, 0:128]))
            V.wait(t, ei)
        v.drain()
    St.wait(v)
    ei = V.inc(v.tensor_copy(wfmb[0:8, :], wfm[0:8, :]))
    V.wait(t, ei); V.wait(g, ei)

    for e in range(E):
        sy.dma_start(out=b2f[e:e + 1, :] if e == 0 else b2f[e:e + 1, :],
                     in_=b2_i[e]).then_inc(Lw.h, 16)
        Lw.val += 16
    Lw.wait(v)
    v.drain()
    ei = V.inc(v.tensor_copy(b2b[0:8, :], b2f[0:8, :]))
    V.wait(t, ei)

    for e in range(E):
        for half in range(4):
            h0 = half * 1024
            T.wait(sy)
            for k in range(JD):
                V.wait(sy)
                Lw.wait(sy)
                sy.dma_start(out=wtf[:], in_=w1_i[e, 128 * k:128 * (k + 1), h0:h0 + 1024]).then_inc(Lw.h, 16)
                Lw.val += 16
                Lw.wait(v)
                V.inc(v.tensor_copy(w1b[:, k * 1024:(k + 1) * 1024], wtf[:]))
            for hk in range(HQ):
                V.wait(sy)
                Lw.wait(sy)
                sy.dma_start(out=wtf[:, 0:D],
                             in_=w2_i[e, 128 * (HQ * half + hk):128 * (HQ * half + hk + 1), :]).then_inc(Lw.h, 16)
                Lw.val += 16
                Lw.wait(v)
                V.inc(v.tensor_copy(w2b[:, hk * D:(hk + 1) * D], wtf[:, 0:D]))
            if half == 0:
                for bh in range(4):
                    V.wait(sy)
                    Lw.wait(sy)
                    sy.dma_start(out=wtf[0:1, 0:1024],
                                 in_=b1_i[e][:, 1024 * bh:1024 * (bh + 1)]).then_inc(Lw.h, 16)
                    Lw.val += 16
                    Lw.wait(v)
                    V.inc(v.tensor_copy(b1b[:, 1024 * bh:1024 * (bh + 1)], wtf[0:1, 0:1024]))
            ei = V.inc(v.drain())
            V.wait(t, ei)
            for tc in range(NTC):
                c0 = tc * TCW
                V.wait(sy); Lw.wait(sy)
                sy.dma_start(out=wrow[:], in_=wfm[e:e + 1, c0:c0 + TCW]).then_inc(Lw.h, 16)
                Lw.val += 16
                Lw.wait(v)
                v.tensor_copy(wrow_b[:], wrow[:])
                ei = V.inc(v.drain())
                V.wait(t, ei)
                mm = t.matmul(pbc[:, 0:TCW], ones_row[:, 0:128], wrow_b[:], start=True, stop=True)
                eit = T.inc(mm)
                T.wait(v, eit)
                ei = V.inc(v.tensor_copy(wbc[:], pbc[:, 0:TCW]))
                V.wait(t, ei)
                for hk in range(HQ):
                    hcol = 128 * hk
                    for k in range(JD):
                        mm = t.matmul(ph[:, 0:TCW],
                                      w1b[:, k * 1024 + hcol: k * 1024 + hcol + 128],
                                      xnb[:, k * TL + c0: k * TL + c0 + TCW],
                                      start=(k == 0), stop=False)
                    mm = t.matmul(ph[:, 0:TCW], b1b[:, h0 + hcol: h0 + hcol + 128],
                                  ones_row[:, 0:TCW], start=False, stop=True)
                    eit = T.inc(mm)
                    T.wait(s, eit)
                    ei = S.inc(s.activation(htmp[:], ph[:, 0:TCW], AF.Gelu))
                    S.wait(t, ei)
                    S.wait(v, ei)
                    v.tensor_mul(hbuf[:, hk * TCW:(hk + 1) * TCW], htmp[:], wbc[:])
                    eiv = V.inc(v.drain())
                    V.wait(s, eiv)
                ei = V.inc(v.drain())
                V.wait(t, ei)
                for m8 in range(JD):
                    for hk in range(HQ):
                        mm = t.matmul(po[:, 0:TCW],
                                      w2b[:, hk * D + 128 * m8: hk * D + 128 * (m8 + 1)],
                                      hbuf[:, hk * TCW:(hk + 1) * TCW],
                                      start=(hk == 0), stop=(hk == HQ - 1))
                    eit = T.inc(mm)
                    T.wait(v, eit)
                    osl = oacc[:, m8 * TL + c0: m8 * TL + c0 + TCW]
                    v.tensor_add(osl, osl, po[:, 0:TCW])
                    ei = V.inc(v.drain())
                    V.wait(t, ei)

    for m8 in range(JD):
        for tc in range(NTC):
            c0 = tc * TCW
            mm = t.matmul(po[:, 0:TCW], b2b[0:8, 128 * m8:128 * (m8 + 1)],
                          wfmb[0:8, c0:c0 + TCW], start=True, stop=True)
            eit = T.inc(mm)
            T.wait(v, eit)
            osl = oacc[:, m8 * TL + c0: m8 * TL + c0 + TCW]
            v.tensor_add(osl, osl, po[:, 0:TCW])
            ei = V.inc(v.drain())
            V.wait(t, ei)

    for m8 in range(JD):
        V.wait(sy)
        Ld.wait(sy)
        sy.dma_start(out=xjs[:, 0:TL] if TL <= JD * TCW else xjs[:, 0:TL],
                     in_=x_i[m8]).then_inc(Ld.h, 16)
        Ld.val += 16
        Ld.wait(v)
        St.wait(v)
        v.tensor_add(oacc[:, m8 * TL:(m8 + 1) * TL], oacc[:, m8 * TL:(m8 + 1) * TL],
                     xjs[:, 0:TL])
        ei = V.inc(v.drain())
        V.wait(g, ei)
        g.wait_ge(St.h, St.val)
        g.dma_start(out=xout_o[m8], in_=oacc[:, m8 * TL:(m8 + 1) * TL]).then_inc(St.h, 16)
        St.val += 16
    St.wait(g)
    return nc


def host_prep_C(x_fm_list_unused, gate_w, gate_b, w1, b1, w2, b2, TLOC):
    return {
        "gate_w": np.ascontiguousarray(gate_w).astype(np.float32),
        "gate_b": gate_b.reshape(1, -1).astype(np.float32),
        "w1": np.ascontiguousarray(w1).astype(np.float32),
        "b1": np.ascontiguousarray(b1.reshape(b1.shape[0], 1, -1)).astype(np.float32),
        "w2": np.ascontiguousarray(w2).astype(np.float32),
        "b2": np.ascontiguousarray(b2.reshape(b2.shape[0], 1, -1)).astype(np.float32),
        "identf": np.eye(128, dtype=np.float32),
    }



def build_prog_D(TLOC):
    TL = B * TLOC
    NTC = max(1, TL // 512)
    TCW = min(512, TL)
    EH = 2 * D

    nc = bass.Bass("TRN2", target_bir_lowering=False, debug=False)
    x_i = din(nc, "x_i", [JD, 128, TL])
    winT = din(nc, "winT", [D, EH])
    b_in = din(nc, "b_in", [1, EH])
    b_h = din(nc, "b_h", [1, D])
    lnwb = din(nc, "lnwb", [2, 128, JD])
    g_o = dout(nc, "g_o", [JD, 128, TL], BF16)
    vb_o = dout(nc, "vb_o", [JD, 128, TL], BF16)
    xn2_o = dout(nc, "xn2_o", [JD, 128, TL])

    xjs = sb(nc, "xjs", [128, JD * TCW], F32)
    yjs = sb(nc, "yjs", [128, JD * TCW], F32)
    xnb = sb(nc, "xnb", [128, JD * TL], BF16)
    xb_c = sb(nc, "xb_c", [128, TCW], BF16)
    sq_c = sb(nc, "sq_c", [128, TCW], BF16)
    onescol = sb(nc, "onescol", [128, 1], BF16)
    ones_row = sb(nc, "ones_row", [1, TL], BF16)
    st_f = sb(nc, "st_f", [1, 4 * TCW], F32)
    st_b = sb(nc, "st_b", [1, 2 * TCW], BF16)
    lnw_sb = sb(nc, "lnw_sb", [128, 2 * JD], F32)
    wtf = sb(nc, "wtf", [128, 1024], F32)
    wtb = sb(nc, "wtb", [128, JD * EH], BF16)
    brow = sb(nc, "brow", [1, EH + D], BF16)
    t_y = sb(nc, "t_y", [128, TCW], F32)
    t_z = sb(nc, "t_z", [128, TCW], F32)
    pA = ps(nc, "pA", [128, TCW], F32)
    pB = ps(nc, "pB", [128, TCW], F32)
    pC = ps(nc, "pC", [128, TCW], F32)
    pD = ps(nc, "pD", [128, TCW], F32)

    Ld = Sem(nc, "Ld"); Lw = Sem(nc, "Lw"); St = Sem(nc, "St")
    T = Sem(nc, "T"); V = Sem(nc, "V"); S = Sem(nc, "S")
    v, s, t, g, sy = nc.vector, nc.scalar, nc.tensor, nc.gpsimd, nc.sync

    sy.dma_start(out=lnw_sb[:, 0:JD], in_=lnwb[0]).then_inc(Ld.h, 16)
    Ld.val += 16
    sy.dma_start(out=lnw_sb[:, JD:2 * JD], in_=lnwb[1]).then_inc(Ld.h, 16)
    Ld.val += 16
    sy.dma_start(out=wtf[0:1, 0:1024], in_=b_in[:, 0:1024]).then_inc(Ld.h, 16)
    Ld.val += 16
    Ld.wait(v)
    V.inc(v.tensor_copy(brow[:, 0:1024], wtf[0:1, 0:1024]))
    V.wait(sy)
    sy.dma_start(out=wtf[0:1, 0:1024], in_=b_in[:, 1024:2048]).then_inc(Ld.h, 16)
    Ld.val += 16
    Ld.wait(v)
    V.inc(v.tensor_copy(brow[:, 1024:2048], wtf[0:1, 0:1024]))
    V.wait(sy)
    sy.dma_start(out=wtf[0:1, 0:1024], in_=b_h).then_inc(Ld.h, 16)
    Ld.val += 16
    Ld.wait(v)
    V.inc(v.tensor_copy(brow[:, 2048:2048 + D], wtf[0:1, 0:1024]))
    v.memset(onescol[:], 1.0)
    v.memset(ones_row[:], 1.0)
    ei = V.inc(v.drain())
    V.wait(t, ei); V.wait(sy)
    for k in range(JD):
        for hh in range(2):
            Lw.wait(sy)
            V.wait(sy)
            sy.dma_start(out=wtf[:], in_=winT[128 * k:128 * (k + 1),
                                            1024 * hh:1024 * (hh + 1)]).then_inc(Lw.h, 16)
            Lw.val += 16
            Lw.wait(v)
            V.inc(v.tensor_copy(wtb[:, k * EH + 1024 * hh: k * EH + 1024 * (hh + 1)], wtf[:]))
    ei = V.inc(v.drain())
    V.wait(t, ei)

    def ln_stats(xsrc_sb, off_fn, eps):
        for j in range(JD):
            xsl = xsrc_sb[:, off_fn(j):off_fn(j) + TCW]
            ei = V.inc(v.tensor_copy(xb_c[:], xsl))
            eis = S.inc(s.activation(sq_c[:], xsl, AF.Square))
            V.wait(t, ei); S.wait(t, eis)
            t.matmul(pA[0:1, 0:TCW], onescol[:], xb_c[:], start=(j == 0), stop=(j == JD - 1))
            mm2 = t.matmul(pB[0:1, 0:TCW], onescol[:], sq_c[:], start=(j == 0), stop=(j == JD - 1))
            eit = T.inc(mm2)
            T.wait(v, eit); T.wait(s, eit)
        v.tensor_scalar_mul(st_f[:, 0:TCW], pA[0:1, 0:TCW], 1.0 / D)
        v.drain()
        v.tensor_scalar_mul(st_f[:, TCW:2 * TCW], pB[0:1, 0:TCW], 1.0 / D)
        v.drain()
        v.tensor_mul(st_f[:, 3 * TCW:4 * TCW], st_f[:, 0:TCW], st_f[:, 0:TCW])
        v.drain()
        v.tensor_sub(st_f[:, TCW:2 * TCW], st_f[:, TCW:2 * TCW], st_f[:, 3 * TCW:4 * TCW])
        v.drain()
        v.tensor_scalar_add(st_f[:, TCW:2 * TCW], st_f[:, TCW:2 * TCW], eps)
        ei = V.inc(v.drain())
        V.wait(s, ei)
        ei = S.inc(s.activation(st_f[:, 2 * TCW:3 * TCW], st_f[:, TCW:2 * TCW], AF.Sqrt))
        S.wait(v, ei)
        v.reciprocal(st_f[:, 2 * TCW:3 * TCW], st_f[:, 2 * TCW:3 * TCW])
        v.drain()
        v.tensor_copy(st_b[:, 0:TCW], st_f[:, 0:TCW])
        v.drain()
        ei = V.inc(v.tensor_copy(st_b[:, TCW:2 * TCW], st_f[:, 2 * TCW:3 * TCW]))
        V.wait(t, ei)
        mm1 = t.matmul(pC[:, 0:TCW], ones_row[:, 0:128], st_b[:, 0:TCW], start=True, stop=True)
        mm2 = t.matmul(pD[:, 0:TCW], ones_row[:, 0:128], st_b[:, TCW:2 * TCW], start=True, stop=True)
        eit = T.inc(mm2)
        T.wait(v, eit)

    for tc in range(NTC):
        c0 = tc * TCW
        V.wait(sy); S.wait(sy); T.wait(sy); Ld.wait(sy)
        for j in range(JD):
            sy.dma_start(out=xjs[:, j * TCW:(j + 1) * TCW],
                         in_=x_i[j][:, c0:c0 + TCW]).then_inc(Ld.h, 16)
        Ld.val += 16 * JD
        Ld.wait(v); Ld.wait(s)
        ln_stats(xjs, lambda j: j * TCW, 1e-5)
        for j in range(JD):
            v.tensor_sub(t_y[:], xjs[:, j * TCW:(j + 1) * TCW], pC[:, 0:TCW])
            v.drain()
            ei = V.inc(v.tensor_mul(yjs[:, j * TCW:(j + 1) * TCW], t_y[:], pD[:, 0:TCW]))
            v.drain()
            V.wait(g, ei)
            g.wait_ge(St.h, St.val)
            g.dma_start(out=xn2_o[j][:, c0:c0 + TCW],
                        in_=yjs[:, j * TCW:(j + 1) * TCW]).then_inc(St.h, 16)
            St.val += 16
        St.wait(v)
        ei = V.inc(v.drain())
        V.wait(t, ei)
        V.wait(s, ei)
        ln_stats(yjs, lambda j: j * TCW, 1e-5)
        for j in range(JD):
            v.tensor_sub(t_y[:], yjs[:, j * TCW:(j + 1) * TCW], pC[:, 0:TCW])
            v.drain()
            v.tensor_mul(t_z[:], t_y[:], pD[:, 0:TCW])
            v.drain()
            v.tensor_scalar(t_z[:], t_z[:], lnw_sb[:, j:j + 1], lnw_sb[:, JD + j:JD + j + 1],
                            ALU.mult, ALU.add)
            v.drain()
            v.tensor_copy(xnb[:, j * TL + c0: j * TL + c0 + TCW], t_z[:])
            v.drain()
        V.wait(t, V.inc(v.drain()))

    for m16 in range(16):
        for tc in range(NTC):
            c0 = tc * TCW
            for k in range(JD):
                mm = t.matmul(pA[:, 0:TCW],
                              wtb[:, k * EH + 128 * m16: k * EH + 128 * (m16 + 1)],
                              xnb[:, k * TL + c0: k * TL + c0 + TCW],
                              start=(k == 0), stop=False)
            mm = t.matmul(pA[:, 0:TCW], brow[:, 128 * m16:128 * (m16 + 1)],
                          ones_row[:, 0:TCW], start=False, stop=(m16 < 8))
            if m16 >= 8:
                mm = t.matmul(pA[:, 0:TCW], brow[:, 2048 + 128 * (m16 - 8):2048 + 128 * (m16 - 7)],
                              ones_row[:, 0:TCW], start=False, stop=True)
            eit = T.inc(mm)
            if m16 < 8:
                T.wait(s, eit)
                ei = S.inc(s.activation(t_y[:].bitcast(BF16)[:, 0:TCW], pA[:, 0:TCW], AF.Sigmoid))
                S.wait(t, ei); S.wait(g, ei)
                g.wait_ge(St.h, St.val)
                g.dma_start(out=g_o[m16][:, c0:c0 + TCW],
                            in_=t_y[:].bitcast(BF16)[:, 0:TCW]).then_inc(St.h, 16)
                St.val += 16
                St.wait(s); St.wait(v)
            else:
                T.wait(v, eit)
                ei = V.inc(v.tensor_copy(t_y[:].bitcast(BF16)[:, 0:TCW], pA[:, 0:TCW]))
                V.wait(t, ei); V.wait(g, ei)
                g.wait_ge(St.h, St.val)
                g.dma_start(out=vb_o[m16 - 8][:, c0:c0 + TCW],
                            in_=t_y[:].bitcast(BF16)[:, 0:TCW]).then_inc(St.h, 16)
                St.val += 16
                St.wait(v); St.wait(s)
    St.wait(g)
    return nc



def build_prog_E(TG_, NCORES_):
    CH = 64
    NCH = TG_ // CH
    COLS = JD * B
    TLL = B * (TG_ // NCORES_)
    CB = CH * B

    nc = bass.Bass("TRN2", target_bir_lowering=False, debug=False)
    g_i = din(nc, "g_i", [NCORES_, JD, 128, TLL], BF16)
    vb_i = din(nc, "vb_i", [NCORES_, JD, 128, TLL], BF16)
    whT = din(nc, "whT", [D, D])
    hs_o = dout(nc, "hs_o", [JD, 128, B * TG_], BF16)

    wtf = sb(nc, "wtf", [128, D], F32)
    wtb = sb(nc, "wtb", [128, JD * D], BF16)
    gb0 = sb(nc, "gb0", [128, JD * CB], BF16)
    gb1 = sb(nc, "gb1", [128, JD * CB], BF16)
    vbb0 = sb(nc, "vbb0", [128, JD * CB], BF16)
    vbb1 = sb(nc, "vbb1", [128, JD * CB], BF16)
    h = sb(nc, "h", [128, COLS], F32)
    hbf = sb(nc, "hbf", [128, COLS], BF16)
    pre = sb(nc, "pre", [128, COLS], F32)
    cc = sb(nc, "cc", [128, COLS], F32)
    hs0 = sb(nc, "hs0", [128, JD * CB], BF16)
    hs1 = sb(nc, "hs1", [128, JD * CB], BF16)
    pz = ps(nc, "pz", [128, COLS], F32)

    Ld = Sem(nc, "Ld"); Lp = Sem(nc, "Lp"); St = Sem(nc, "St"); St1 = Sem(nc, "St1")
    s_mm = Sem(nc, "s_mm"); s_pre = Sem(nc, "s_pre"); s_tanh = Sem(nc, "s_tanh")
    s_bl = Sem(nc, "s_bl"); s_fl = Sem(nc, "s_fl"); V2 = Sem(nc, "V2")
    v, s, t, g, sy = nc.vector, nc.scalar, nc.tensor, nc.gpsimd, nc.sync

    gbuf = [gb0, gb1]; vbuf = [vbb0, vbb1]; hsb = [hs0, hs1]

    for k in range(JD):
        if k > 0:
            Ld.wait(sy)
            V2.wait(sy)
        sy.dma_start(out=wtf[:], in_=whT[128 * k:128 * (k + 1), :]).then_inc(Ld.h, 16)
        Ld.val += 16
        Ld.wait(v)
        V2.inc(v.tensor_copy(wtb[:, k * D:(k + 1) * D], wtf[:]))
    v.drain()
    v.memset(h[:], 0.0)
    v.memset(hbf[:], 0.0).then_inc(s_bl.h, 1)
    s_bl.val += 1

    def chunk_src(c):
        r = c // (NCH // NCORES_)
        tl0 = (c % (NCH // NCORES_)) * CB
        return r, tl0

    r, tl0 = chunk_src(0)
    for j in range(JD):
        sy.dma_start(out=gbuf[0][:, j * CB:(j + 1) * CB],
                     in_=g_i[r, j][:, tl0:tl0 + CB]).then_inc(Lp.h, 16)
        sy.dma_start(out=vbuf[0][:, j * CB:(j + 1) * CB],
                     in_=vb_i[r, j][:, tl0:tl0 + CB]).then_inc(Lp.h, 16)
    Lp.val += 16 * 2 * JD
    for c in range(1, NCH):
        r, tl0 = chunk_src(c)
        sy.wait_ge(s_bl.h, 1 + (c - 1) * CH)
        Lp.wait(sy)
        for j in range(JD):
            sy.dma_start(out=gbuf[c % 2][:, j * CB:(j + 1) * CB],
                         in_=g_i[r, j][:, tl0:tl0 + CB]).then_inc(Lp.h, 16)
            sy.dma_start(out=vbuf[c % 2][:, j * CB:(j + 1) * CB],
                         in_=vb_i[r, j][:, tl0:tl0 + CB]).then_inc(Lp.h, 16)
        Lp.val += 16 * 2 * JD

    t.wait_ge(Ld.h, Ld.val)
    t.wait_ge(V2.h, V2.val)
    with t.Fori(0, TG_) as i:
        t.wait_ge(s_bl.h, i + 1)
        for o in range(JD):
            for k in range(JD):
                mm = t.matmul(pz[:, 4 * o:4 * (o + 1)],
                              wtb[:, k * D + 128 * o: k * D + 128 * (o + 1)],
                              hbf[:, 4 * k:4 * (k + 1)],
                              start=(k == 0), stop=(k == JD - 1))
        mm.then_inc(s_mm.h, 1)
    s_mm.val += TG_

    with s.Fori(0, TG_) as i:
        s.wait_ge(s_pre.h, i + 1)
        s.activation(cc[:], pre[:], AF.Tanh).then_inc(s_tanh.h, 1)
    s_tanh.val += TG_

    PAIRS = NCH // 2
    with v.Fori(0, PAIRS) as oc2:
        for p in (0, 1):
            v.wait_ge(Lp.h, oc2 * (16 * 2 * JD * 2) + (p + 1) * 16 * 2 * JD)
            with v.Fori(0, CH) as i:
                nsy = oc2 * (2 * CH) + p * CH
                v.wait_ge(s_mm.h, nsy + i + 1)
                v.scalar_tensor_tensor(
                    pre[:], pz[:], 1.0,
                    vbuf[p][:, 0:JD * CB].rearrange("q (j t) -> q j t", j=JD)[:, :, ts(i, 4)],
                    ALU.mult, ALU.add)
                v.drain().then_inc(s_pre.h, 1)
                v.wait_ge(s_tanh.h, nsy + i + 1)
                v.tensor_sub(cc[:], cc[:], h[:])
                v.drain()
                v.tensor_mul(cc[:], cc[:],
                             gbuf[p][:, 0:JD * CB].rearrange("q (j t) -> q j t", j=JD)[:, :, ts(i, 4)])
                v.drain()
                v.tensor_add(h[:], h[:], cc[:])
                v.drain()
                v.tensor_copy(hbf[:], h[:])
                v.drain()
                v.tensor_copy(hsb[p][:, 0:JD * CB].rearrange("q (j t) -> q j t", j=JD)[:, :, ts(i, 4)],
                              h[:]).then_inc(s_bl.h, 1)
            v.wait_ge(s_fl.h, oc2 * 2 + p + 1)
    s_pre.val += TG_
    s_bl.val += TG_
    s_fl_total = 0

    for c in range(NCH):
        par = c % 2
        Sc = St if par == 0 else St1
        g.wait_ge(s_bl.h, 1 + (c + 1) * CH)
        g.wait_ge(Sc.h, Sc.val)
        for j in range(JD):
            g.dma_start(out=hs_o[j][:, c * CB:(c + 1) * CB],
                        in_=hsb[par][:, j * CB:(j + 1) * CB]).then_inc(Sc.h, 16)
        Sc.val += 16 * JD
        g.wait_ge(Sc.h, Sc.val)
        g.engine_nop().then_inc(s_fl.h, 1)
        s_fl.val += 1
    g.wait_ge(St.h, St.val)
    g.wait_ge(St1.h, St1.val)
    return nc



def build_prog_F(TLOC):
    TL = B * TLOC
    NTC = max(1, TL // 512)
    TCW = min(512, TL)

    nc = bass.Bass("TRN2", target_bir_lowering=False, debug=False)
    x_i = din(nc, "x_i", [JD, 128, TL])
    xn2_i = din(nc, "xn2_i", [JD, 128, TL])
    hs_i = din(nc, "hs_i", [JD, 128, TL], BF16)
    woutT = din(nc, "woutT", [D, D])
    bout = din(nc, "bout", [1, D])
    xout_o = dout(nc, "xout", [JD, 128, TL])

    hsb = sb(nc, "hsb", [128, JD * TL], BF16)
    wtf = sb(nc, "wtf", [128, D], F32)
    wtb = sb(nc, "wtb", [128, JD * D], BF16)
    brow = sb(nc, "brow", [1, D], BF16)
    ones_row = sb(nc, "ones_row", [1, TL], BF16)
    xs = sb(nc, "xs", [128, TL], F32)
    xn2s = sb(nc, "xn2s", [128, TL], F32)
    t_y = sb(nc, "t_y", [128, TCW], F32)
    pA = ps(nc, "pA", [128, TCW], F32)

    Ld = Sem(nc, "Ld"); Lw = Sem(nc, "Lw"); St = Sem(nc, "St")
    T = Sem(nc, "T"); V = Sem(nc, "V"); S = Sem(nc, "S")
    v, s, t, g, sy = nc.vector, nc.scalar, nc.tensor, nc.gpsimd, nc.sync

    for j in range(JD):
        sy.dma_start(out=hsb[:, j * TL:(j + 1) * TL], in_=hs_i[j]).then_inc(Ld.h, 16)
        Ld.val += 16
    sy.dma_start(out=wtf[0:1, :], in_=bout).then_inc(Ld.h, 16)
    Ld.val += 16
    Ld.wait(v)
    v.tensor_copy(brow[:], wtf[0:1, :])
    v.memset(ones_row[:], 1.0)
    ei = V.inc(v.drain())
    V.wait(sy)
    for k in range(JD):
        Lw.wait(sy)
        V.wait(sy)
        sy.dma_start(out=wtf[:], in_=woutT[128 * k:128 * (k + 1), :]).then_inc(Lw.h, 16)
        Lw.val += 16
        Lw.wait(v)
        V.inc(v.tensor_copy(wtb[:, k * D:(k + 1) * D], wtf[:]))
    ei = V.inc(v.drain())
    V.wait(t, ei)

    for m8 in range(JD):
        V.wait(sy); Lw.wait(sy)
        sy.dma_start(out=xs[:], in_=x_i[m8]).then_inc(Lw.h, 16)
        sy.dma_start(out=xn2s[:], in_=xn2_i[m8]).then_inc(Lw.h, 16)
        Lw.val += 32
        for tc in range(NTC):
            c0 = tc * TCW
            for k in range(JD):
                mm = t.matmul(pA[:, 0:TCW],
                              wtb[:, k * D + 128 * m8: k * D + 128 * (m8 + 1)],
                              hsb[:, k * TL + c0: k * TL + c0 + TCW],
                              start=(k == 0), stop=False)
            mm = t.matmul(pA[:, 0:TCW], brow[:, 128 * m8:128 * (m8 + 1)],
                          ones_row[:, 0:TCW], start=False, stop=True)
            eit = T.inc(mm)
            T.wait(v, eit)
            Lw.wait(v)
            v.tensor_add(t_y[:], pA[:, 0:TCW], xs[:, c0:c0 + TCW])
            v.drain()
            ei = V.inc(v.tensor_add(t_y[:], t_y[:], xn2s[:, c0:c0 + TCW]))
            V.wait(t, ei)
            V.wait(g, ei)
            g.wait_ge(St.h, St.val)
            g.dma_start(out=xout_o[m8][:, c0:c0 + TCW], in_=t_y[:]).then_inc(St.h, 16)
            St.val += 16
            St.wait(v)
    St.wait(g)
    return nc



def build_prog_I(TLOC):
    TL = B * TLOC
    NTC = max(1, TL // 512)
    TCW = min(512, TL)
    E = 32
    HI = 512
    HCE = HI // 128
    HS = 1024
    HCS = HS // 128

    nc = bass.Bass("TRN2", target_bir_lowering=False, debug=False)
    x_i = din(nc, "x_i", [JD, 128, TL])
    gw_i = din(nc, "gw", [D, E])
    gb_i = din(nc, "gb", [1, E])
    w1_i = din(nc, "w1", [E, D, HI])
    b1_i = din(nc, "b1", [E, HI])
    w3_i = din(nc, "w3", [E, D, HI])
    b3_i = din(nc, "b3", [E, HI])
    w2_i = din(nc, "w2", [E, HI, D])
    b2_i = din(nc, "b2", [E, D])
    sw1_i = din(nc, "sw1", [D, HS])
    sb1_i = din(nc, "sb1", [1, HS])
    sw3_i = din(nc, "sw3", [D, HS])
    sb3_i = din(nc, "sb3", [1, HS])
    sw2_i = din(nc, "sw2", [HS, D])
    sb2_i = din(nc, "sb2", [1, D])
    identf = din(nc, "identf", [128, 128], F32)
    xout_o = dout(nc, "xout", [JD, 128, TL])

    uni = sb(nc, "uni", [128, JD * TCW], F32)
    xjs = uni
    xnb = sb(nc, "xnb", [128, JD * TL], BF16)
    xb_c = sb(nc, "xb_c", [128, TCW], BF16)
    sq_c = sb(nc, "sq_c", [128, TCW], BF16)
    onescol = sb(nc, "onescol", [128, 1], BF16)
    ones_row = sb(nc, "ones_row", [1, TL], BF16)
    st_f = sb(nc, "st_f", [1, 4 * TCW], F32)
    st_b = sb(nc, "st_b", [1, 2 * TCW], BF16)
    gw_sb = sb(nc, "gw_sb", [128, JD * E], BF16)
    gb_sb = sb(nc, "gb_sb", [1, E], BF16)
    idf = sb(nc, "idf", [128, 128], F32)
    lg = sb(nc, "lg", [32, TCW], F32)
    lgt = sb(nc, "lgt", [128, E], F32)
    gm1 = sb(nc, "gm1", [128, 1], F32)
    gm8 = sb(nc, "gm8", [128, 8], F32)
    gsc = sb(nc, "gsc", [128, 8], F32)
    gtmp = sb(nc, "gtmp", [128, 8], F32)
    kbc = sb(nc, "kbc", [128, E], F32)
    wmap = sb(nc, "wmap", [128, E], F32)
    wfmb = sb(nc, "wfmb", [32, NTC * TCW], BF16)
    wrow_b = sb(nc, "wrow_b", [1, TCW], BF16)
    wbc = sb(nc, "wbc", [128, TCW], BF16)
    w1b = sb(nc, "w1b", [128, JD * HS], BF16)
    w3b = sb(nc, "w3b", [128, JD * HS], BF16)
    w2b = sb(nc, "w2b", [128, HCS * D], BF16)
    wtf = sb(nc, "wtf", [128, 1024], F32)
    brow = sb(nc, "brow", [1, 2 * HI], BF16)
    b2b = sb(nc, "b2b", [32, D], BF16)
    b2f = sb(nc, "b2f", [32, D], F32)
    sbrow = sb(nc, "sbrow", [1, 2 * HS + D], BF16)
    hbuf = uni.bitcast(BF16)
    htmp = sb(nc, "htmp", [128, TCW], BF16)
    oacc = sb(nc, "oacc", [128, JD * TL], F32)
    t_y = sb(nc, "t_y", [128, TCW], F32)
    pA = ps(nc, "pA", [128, TCW], F32)
    pB = ps(nc, "pB", [128, TCW], F32)
    pC = ps(nc, "pC", [128, TCW], F32)
    pD = ps(nc, "pD", [128, TCW], F32)
    pb1 = pA; pb2 = pB; pbc = pC; pbc2 = pD
    pl = pA; ptt = pB; pth = pC
    pha = pA; phb = pB; po = pC; pw = pD

    Ld = Sem(nc, "Ld"); Lw = Sem(nc, "Lw"); St = Sem(nc, "St")
    T = Sem(nc, "T"); V = Sem(nc, "V"); S = Sem(nc, "S")
    v, s, t, g, sy = nc.vector, nc.scalar, nc.tensor, nc.gpsimd, nc.sync

    sy.dma_start(out=idf[:], in_=identf).then_inc(Ld.h, 16)
    Ld.val += 16
    for j in range(JD):
        Ld.wait(sy); V.wait(sy)
        sy.dma_start(out=wtf[:, 0:E], in_=gw_i[128 * j:128 * (j + 1), :]).then_inc(Ld.h, 16)
        Ld.val += 16
        Ld.wait(v)
        V.inc(v.tensor_copy(gw_sb[:, j * E:(j + 1) * E], wtf[:, 0:E]))
    Ld.wait(sy); V.wait(sy)
    sy.dma_start(out=wtf[0:1, 0:E], in_=gb_i).then_inc(Ld.h, 16)
    Ld.val += 16
    Ld.wait(v)
    v.tensor_copy(gb_sb[:], wtf[0:1, 0:E])
    v.memset(onescol[:], 1.0)
    v.memset(ones_row[:], 1.0)
    v.memset(oacc[:], 0.0)
    ei = V.inc(v.drain())
    V.wait(t, ei); V.wait(sy)
    for (name, src, off, w) in [("sb1", sb1_i, 0, HS), ("sb3", sb3_i, HS, HS), ("sb2", sb2_i, 2 * HS, D)]:
        Ld.wait(sy); V.wait(sy)
        sy.dma_start(out=wtf[0:1, 0:w], in_=src).then_inc(Ld.h, 16)
        Ld.val += 16
        Ld.wait(v)
        V.inc(v.tensor_copy(sbrow[:, off:off + w], wtf[0:1, 0:w]))
    Ld.wait(sy); V.wait(sy)
    sy.dma_start(out=b2f[0:32, :], in_=b2_i).then_inc(Ld.h, 16)
    Ld.val += 16
    Ld.wait(v)
    ei = V.inc(v.tensor_copy(b2b[0:32, :], b2f[0:32, :]))
    V.wait(t, ei)

    for tc in range(NTC):
        c0 = tc * TCW
        V.wait(sy); S.wait(sy); T.wait(sy); Ld.wait(sy)
        for j in range(JD):
            sy.dma_start(out=xjs[:, j * TCW:(j + 1) * TCW],
                         in_=x_i[j][:, c0:c0 + TCW]).then_inc(Ld.h, 16)
        Ld.val += 16 * JD
        Ld.wait(v); Ld.wait(s)
        for j in range(JD):
            xsl = xjs[:, j * TCW:(j + 1) * TCW]
            ei = V.inc(v.tensor_copy(xb_c[:], xsl))
            eis = S.inc(s.activation(sq_c[:], xsl, AF.Square))
            V.wait(t, ei); S.wait(t, eis)
            t.matmul(pb1[0:1, 0:TCW], onescol[:], xb_c[:], start=(j == 0), stop=(j == JD - 1))
            mm2 = t.matmul(pb2[0:1, 0:TCW], onescol[:], sq_c[:], start=(j == 0), stop=(j == JD - 1))
            eit = T.inc(mm2)
            T.wait(v, eit); T.wait(s, eit)
        v.tensor_scalar_mul(st_f[:, 0:TCW], pb1[0:1, 0:TCW], 1.0 / D)
        v.drain()
        v.tensor_scalar_mul(st_f[:, TCW:2 * TCW], pb2[0:1, 0:TCW], 1.0 / D)
        v.drain()
        v.tensor_mul(st_f[:, 3 * TCW:4 * TCW], st_f[:, 0:TCW], st_f[:, 0:TCW])
        v.drain()
        v.tensor_sub(st_f[:, TCW:2 * TCW], st_f[:, TCW:2 * TCW], st_f[:, 3 * TCW:4 * TCW])
        v.drain()
        v.tensor_scalar_add(st_f[:, TCW:2 * TCW], st_f[:, TCW:2 * TCW], 1e-5)
        ei = V.inc(v.drain())
        V.wait(s, ei)
        ei = S.inc(s.activation(st_f[:, 2 * TCW:3 * TCW], st_f[:, TCW:2 * TCW], AF.Sqrt))
        S.wait(v, ei)
        v.reciprocal(st_f[:, 2 * TCW:3 * TCW], st_f[:, 2 * TCW:3 * TCW])
        v.drain()
        v.tensor_copy(st_b[:, 0:TCW], st_f[:, 0:TCW])
        v.drain()
        ei = V.inc(v.tensor_copy(st_b[:, TCW:2 * TCW], st_f[:, 2 * TCW:3 * TCW]))
        V.wait(t, ei)
        mm1 = t.matmul(pbc[:, 0:TCW], ones_row[:, 0:128], st_b[:, 0:TCW], start=True, stop=True)
        mm2 = t.matmul(pbc2[:, 0:TCW], ones_row[:, 0:128], st_b[:, TCW:2 * TCW], start=True, stop=True)
        eit = T.inc(mm2)
        T.wait(v, eit)
        for j in range(JD):
            v.tensor_sub(t_y[:], xjs[:, j * TCW:(j + 1) * TCW], pbc[:, 0:TCW])
            v.drain()
            v.tensor_mul(xnb[:, j * TL + c0: j * TL + c0 + TCW], t_y[:], pbc2[:, 0:TCW])
            v.drain()
        ei = V.inc(v.drain())
        V.wait(t, ei)
        for j in range(JD):
            t.matmul(pl[0:32, 0:TCW], gw_sb[:, j * E:(j + 1) * E],
                     xnb[:, j * TL + c0: j * TL + c0 + TCW], start=(j == 0), stop=False)
        mm = t.matmul(pl[0:32, 0:TCW], gb_sb[:], ones_row[:, 0:TCW], start=False, stop=True)
        eit = T.inc(mm)
        T.wait(v, eit)
        ei = V.inc(v.tensor_copy(lg[0:32, 0:TCW], pl[0:32, 0:TCW]))
        V.wait(t, ei)
        for sbt in range(TCW // 128):
            mm = t.matmul(ptt[:, 0:E], lg[0:32, 128 * sbt:128 * (sbt + 1)], idf[0:32, 0:E],
                          start=True, stop=True)
            eit = T.inc(mm)
            T.wait(v, eit)
            v.tensor_copy(lgt[:], ptt[:, 0:E])
            v.drain()
            v.reduce_max(gm1[:], lgt[:], AX.X)
            v.drain()
            ei = V.inc(v.tensor_scalar(lgt[:], lgt[:], gm1[:], None, ALU.subtract))
            V.wait(s, ei)
            ei = S.inc(s.activation(lgt[:], lgt[:], AF.Exp))
            S.wait(v, ei)
            v.tensor_reduce(gm1[:], lgt[:], AX.X, ALU.add)
            v.drain()
            v.reciprocal(gm1[:], gm1[:])
            v.drain()
            v.tensor_scalar(lgt[:], lgt[:], gm1[:], None, ALU.mult)
            v.drain()
            pairs = [(0, 1), (0, 2), (0, 3), (1, 2), (1, 3), (2, 3)]
            first = True
            for (a, b_) in pairs:
                v.tensor_add(gtmp[:], lgt[:, a:E:4], lgt[:, b_:E:4])
                v.drain()
                if first:
                    v.tensor_copy(gsc[:], gtmp[:])
                    first = False
                else:
                    v.tensor_max(gsc[:], gsc[:], gtmp[:])
                v.drain()
            v.max(out=gm8[:], in_=gsc[:])
            v.drain()
            v.tensor_scalar(gsc[:], gsc[:], gm8[:, 3:4], None, ALU.is_ge)
            v.drain()
            v.tensor_copy(kbc[:].rearrange("p (g i) -> p g i", g=8),
                          gsc[:].unsqueeze(2).to_broadcast([128, 8, 4]))
            v.drain()
            v.tensor_mul(wmap[:], lgt[:], kbc[:])
            v.drain()
            v.tensor_scalar(kbc[:], kbc[:], -1.0, 1e30, ALU.add, ALU.mult)
            v.drain()
            v.tensor_add(wmap[:], wmap[:], kbc[:])
            v.drain()
            v.max(out=gm8[:], in_=wmap[:])
            v.drain()
            v.tensor_scalar(wmap[:], wmap[:], gm8[:, 3:4], None, ALU.is_ge)
            v.drain()
            ei = V.inc(v.tensor_mul(wmap[:], wmap[:], lgt[:]))
            V.wait(t, ei)
            mm = t.transpose(pth[0:32, 0:128], wmap[:], idf[:])
            eit = T.inc(mm)
            T.wait(v, eit)
            ei = V.inc(v.tensor_copy(wfmb[0:32, c0 + 128 * sbt: c0 + 128 * (sbt + 1)],
                                     pth[0:32, 0:128]))
            V.wait(t, ei)
        v.drain()
    St.wait(v)
    ei = V.inc(v.drain())
    V.wait(t, ei); V.wait(g, ei); V.wait(sy)

    for e in range(-1, E):
        HCX = HCS if e < 0 else HCE
        T.wait(sy)
        for k in range(JD):
            for (wsrc, dstb, hw) in ([(sw1_i, w1b, HS), (sw3_i, w3b, HS)] if e < 0
                                     else [(w1_i[e], w1b, HI), (w3_i[e], w3b, HI)]):
                V.wait(sy); Lw.wait(sy)
                sy.dma_start(out=wtf[:, 0:hw], in_=wsrc[128 * k:128 * (k + 1), :]).then_inc(Lw.h, 16)
                Lw.val += 16
                Lw.wait(v)
                V.inc(v.tensor_copy(dstb[:, k * hw:(k + 1) * hw], wtf[:, 0:hw]))
        for hk in range(HCX):
            V.wait(sy); Lw.wait(sy)
            src = sw2_i[128 * hk:128 * (hk + 1), :] if e < 0 else w2_i[e, 128 * hk:128 * (hk + 1), :]
            sy.dma_start(out=wtf[:, 0:D], in_=src).then_inc(Lw.h, 16)
            Lw.val += 16
            Lw.wait(v)
            V.inc(v.tensor_copy(w2b[:, hk * D:(hk + 1) * D], wtf[:, 0:D]))
        if e >= 0:
            for (bsrc, off) in [(b1_i[e:e + 1, :], 0), (b3_i[e:e + 1, :], HI)]:
                V.wait(sy); Lw.wait(sy)
                sy.dma_start(out=wtf[0:1, 0:HI], in_=bsrc).then_inc(Lw.h, 16)
                Lw.val += 16
                Lw.wait(v)
                V.inc(v.tensor_copy(brow[:, off:off + HI], wtf[0:1, 0:HI]))
        ei = V.inc(v.drain())
        V.wait(t, ei)
        for tc in range(NTC):
            c0 = tc * TCW
            if e >= 0:
                V.wait(sy); Lw.wait(sy)
                T.wait(sy)
                sy.dma_start(out=wrow_b[:], in_=wfmb[e:e + 1, c0:c0 + TCW]).then_inc(Lw.h, 16)
                Lw.val += 16
                Lw.wait(t)
                mm = t.matmul(pw[:, 0:TCW], ones_row[:, 0:128], wrow_b[:], start=True, stop=True)
                eit = T.inc(mm)
                T.wait(v, eit)
                ei = V.inc(v.tensor_copy(wbc[:], pw[:, 0:TCW]))
                V.wait(t, ei)
            for hk in range(HCX):
                hcol = 128 * hk
                HW = HS if e < 0 else HI
                for k in range(JD):
                    t.matmul(pha[:, 0:TCW], w1b[:, k * HW + hcol: k * HW + hcol + 128],
                             xnb[:, k * TL + c0: k * TL + c0 + TCW],
                             start=(k == 0), stop=False)
                brsrc = sbrow[:, hcol:hcol + 128] if e < 0 else brow[:, hcol:hcol + 128]
                t.matmul(pha[:, 0:TCW], brsrc, ones_row[:, 0:TCW], start=False, stop=True)
                for k in range(JD):
                    t.matmul(phb[:, 0:TCW], w3b[:, k * HW + hcol: k * HW + hcol + 128],
                             xnb[:, k * TL + c0: k * TL + c0 + TCW],
                             start=(k == 0), stop=False)
                brsrc = sbrow[:, HS + hcol:HS + hcol + 128] if e < 0 else brow[:, HI + hcol:HI + hcol + 128]
                mm = t.matmul(phb[:, 0:TCW], brsrc, ones_row[:, 0:TCW], start=False, stop=True)
                eit = T.inc(mm)
                T.wait(s, eit)
                ei = S.inc(s.activation(htmp[:], pha[:, 0:TCW], AF.Silu))
                S.wait(v, ei)
                v.tensor_mul(hbuf[:, hk * TCW:(hk + 1) * TCW], htmp[:], phb[:, 0:TCW])
                v.drain()
                if e >= 0:
                    v.tensor_mul(hbuf[:, hk * TCW:(hk + 1) * TCW],
                                 hbuf[:, hk * TCW:(hk + 1) * TCW], wbc[:])
                    v.drain()
                eiv = V.inc(v.drain())
                V.wait(s, eiv)
                V.wait(t, eiv)
            for m8 in range(JD):
                for hk in range(HCX):
                    mm = t.matmul(po[:, 0:TCW],
                                  w2b[:, hk * D + 128 * m8: hk * D + 128 * (m8 + 1)],
                                  hbuf[:, hk * TCW:(hk + 1) * TCW],
                                  start=(hk == 0), stop=(hk == HCX - 1))
                eit = T.inc(mm)
                T.wait(v, eit)
                osl = oacc[:, m8 * TL + c0: m8 * TL + c0 + TCW]
                v.tensor_add(osl, osl, po[:, 0:TCW])
                ei = V.inc(v.drain())
                V.wait(t, ei)

    for m8 in range(JD):
        for tc in range(NTC):
            c0 = tc * TCW
            t.matmul(po[:, 0:TCW], b2b[0:32, 128 * m8:128 * (m8 + 1)],
                     wfmb[0:32, c0:c0 + TCW], start=True, stop=False)
            mm = t.matmul(po[:, 0:TCW], sbrow[:, 2 * HS + 128 * m8:2 * HS + 128 * (m8 + 1)],
                          ones_row[:, 0:TCW], start=False, stop=True)
            eit = T.inc(mm)
            T.wait(v, eit)
            osl = oacc[:, m8 * TL + c0: m8 * TL + c0 + TCW]
            v.tensor_add(osl, osl, po[:, 0:TCW])
            ei = V.inc(v.drain())
            V.wait(t, ei)
    for m8 in range(JD):
        V.wait(sy); Ld.wait(sy)
        sy.dma_start(out=xjs[:, 0:TL], in_=x_i[m8]).then_inc(Ld.h, 16)
        Ld.val += 16
        Ld.wait(v)
        St.wait(v)
        v.tensor_add(oacc[:, m8 * TL:(m8 + 1) * TL], oacc[:, m8 * TL:(m8 + 1) * TL],
                     xjs[:, 0:TL])
        ei = V.inc(v.drain())
        V.wait(g, ei)
        g.wait_ge(St.h, St.val)
        g.dma_start(out=xout_o[m8], in_=oacc[:, m8 * TL:(m8 + 1) * TL]).then_inc(St.h, 16)
        St.val += 16
    St.wait(g)
    return nc



def build_prog_J(TLOC):
    TL = B * TLOC
    NTC = max(1, TL // 512)
    TCW = min(512, TL)
    CH = 64
    NCHJ = TLOC // CH
    PI = float(np.pi)
    nc = bass.Bass("TRN2", target_bir_lowering=False, debug=False)
    x_i = din(nc, "x_i", [JD, 128, TL])
    Wk_i = din(nc, "Wk", [D, D]); bk_i = din(nc, "bk", [1, D])
    Wv_i = din(nc, "Wv", [D, D]); bv_i = din(nc, "bv", [1, D])
    Wq_i = din(nc, "Wq", [D, D]); bq_i = din(nc, "bq", [1, D])
    Wkp_i = din(nc, "Wkp", [D, D]); bkp_i = din(nc, "bkp", [1, D])
    Wqp_i = din(nc, "Wqp", [D, D]); bqp_i = din(nc, "bqp", [1, D])
    Wo_i = din(nc, "Wo", [D, D]); bo_i = din(nc, "bo", [1, D])
    ps_i = din(nc, "phs", [128, JD])
    lnwb = din(nc, "lnwb", [2, 128, JD])
    xout_o = dout(nc, "xout", [JD, 128, TL])
    xn1_s = dint(nc, "xn1_s", [JD, 128, TL])

    xnb = sb(nc, "xnb", [128, JD * TL], BF16)
    kqb = sb(nc, "kqb", [128, JD * TL], BF16)
    vbuf = sb(nc, "vbuf", [128, JD * TL], BF16)
    mrb = sb(nc, "mrb", [128, JD * TL], BF16)
    mib = sb(nc, "mib", [128, JD * TL], BF16)
    wtf = sb(nc, "wtf", [128, 512], F32)
    wtb = sb(nc, "wtb", [128, JD * D], BF16)
    brow = sb(nc, "brow", [65, 2 * D], BF16)

    def bro(q, lo, w_):
        p0 = 32 * (q % 3)
        c0_ = D * (q // 3)
        return brow[p0:p0 + 1, c0_ + lo:c0_ + lo + w_]

    def onr(q, w_):
        p0 = 32 * (q % 3)
        return ones_row[p0:p0 + 1, 0:w_]
    lnw_sb = sb(nc, "lnw_sb", [128, 2 * JD], F32)
    ps_sb = sb(nc, "ps_sb", [128, JD], F32)
    onescol = sb(nc, "onescol", [128, 1], BF16)
    ones_row = sb(nc, "ones_row", [65, TCW], BF16)
    st_f = sb(nc, "st_f", [1, 4 * TCW], F32)
    st_b = sb(nc, "st_b", [1, 2 * TCW], BF16)
    xb_c = sb(nc, "xb_c", [128, TCW], BF16)
    sq_c = sb(nc, "sq_c", [128, TCW], BF16)
    t_y = sb(nc, "t_y", [128, TCW], F32)
    t_z = sb(nc, "t_z", [128, TCW], F32)
    t_w = sb(nc, "t_w", [128, TCW], F32)
    pA = ps(nc, "pA", [128, TCW], F32)
    pB = ps(nc, "pB", [128, TCW], F32)
    pC = ps(nc, "pC", [128, TCW], F32)
    pD = ps(nc, "pD", [128, TCW], F32)

    Ld = Sem(nc, "Ld"); Lw = Sem(nc, "Lw"); St = Sem(nc, "St")
    T = Sem(nc, "T"); V = Sem(nc, "V"); S = Sem(nc, "S")
    v, s, t, g, sy = nc.vector, nc.scalar, nc.tensor, nc.gpsimd, nc.sync

    sy.dma_start(out=lnw_sb[:, 0:JD], in_=lnwb[0]).then_inc(Ld.h, 16); Ld.val += 16
    sy.dma_start(out=lnw_sb[:, JD:2 * JD], in_=lnwb[1]).then_inc(Ld.h, 16); Ld.val += 16
    sy.dma_start(out=ps_sb[:], in_=ps_i).then_inc(Ld.h, 16); Ld.val += 16
    for q, src in enumerate([bk_i, bv_i, bq_i, bkp_i, bqp_i, bo_i]):
        for hh in range(2):
            Ld.wait(sy); V.wait(sy)
            sy.dma_start(out=wtf[0:1, :], in_=src[:, 512 * hh:512 * (hh + 1)]).then_inc(Ld.h, 16)
            Ld.val += 16
            Ld.wait(v)
            V.inc(v.tensor_copy(bro(q, 512 * hh, 512), wtf[0:1, :]))
    v.memset(onescol[:], 1.0)
    v.memset(ones_row[:], 1.0)
    ei = V.inc(v.drain())
    V.wait(t, ei); V.wait(sy)

    def load_w(wsrc):
        for k in range(JD):
            for hh in range(2):
                Lw.wait(sy); V.wait(sy); T.wait(sy)
                sy.dma_start(out=wtf[:], in_=wsrc[128 * k:128 * (k + 1),
                                                  512 * hh:512 * (hh + 1)]).then_inc(Lw.h, 16)
                Lw.val += 16
                Lw.wait(v)
                V.inc(v.tensor_copy(wtb[:, k * D + 512 * hh: k * D + 512 * (hh + 1)], wtf[:]))
        ei = V.inc(v.drain())
        V.wait(t, ei)

    def proj(src_sb, dst_sb, brow_off):
        for m8 in range(JD):
            for tc in range(NTC):
                c0 = tc * TCW
                for k in range(JD):
                    t.matmul(pA[:, 0:TCW], wtb[:, k * D + 128 * m8: k * D + 128 * (m8 + 1)],
                             src_sb[:, k * TL + c0: k * TL + c0 + TCW],
                             start=(k == 0), stop=False)
                mm = t.matmul(pA[:, 0:TCW], bro(brow_off, 128 * m8, 128),
                              onr(brow_off, TCW), start=False, stop=True)
                eit = T.inc(mm)
                T.wait(v, eit)
                ei = V.inc(v.tensor_copy(dst_sb[:, m8 * TL + c0: m8 * TL + c0 + TCW],
                                         pA[:, 0:TCW]))
                V.wait(t, ei)

    def emit_trig(phase_src_psum, m8, tc):
        c0 = tc * TCW
        eit = T.inc(phase_src_psum)
        T.wait(s, eit)
        ei = S.inc(s.activation(t_y[:], pA[:, 0:TCW], AF.Tanh))
        S.wait(t, ei)
        S.wait(v, ei)
        v.tensor_scalar(t_y[:], t_y[:], ps_sb[:, m8:m8 + 1], None, ALU.mult)
        v.drain()
        v.tensor_scalar_mul(t_z[:], t_y[:], 1.0 / (2 * PI))
        v.drain()
        v.tensor_copy(t_w[:].bitcast(I32), t_z[:])
        v.drain()
        v.tensor_copy(t_z[:], t_w[:].bitcast(I32))
        v.drain()
        v.scalar_tensor_tensor(t_y[:], t_z[:], -2 * PI, t_y[:], ALU.mult, ALU.add)
        v.drain()
        v.tensor_scalar(t_z[:], t_y[:], PI, -2 * PI, ALU.is_gt, ALU.mult)
        v.drain()
        v.tensor_add(t_y[:], t_y[:], t_z[:])
        v.drain()
        v.tensor_scalar(t_z[:], t_y[:], -PI, 2 * PI, ALU.is_lt, ALU.mult)
        v.drain()
        v.tensor_add(t_y[:], t_y[:], t_z[:])
        v.drain()

    def sin_of(src_ap, dst_ap):
        ei = V.inc(v.drain())
        V.wait(s, ei)
        ei = S.inc(s.activation(dst_ap, src_ap, AF.Sin))
        S.wait(v, ei)

    for tc in range(NTC):
        c0 = tc * TCW
        V.wait(sy); S.wait(sy); T.wait(sy); Ld.wait(sy)
        for j in range(JD):
            sy.dma_start(out=t_w[:], in_=x_i[j][:, c0:c0 + TCW]).then_inc(Ld.h, 16)
            Ld.val += 16
            Ld.wait(v); Ld.wait(s)
            ei = V.inc(v.tensor_copy(xb_c[:], t_w[:]))
            eis = S.inc(s.activation(sq_c[:], t_w[:], AF.Square))
            V.wait(t, ei); S.wait(t, eis)
            t.matmul(pA[0:1, 0:TCW], onescol[:], xb_c[:], start=(j == 0), stop=(j == JD - 1))
            mm2 = t.matmul(pB[0:1, 0:TCW], onescol[:], sq_c[:], start=(j == 0), stop=(j == JD - 1))
            eit = T.inc(mm2)
            T.wait(v, eit); T.wait(s, eit); T.wait(sy, eit)
            V.wait(sy)
        v.tensor_scalar_mul(st_f[:, 0:TCW], pA[0:1, 0:TCW], 1.0 / D)
        v.drain()
        v.tensor_scalar_mul(st_f[:, TCW:2 * TCW], pB[0:1, 0:TCW], 1.0 / D)
        v.drain()
        v.tensor_mul(st_f[:, 3 * TCW:4 * TCW], st_f[:, 0:TCW], st_f[:, 0:TCW])
        v.drain()
        v.tensor_sub(st_f[:, TCW:2 * TCW], st_f[:, TCW:2 * TCW], st_f[:, 3 * TCW:4 * TCW])
        v.drain()
        v.tensor_scalar_add(st_f[:, TCW:2 * TCW], st_f[:, TCW:2 * TCW], 1e-5)
        ei = V.inc(v.drain())
        V.wait(s, ei)
        ei = S.inc(s.activation(st_f[:, 2 * TCW:3 * TCW], st_f[:, TCW:2 * TCW], AF.Sqrt))
        S.wait(v, ei)
        v.reciprocal(st_f[:, 2 * TCW:3 * TCW], st_f[:, 2 * TCW:3 * TCW])
        v.drain()
        v.tensor_copy(st_b[:, 0:TCW], st_f[:, 0:TCW])
        v.drain()
        ei = V.inc(v.tensor_copy(st_b[:, TCW:2 * TCW], st_f[:, 2 * TCW:3 * TCW]))
        V.wait(t, ei)
        mm1 = t.matmul(pC[:, 0:TCW], ones_row[0:1, 0:128], st_b[:, 0:TCW], start=True, stop=True)
        mm2 = t.matmul(pD[:, 0:TCW], ones_row[0:1, 0:128], st_b[:, TCW:2 * TCW], start=True, stop=True)
        eit = T.inc(mm2)
        T.wait(v, eit)
        T.wait(sy, eit)
        for j in range(JD):
            sy.dma_start(out=t_w[:], in_=x_i[j][:, c0:c0 + TCW]).then_inc(Ld.h, 16)
            Ld.val += 16
            Ld.wait(v)
            v.tensor_sub(t_y[:], t_w[:], pC[:, 0:TCW])
            v.drain()
            v.tensor_mul(t_z[:], t_y[:], pD[:, 0:TCW])
            v.drain()
            ei = V.inc(v.tensor_copy(xnb[:, j * TL + c0: j * TL + c0 + TCW], t_z[:]))
            V.wait(g, ei)
            g.wait_ge(St.h, St.val)
            g.dma_start(out=xn1_s[j][:, c0:c0 + TCW], in_=t_z[:]).then_inc(St.h, 16)
            St.val += 16
            St.wait(v)
            V.wait(sy)
        V.wait(t, V.val)

    load_w(Wv_i)
    proj(xnb, vbuf, 1)
    load_w(Wk_i)
    proj(xnb, kqb, 0)
    load_w(Wkp_i)
    for m8 in range(JD):
        for tc in range(NTC):
            c0 = tc * TCW
            for k in range(JD):
                t.matmul(pA[:, 0:TCW], wtb[:, k * D + 128 * m8: k * D + 128 * (m8 + 1)],
                         kqb[:, k * TL + c0: k * TL + c0 + TCW],
                         start=(k == 0), stop=False)
            mm = t.matmul(pA[:, 0:TCW], bro(3, 128 * m8, 128),
                          onr(3, TCW), start=False, stop=True)
            emit_trig(mm, m8, tc)
            vsl = vbuf[:, m8 * TL + c0: m8 * TL + c0 + TCW]
            sin_of(t_y[:], t_w[:])
            v.tensor_mul(mib[:, m8 * TL + c0: m8 * TL + c0 + TCW], vsl, t_w[:])
            v.drain()
            v.tensor_scalar_add(t_y[:], t_y[:], PI / 2)
            v.drain()
            v.tensor_scalar(t_z[:], t_y[:], PI, -2 * PI, ALU.is_gt, ALU.mult)
            v.drain()
            v.tensor_add(t_y[:], t_y[:], t_z[:])
            sin_of(t_y[:], t_w[:])
            ei = V.inc(v.tensor_mul(mrb[:, m8 * TL + c0: m8 * TL + c0 + TCW], vsl, t_w[:]))
            V.wait(s, ei)

    for j in range(JD):
        jo = j * TL
        for b_ in range(B):
            for ch in range(NCHJ):
                off = jo + ch * CH * B + b_
                e_ = off + (CH - 1) * B + 1
                v.tensor_tensor_scan(mrb[:, off:e_:B], mrb[:, off:e_:B],
                                     mrb[:, off:e_:B], 0.0, ALU.add, ALU.bypass)
                v.tensor_tensor_scan(mib[:, off:e_:B], mib[:, off:e_:B],
                                     mib[:, off:e_:B], 0.0, ALU.add, ALU.bypass)
        v.drain()

    load_w(Wq_i)
    proj(xnb, kqb, 2)
    load_w(Wqp_i)
    for m8 in range(JD):
        for tc in range(NTC):
            c0 = tc * TCW
            sl = slice(m8 * TL + c0, m8 * TL + c0 + TCW)
            for k in range(JD):
                t.matmul(pA[:, 0:TCW], wtb[:, k * D + 128 * m8: k * D + 128 * (m8 + 1)],
                         kqb[:, k * TL + c0: k * TL + c0 + TCW],
                         start=(k == 0), stop=False)
            mm = t.matmul(pA[:, 0:TCW], bro(4, 128 * m8, 128),
                          onr(4, TCW), start=False, stop=True)
            emit_trig(mm, m8, tc)
            sin_of(t_y[:], t_w[:])
            v.tensor_mul(t_z[:], mib[:, sl], t_w[:])
            v.drain()
            v.tensor_scalar_add(t_y[:], t_y[:], PI / 2)
            v.drain()
            v.tensor_scalar(t_w[:], t_y[:], PI, -2 * PI, ALU.is_gt, ALU.mult)
            v.drain()
            v.tensor_add(t_y[:], t_y[:], t_w[:])
            sin_of(t_y[:], t_w[:])
            v.tensor_mul(t_y[:], mrb[:, sl], t_w[:])
            v.drain()
            ei = V.inc(v.tensor_add(mrb[:, sl], t_y[:], t_z[:]))
            V.wait(s, ei)

    for tc in range(NTC):
        c0 = tc * TCW
        S.wait(v); T.wait(v)
        for j in range(JD):
            xsl = mrb[:, j * TL + c0: j * TL + c0 + TCW]
            eis = S.inc(s.activation(sq_c[:], xsl, AF.Square))
            S.wait(t, eis)
            t.matmul(pA[0:1, 0:TCW], onescol[:], xsl, start=(j == 0), stop=(j == JD - 1))
            mm2 = t.matmul(pB[0:1, 0:TCW], onescol[:], sq_c[:], start=(j == 0), stop=(j == JD - 1))
            eit = T.inc(mm2)
            T.wait(s, eit)
        T.wait(v)
        v.tensor_scalar_mul(st_f[:, 0:TCW], pA[0:1, 0:TCW], 1.0 / D)
        v.drain()
        v.tensor_scalar_mul(st_f[:, TCW:2 * TCW], pB[0:1, 0:TCW], 1.0 / D)
        v.drain()
        v.tensor_mul(st_f[:, 3 * TCW:4 * TCW], st_f[:, 0:TCW], st_f[:, 0:TCW])
        v.drain()
        v.tensor_sub(st_f[:, TCW:2 * TCW], st_f[:, TCW:2 * TCW], st_f[:, 3 * TCW:4 * TCW])
        v.drain()
        v.tensor_scalar_add(st_f[:, TCW:2 * TCW], st_f[:, TCW:2 * TCW], float(D) * 1e-5)
        ei = V.inc(v.drain())
        V.wait(s, ei)
        ei = S.inc(s.activation(st_f[:, 2 * TCW:3 * TCW], st_f[:, TCW:2 * TCW], AF.Sqrt))
        S.wait(v, ei)
        v.reciprocal(st_f[:, 2 * TCW:3 * TCW], st_f[:, 2 * TCW:3 * TCW])
        v.drain()
        v.tensor_copy(st_b[:, 0:TCW], st_f[:, 0:TCW])
        v.drain()
        ei = V.inc(v.tensor_copy(st_b[:, TCW:2 * TCW], st_f[:, 2 * TCW:3 * TCW]))
        V.wait(t, ei)
        mm1 = t.matmul(pC[:, 0:TCW], ones_row[0:1, 0:128], st_b[:, 0:TCW], start=True, stop=True)
        mm2 = t.matmul(pD[:, 0:TCW], ones_row[0:1, 0:128], st_b[:, TCW:2 * TCW], start=True, stop=True)
        eit = T.inc(mm2)
        T.wait(v, eit)
        for j in range(JD):
            sl = slice(j * TL + c0, j * TL + c0 + TCW)
            v.tensor_sub(t_y[:], mrb[:, sl], pC[:, 0:TCW])
            v.drain()
            v.tensor_mul(t_y[:], t_y[:], pD[:, 0:TCW])
            v.drain()
            v.tensor_scalar(t_y[:], t_y[:], lnw_sb[:, j:j + 1], lnw_sb[:, JD + j:JD + j + 1],
                            ALU.mult, ALU.add)
            v.drain()
            v.tensor_copy(mrb[:, sl], t_y[:])
            v.drain()
        ei = V.inc(v.drain())
        V.wait(t, ei)

    load_w(Wo_i)
    for m8 in range(JD):
        for tc in range(NTC):
            c0 = tc * TCW
            for k in range(JD):
                t.matmul(pA[:, 0:TCW], wtb[:, k * D + 128 * m8: k * D + 128 * (m8 + 1)],
                         mrb[:, k * TL + c0: k * TL + c0 + TCW],
                         start=(k == 0), stop=False)
            mm = t.matmul(pA[:, 0:TCW], bro(5, 128 * m8, 128),
                          onr(5, TCW), start=False, stop=True)
            eit = T.inc(mm)
            T.wait(sy, eit)
            V.wait(sy)
            sy.dma_start(out=t_w[:], in_=x_i[m8][:, c0:c0 + TCW]).then_inc(Ld.h, 16)
            Ld.val += 16
            sy.dma_start(out=t_z[:], in_=xn1_s[m8][:, c0:c0 + TCW]).then_inc(Ld.h, 16)
            Ld.val += 16
            T.wait(v, eit)
            Ld.wait(v)
            v.tensor_add(t_y[:], pA[:, 0:TCW], t_w[:])
            v.drain()
            ei = V.inc(v.tensor_add(t_y[:], t_y[:], t_z[:]))
            V.wait(t, ei)
            V.wait(g, ei)
            g.wait_ge(St.h, St.val)
            g.dma_start(out=xout_o[m8][:, c0:c0 + TCW], in_=t_y[:]).then_inc(St.h, 16)
            St.val += 16
            St.wait(v)
    St.wait(g)
    return nc



def build_prog_H(TLOC):
    import os as _os
    TL = B * TLOC
    NTC = max(1, TL // 512)
    TCW = min(512, TL)
    EH = 2 * D
    NPT = TL // 128
    NG = NPT // 4

    nc = bass.Bass("TRN2", target_bir_lowering=False, debug=False)
    x_i = din(nc, "x_i", [JD, 128, TL])
    lnwb = din(nc, "lnwb", [2, 128, JD])
    wuT = din(nc, "wuT", [D, EH])
    bu_i = din(nc, "bu", [1, EH])
    wvT = din(nc, "wvT", [D, EH])
    bv_i = din(nc, "bv", [1, EH])
    wbT = din(nc, "wbT", [D, 128])
    bb_i = din(nc, "bb", [128, 1])
    qaff = din(nc, "qaff", [128, 4])
    woT = din(nc, "woT", [EH, D])
    bo_i = din(nc, "bo", [1, D])
    wrel = din(nc, "wrel", [1, 1023])
    costab = din(nc, "costab", [128, TL], BF16)
    sintab = din(nc, "sintab", [128, TL], BF16)
    maskt = din(nc, "maskt", [128, 128], BF16)
    aind = din(nc, "aind", [4, 128], BF16)
    identb = din(nc, "identb", [128, 128], BF16)
    xout_o = dout(nc, "xout", [JD, 128, TL])
    xn1_s = dint(nc, "xn1_s", [JD, 128, TL])

    xnb = sb(nc, "xnb", [128, JD * TL], BF16)
    wub = sb(nc, "wub", [128, JD * EH], BF16)
    wvb = sb(nc, "wvb", [128, JD * EH], BF16)
    wob = sb(nc, "wob", [128, 16 * D], BF16)
    wtf = sb(nc, "wtf", [128, 1024], F32)
    base_b = sb(nc, "base_b", [128, TL], BF16)
    qq_b = sb(nc, "qq_b", [128, TL], BF16)
    qk_b = sb(nc, "qk_b", [128, TL], BF16)
    cs_sb = sb(nc, "cs_sb", [128, max(TL, EH)], BF16)
    sn_sb = sb(nc, "sn_sb", [128, max(TL, EH)], BF16)
    u_tm = cs_sb
    v_tm = sn_sb
    prod_fm = sb(nc, "prod_fm", [128, 16 * 512], BF16)
    mask_sb = sb(nc, "mask_sb", [128, 128], BF16)
    aind_sb = sb(nc, "aind_sb", [4, 128], BF16)
    b4_sb = sb(nc, "b4_sb", [4, 128], BF16)
    idb = sb(nc, "idb", [128, 128], BF16)
    w7 = sb(nc, "w7", [1, 16], F32)
    w7b = sb(nc, "w7b", [1, 128], BF16)
    lnw_sb = sb(nc, "lnw_sb", [128, 2 * JD], F32)
    bbp = sb(nc, "bbp", [128, 1], F32)
    qaf = sb(nc, "qaf", [128, 8], F32)
    brow = sb(nc, "brow", [1, 2 * EH + D], BF16)
    onescol = sb(nc, "onescol", [128, 1], BF16)
    ones_row = sb(nc, "ones_row", [1, TCW], BF16)
    st_f = sb(nc, "st_f", [1, 4 * TCW], F32)
    st_b = sb(nc, "st_b", [1, 2 * TCW], BF16)
    xb_c = sb(nc, "xb_c", [128, TCW], BF16)
    sq_c = sb(nc, "sq_c", [128, TCW], BF16)
    t_y = sb(nc, "t_y", [128, TCW], F32)
    t_z = sb(nc, "t_z", [128, TCW], F32)
    t_w = sb(nc, "t_w", [128, TCW], F32)
    sG = sb(nc, "sG", [128, 128], F32)
    sGb = sb(nc, "sGb", [128, 128], BF16)
    sS = sb(nc, "sS", [128, 128], BF16)
    pA = ps(nc, "pA", [128, TCW], F32)
    pB = ps(nc, "pB", [128, TCW], F32)
    pC = ps(nc, "pC", [128, TCW], F32)
    pD = ps(nc, "pD", [128, TCW], F32)
    pT = ps(nc, "pT", [128, 128], BF16)

    Ld = Sem(nc, "Ld"); Lw = Sem(nc, "Lw"); St = Sem(nc, "St")
    T = Sem(nc, "T"); V = Sem(nc, "V"); S = Sem(nc, "S")
    v, s, t, g, sy = nc.vector, nc.scalar, nc.tensor, nc.gpsimd, nc.sync

    for dst, src in [(cs_sb[:, 0:TL], costab), (sn_sb[:, 0:TL], sintab), (mask_sb, maskt),
                     (aind_sb, aind), (idb, identb), (lnw_sb[:, 0:JD], lnwb[0]),
                     (lnw_sb[:, JD:2 * JD], lnwb[1]), (bbp, bb_i), (qaf[:, 0:4], qaff)]:
        sy.dma_start(out=dst if not hasattr(dst, "shape") else dst[:] if hasattr(dst, "ap") else dst,
                     in_=src).then_inc(Ld.h, 16)
        Ld.val += 16
    sy.dma_start(out=w7[0:1, 0:7], in_=wrel[:, 508:515]).then_inc(Ld.h, 16)
    Ld.val += 16
    Ld.wait(v)
    v.tensor_scalar_mul(qaf[:, 4:8], qaf[:, 0:4], 0.125)
    v.drain()
    v.memset(onescol[:], 1.0)
    v.memset(ones_row[:], 1.0)
    for r in range(4):
        for b_ in range(4):
            v.tensor_copy(w7b[0:1, b_:b_ + 125:4],
                          w7[0:1, 3 - r + b_:3 - r + b_ + 1].to_broadcast([1, 32]))
        ei = V.inc(v.drain())
        V.wait(g, ei)
        g.wait_ge(St.h, St.val)
        g.dma_start(out=b4_sb[r:r + 1, :].bitcast(BF16) if False else b4_sb[r:r + 1, :],
                    in_=w7b[0:1, :]).then_inc(St.h, 16)
        St.val += 16
        St.wait(v)
    ei = V.inc(v.drain())
    V.wait(t, ei)

    def load_cast(dst, src, cols, koff=0):
        for k in range(JD if cols != 16 else 16):
            pass
    for k in range(JD):
        for hh in range(2):
            Lw.wait(sy); V.wait(sy)
            sy.dma_start(out=wtf[:], in_=wuT[128 * k:128 * (k + 1),
                                            1024 * hh:1024 * (hh + 1)]).then_inc(Lw.h, 16)
            Lw.val += 16
            Lw.wait(v)
            V.inc(v.tensor_copy(wub[:, k * EH + 1024 * hh: k * EH + 1024 * (hh + 1)], wtf[:]))
    for k in range(JD):
        for hh in range(2):
            Lw.wait(sy); V.wait(sy)
            sy.dma_start(out=wtf[:], in_=wvT[128 * k:128 * (k + 1),
                                            1024 * hh:1024 * (hh + 1)]).then_inc(Lw.h, 16)
            Lw.val += 16
            Lw.wait(v)
            V.inc(v.tensor_copy(wvb[:, k * EH + 1024 * hh: k * EH + 1024 * (hh + 1)], wtf[:]))
    for k in range(16):
        Lw.wait(sy); V.wait(sy)
        sy.dma_start(out=wtf[:], in_=woT[128 * k:128 * (k + 1), :]).then_inc(Lw.h, 16)
        Lw.val += 16
        Lw.wait(v)
        V.inc(v.tensor_copy(wob[:, k * D:(k + 1) * D], wtf[:]))
    wbb = sb(nc, "wbb", [128, JD * 128], BF16)
    for k in range(JD):
        Lw.wait(sy); V.wait(sy)
        sy.dma_start(out=wtf[:, 0:128], in_=wbT[128 * k:128 * (k + 1), :]).then_inc(Lw.h, 16)
        Lw.val += 16
        Lw.wait(v)
        V.inc(v.tensor_copy(wbb[:, k * 128:(k + 1) * 128], wtf[:, 0:128]))
    for q, (src, w_) in enumerate([(bu_i, EH), (bv_i, EH), (bo_i, D)]):
        for hh in range(w_ // 1024):
            Lw.wait(sy); V.wait(sy)
            sy.dma_start(out=wtf[0:1, :], in_=src[:, 1024 * hh:1024 * (hh + 1)]).then_inc(Lw.h, 16)
            Lw.val += 16
            Lw.wait(v)
            V.inc(v.tensor_copy(brow[0:1, q * EH + 1024 * hh: q * EH + 1024 * (hh + 1)],
                                wtf[0:1, :]))
    ei = V.inc(v.drain())
    V.wait(t, ei)

    yjs = prod_fm.bitcast(F32)
    for tc in range(NTC):
        c0 = tc * TCW

        def ln_pass(src_getter, eps):
            for j in range(JD):
                xsl = src_getter(j)
                ei = V.inc(v.tensor_copy(xb_c[:], xsl))
                eis = S.inc(s.activation(sq_c[:], xsl, AF.Square))
                V.wait(t, ei); S.wait(t, eis)
                t.matmul(pA[0:1, 0:TCW], onescol[:], xb_c[:], start=(j == 0), stop=(j == JD - 1))
                mm2 = t.matmul(pB[0:1, 0:TCW], onescol[:], sq_c[:], start=(j == 0), stop=(j == JD - 1))
                eit = T.inc(mm2)
                T.wait(v, eit); T.wait(s, eit)
            v.tensor_scalar_mul(st_f[:, 0:TCW], pA[0:1, 0:TCW], 1.0 / D)
            v.drain()
            v.tensor_scalar_mul(st_f[:, TCW:2 * TCW], pB[0:1, 0:TCW], 1.0 / D)
            v.drain()
            v.tensor_mul(st_f[:, 3 * TCW:4 * TCW], st_f[:, 0:TCW], st_f[:, 0:TCW])
            v.drain()
            v.tensor_sub(st_f[:, TCW:2 * TCW], st_f[:, TCW:2 * TCW], st_f[:, 3 * TCW:4 * TCW])
            v.drain()
            v.tensor_scalar_add(st_f[:, TCW:2 * TCW], st_f[:, TCW:2 * TCW], eps)
            ei = V.inc(v.drain())
            V.wait(s, ei)
            ei = S.inc(s.activation(st_f[:, 2 * TCW:3 * TCW], st_f[:, TCW:2 * TCW], AF.Sqrt))
            S.wait(v, ei)
            v.reciprocal(st_f[:, 2 * TCW:3 * TCW], st_f[:, 2 * TCW:3 * TCW])
            v.drain()
            v.tensor_copy(st_b[:, 0:TCW], st_f[:, 0:TCW])
            v.drain()
            ei = V.inc(v.tensor_copy(st_b[:, TCW:2 * TCW], st_f[:, 2 * TCW:3 * TCW]))
            V.wait(t, ei)
            mm1 = t.matmul(pC[:, 0:TCW], ones_row[0:1, 0:128], st_b[:, 0:TCW], start=True, stop=True)
            mm2 = t.matmul(pD[:, 0:TCW], ones_row[0:1, 0:128], st_b[:, TCW:2 * TCW], start=True, stop=True)
            eit = T.inc(mm2)
            T.wait(v, eit)

        V.wait(sy); S.wait(sy); T.wait(sy); Ld.wait(sy)
        for j in range(JD):
            sy.dma_start(out=yjs[:, j * TCW:(j + 1) * TCW],
                         in_=x_i[j][:, c0:c0 + TCW]).then_inc(Ld.h, 16)
        Ld.val += 16 * JD
        Ld.wait(v); Ld.wait(s)
        S.wait(v)
        ln_pass(lambda j: yjs[:, j * TCW:(j + 1) * TCW], 1e-5)
        for j in range(JD):
            v.tensor_sub(t_y[:], yjs[:, j * TCW:(j + 1) * TCW], pC[:, 0:TCW])
            v.drain()
            ei = V.inc(v.tensor_mul(yjs[:, j * TCW:(j + 1) * TCW], t_y[:], pD[:, 0:TCW]))
            v.drain()
            V.wait(g, ei)
            g.wait_ge(St.h, St.val)
            g.dma_start(out=xn1_s[j][:, c0:c0 + TCW],
                        in_=yjs[:, j * TCW:(j + 1) * TCW]).then_inc(St.h, 16)
            St.val += 16
        St.wait(v)
        v.drain()
        V.wait(s); V.wait(t)
        ln_pass(lambda j: yjs[:, j * TCW:(j + 1) * TCW], 1e-5)
        for j in range(JD):
            v.tensor_sub(t_y[:], yjs[:, j * TCW:(j + 1) * TCW], pC[:, 0:TCW])
            v.drain()
            v.tensor_mul(t_z[:], t_y[:], pD[:, 0:TCW])
            v.drain()
            v.tensor_scalar(t_z[:], t_z[:], lnw_sb[:, j:j + 1], lnw_sb[:, JD + j:JD + j + 1],
                            ALU.mult, ALU.add)
            v.drain()
            v.tensor_copy(xnb[:, j * TL + c0: j * TL + c0 + TCW], t_z[:])
            v.drain()
        ei = V.inc(v.drain())
        V.wait(t, ei)

    if _os.environ.get("HDBG") == "1":
        for j in range(JD):
            V.wait(g, V.inc(v.tensor_copy(t_y[:, 0:TCW], xnb[:, j * TL: j * TL + TCW])))
            g.wait_ge(St.h, St.val)
            g.dma_start(out=xout_o[j][:, 0:TCW], in_=t_y[:, 0:TCW]).then_inc(St.h, 16)
            St.val += 16
            St.wait(v)
        St.wait(g)
        return nc
    for tc in range(NTC):
        c0 = tc * TCW
        for k in range(JD):
            mm = t.matmul(pA[:, 0:TCW], wbb[:, k * 128:(k + 1) * 128],
                          xnb[:, k * TL + c0: k * TL + c0 + TCW],
                          start=(k == 0), stop=(k == JD - 1))
        eit = T.inc(mm)
        T.wait(s, eit)
        ei = S.inc(s.activation(base_b[:, c0:c0 + TCW], pA[:, 0:TCW], AF.Silu, bias=bbp[:]))
        S.wait(t, ei); S.wait(v, ei)
    for (dst, wq, bq2) in [(qq_b, 4, 5), (qk_b, 6, 7)]:
        v.tensor_scalar(t_y[:, 0:128] if False else dst[:], base_b[:],
                        qaf[:, wq:wq + 1], qaf[:, bq2:bq2 + 1], ALU.mult, ALU.add)
        v.drain()
    for (dst,) in [(qq_b,), (qk_b,)]:
        for tc in range(NTC):
            c0 = tc * TCW
            v.tensor_copy(t_w[0:64, 0:TCW], dst[64:128, c0:c0 + TCW])
            v.drain()
            v.tensor_copy(t_w[64:128, 0:TCW], dst[0:64, c0:c0 + TCW])
            v.drain()
            v.tensor_mul(t_y[0:64, 0:TCW], dst[0:64, c0:c0 + TCW], cs_sb[0:64, c0:c0 + TCW])
            v.drain()
            v.tensor_mul(t_z[0:64, 0:TCW], t_w[0:64, 0:TCW], sn_sb[0:64, c0:c0 + TCW])
            v.drain()
            v.tensor_mul(t_y[64:128, 0:TCW], dst[64:128, c0:c0 + TCW], cs_sb[64:128, c0:c0 + TCW])
            v.drain()
            v.tensor_mul(t_z[64:128, 0:TCW], t_w[64:128, 0:TCW], sn_sb[64:128, c0:c0 + TCW])
            v.drain()
            v.tensor_sub(dst[0:64, c0:c0 + TCW], t_y[0:64, 0:TCW], t_z[0:64, 0:TCW])
            v.drain()
            v.tensor_add(dst[64:128, c0:c0 + TCW], t_y[64:128, 0:TCW], t_z[64:128, 0:TCW])
            v.drain()
    ei = V.inc(v.drain())
    V.wait(t, ei)

    if _os.environ.get("HDBG") == "2":
        for (nm, buf) in [(0, base_b), (1, qq_b), (2, qk_b)]:
            V.wait(g, V.inc(v.tensor_copy(t_y[:, 0:TCW], buf[:, 0:TCW])))
            g.wait_ge(St.h, St.val)
            g.dma_start(out=xout_o[nm][:, 0:TCW], in_=t_y[:, 0:TCW]).then_inc(St.h, 16)
            St.val += 16
            St.wait(v)
        St.wait(g)
        return nc
    for tg in range(NG):
        for ti in range(4):
            pt0 = (tg * 4 + ti) * 128
            for (dstb, wsrc, bq) in [(u_tm, wub, 0), (v_tm, wvb, 1)]:
                for ec in range(4):
                    for k in range(JD):
                        t.matmul(pA[:, 0:TCW],
                                 xnb[:, k * TL + pt0: k * TL + pt0 + 128],
                                 wsrc[:, k * EH + 512 * ec: k * EH + 512 * (ec + 1)],
                                 start=(k == 0), stop=False)
                    mm = t.matmul(pA[:, 0:TCW], ones_row[0:1, 0:128],
                                  brow[0:1, bq * EH + 512 * ec: bq * EH + 512 * (ec + 1)],
                                  start=False, stop=True)
                    eit = T.inc(mm)
                    T.wait(s, eit)
                    ei = S.inc(s.activation(dstb[:, 512 * ec:512 * (ec + 1)], pA[:, 0:TCW], AF.Silu))
                    S.wait(t, ei)
            t.matmul(pA[:, 0:128], qq_b[:, pt0:pt0 + 128], qk_b[:, pt0:pt0 + 128],
                     start=True, stop=False)
            mm = t.matmul(pA[:, 0:128], aind_sb[0:4, :], b4_sb[0:4, :], start=False, stop=True)
            eit = T.inc(mm)
            T.wait(v, eit)
            v.tensor_relu(sG[:], pA[:, 0:128])
            ei = V.inc(v.drain())
            V.wait(s, ei); V.wait(t, ei)
            ei = S.inc(s.activation(sG[:], sG[:], AF.Square))
            S.wait(v, ei)
            ei = V.inc(v.tensor_mul(sGb[:], sG[:], mask_sb[:]))
            V.wait(t, ei)
            mm = t.transpose(pT[:, 0:128], sGb[:], idb[:])
            eit = T.inc(mm)
            T.wait(v, eit)
            ei = V.inc(v.tensor_copy(sS[:], pT[:, 0:128]))
            V.wait(t, ei)
            for ec in range(4):
                mm = t.matmul(pB[:, 0:TCW], sS[:], v_tm[:, 512 * ec:512 * (ec + 1)],
                              start=True, stop=True)
                eit = T.inc(mm)
                T.wait(v, eit)
                ei = V.inc(v.tensor_mul(xb_c[:], u_tm[:, 512 * ec:512 * (ec + 1)], pB[:, 0:TCW]))
                V.wait(t, ei)
                for q4 in range(4):
                    mm = t.transpose(pT[:, 0:128], xb_c[:, 128 * q4:128 * (q4 + 1)], idb[:])
                    eit = T.inc(mm)
                    T.wait(v, eit)
                    ei = V.inc(v.tensor_copy(
                        prod_fm[:, (ec * 4 + q4) * 512 + ti * 128:(ec * 4 + q4) * 512 + ti * 128 + 128],
                        pT[:, 0:128]))
                    V.wait(t, ei)
                    V.wait(s, ei) if False else None
        if _os.environ.get("HDBG") == "3" and tg == 0:
            for (nm, buf, wd) in [(0, u_tm, EH), (1, v_tm, EH), (2, sS, 128), (3, prod_fm, 512)]:
                wd2 = min(wd, TCW)
                V.wait(g, V.inc(v.tensor_copy(t_y[:, 0:wd2], buf[:, 0:wd2])))
                g.wait_ge(St.h, St.val)
                g.dma_start(out=xout_o[nm][:, 0:wd2], in_=t_y[:, 0:wd2]).then_inc(St.h, 16)
                St.val += 16
                St.wait(v)
            St.wait(g)
            return nc
        gc0 = tg * 512
        for m8 in range(JD):
            for ke in range(16):
                t.matmul(pC[:, 0:TCW], wob[:, ke * D + 128 * m8: ke * D + 128 * (m8 + 1)],
                         prod_fm[:, ke * 512:(ke + 1) * 512],
                         start=(ke == 0), stop=False)
            mm = t.matmul(pC[:, 0:TCW], ones_row[0:1, 0:TCW] if False else brow[0:1, 2 * EH + 128 * m8: 2 * EH + 128 * (m8 + 1)],
                          ones_row[0:1, 0:TCW], start=False, stop=True)
            eit = T.inc(mm)
            T.wait(sy, eit)
            V.wait(sy)
            sy.dma_start(out=t_w[:], in_=x_i[m8][:, gc0:gc0 + TCW]).then_inc(Ld.h, 16)
            Ld.val += 16
            sy.dma_start(out=t_z[:], in_=xn1_s[m8][:, gc0:gc0 + TCW]).then_inc(Ld.h, 16)
            Ld.val += 16
            T.wait(v, eit)
            Ld.wait(v)
            v.tensor_add(t_y[:], pC[:, 0:TCW], t_w[:])
            v.drain()
            ei = V.inc(v.tensor_add(t_y[:], t_y[:], t_z[:]))
            V.wait(t, ei)
            V.wait(g, ei)
            g.wait_ge(St.h, St.val)
            g.dma_start(out=xout_o[m8][:, gc0:gc0 + TCW], in_=t_y[:]).then_inc(St.h, 16)
            St.val += 16
            St.wait(v)
    St.wait(g)
    return nc



_PROGS = {}


def _get_progs(TLOC):
    if "A" not in _PROGS:
        _PROGS["A"] = build_prog_A(TLOC)
        _PROGS["B"] = build_prog_B(TLOC)
        _PROGS["C"] = build_prog_C(TLOC)
        _PROGS["D"] = build_prog_D(TLOC)
        _PROGS["E"] = build_prog_E(NCORES * TLOC, NCORES)
        _PROGS["F"] = build_prog_F(TLOC)
        _PROGS["H"] = build_prog_H(TLOC)
        _PROGS["I"] = build_prog_I(TLOC)
        _PROGS["J"] = build_prog_J(TLOC)
    return _PROGS


def _from_fm(xo, TLOC):
    return np.asarray(xo).reshape(D, TLOC, B).transpose(2, 1, 0)


def kernel(**inp):
    TLOC = 512
    TL = B * TLOC
    P = _get_progs(TLOC)
    run = lambda nc_, ins: run_bass_kernel_spmd(nc_, ins, core_ids=list(range(NCORES))).results

    vec = lambda a: np.ascontiguousarray(np.asarray(a, np.float32).reshape(-1))
    x0 = np.asarray(inp["L0_attn_a_x"], np.float32)

    mixes = [vec(inp[f"L0_attn_a_x_{m}"]) for m in "rwkvag"]
    wlist = [np.asarray(inp[f"L0_attn_a_W_{m}"], np.float32) for m in "rwkvag"]
    insA = [host_prep_A(x0, mixes, wlist, c, TLOC) for c in range(NCORES)]
    outA = run(P["A"], insA)
    insB = [host_prep_B(outA, vec(inp["L0_attn_a_gn_w"]), vec(inp["L0_attn_a_gn_b"]),
                        np.asarray(inp["L0_attn_a_W_out"], np.float32),
                        vec(inp["L0_attn_a_b_out"]), c, TLOC) for c in range(NCORES)]
    outB = run(P["B"], insB)
    X = [np.asarray(o["xout"]) for o in outB]

    statC = host_prep_C(None, np.asarray(inp["L0_ffn_gate_w"], np.float32),
                        vec(inp["L0_ffn_gate_b"]),
                        np.asarray(inp["L0_ffn_w1"], np.float32),
                        np.asarray(inp["L0_ffn_b1"], np.float32),
                        np.asarray(inp["L0_ffn_w2"], np.float32),
                        np.asarray(inp["L0_ffn_b2"], np.float32), TLOC)
    insC = []
    for c in range(NCORES):
        d = dict(statC); d["x_i"] = X[c]; insC.append(d)
    outC = run(P["C"], insC)
    X = [np.asarray(o["xout"]) for o in outC]

    insD = [{
        "x_i": X[c],
        "winT": np.ascontiguousarray(np.asarray(inp["L0_attn_b_W_in"], np.float32).T),
        "b_in": vec(inp["L0_attn_b_b_in"]).reshape(1, -1),
        "b_h": vec(inp["L0_attn_b_b_h"]).reshape(1, -1),
        "lnwb": np.stack([vec(inp["L0_attn_b_ln_w"]).reshape(JD, 128).T,
                          vec(inp["L0_attn_b_ln_b"]).reshape(JD, 128).T]).astype(np.float32),
    } for c in range(NCORES)]
    outD = run(P["D"], insD)
    g_all = np.stack([np.asarray(o["g_o"]) for o in outD])
    vb_all = np.stack([np.asarray(o["vb_o"]) for o in outD])
    insE = {"g_i": g_all, "vb_i": vb_all,
            "whT": np.ascontiguousarray(np.asarray(inp["L0_attn_b_W_h"], np.float32).T)}
    outE = run(P["E"], [insE] * NCORES)
    hs_full = np.asarray(outE[0]["hs_o"])
    insF = [{
        "x_i": X[c], "xn2_i": np.asarray(outD[c]["xn2_o"]),
        "hs_i": np.ascontiguousarray(hs_full[:, :, c * TL:(c + 1) * TL]),
        "woutT": np.ascontiguousarray(np.asarray(inp["L0_attn_b_W_out"], np.float32).T),
        "bout": vec(inp["L0_attn_b_b_out"]).reshape(1, -1),
    } for c in range(NCORES)]
    outF = run(P["F"], insF)
    X = [np.asarray(o["xout"]) for o in outF]

    insC2 = []
    for c in range(NCORES):
        d = dict(statC); d["x_i"] = X[c]; insC2.append(d)
    outC2 = run(P["C"], insC2)
    X = [np.asarray(o["xout"]) for o in outC2]

    invf = 10000.0 ** (-np.arange(64, dtype=np.float32) / 64)
    pos_col = np.tile(np.arange(B, dtype=np.float32), TLOC)
    sinus = pos_col[None, :] * np.concatenate([invf, invf])[:, None]
    costab = np.cos(sinus).astype(BF)
    sintab = np.sin(sinus).astype(BF)
    maskt = np.zeros((128, 128), BF)
    for p_ in range(128):
        for q_ in range(128):
            if p_ // 4 == q_ // 4 and q_ % 4 <= p_ % 4:
                maskt[p_, q_] = 1
    aind = np.zeros((4, 128), BF)
    for r_ in range(4):
        aind[r_, r_::4] = 1
    statH = {
        "lnwb": np.stack([vec(inp["L1_attn_a_ln_g"]).reshape(JD, 128).T,
                          vec(inp["L1_attn_a_ln_b"]).reshape(JD, 128).T]).astype(np.float32),
        "wuT": np.ascontiguousarray(np.asarray(inp["L1_attn_a_Wu"], np.float32).T),
        "bu": vec(inp["L1_attn_a_bu"]).reshape(1, -1),
        "wvT": np.ascontiguousarray(np.asarray(inp["L1_attn_a_Wv"], np.float32).T),
        "bv": vec(inp["L1_attn_a_bv"]).reshape(1, -1),
        "wbT": np.ascontiguousarray(np.asarray(inp["L1_attn_a_Wb"], np.float32).T),
        "bb": vec(inp["L1_attn_a_bb"]).reshape(128, 1),
        "qaff": np.stack([vec(inp["L1_attn_a_qqw"]), vec(inp["L1_attn_a_qqb"]),
                          vec(inp["L1_attn_a_qkw"]), vec(inp["L1_attn_a_qkb"])],
                         axis=1).astype(np.float32),
        "woT": np.ascontiguousarray(np.asarray(inp["L1_attn_a_Wo"], np.float32).T),
        "bo": vec(inp["L1_attn_a_bo"]).reshape(1, -1),
        "wrel": vec(inp["L1_attn_a_w_rel"]).reshape(1, -1),
        "costab": costab, "sintab": sintab, "maskt": maskt, "aind": aind,
        "identb": np.eye(128, dtype=BF),
    }
    insH = []
    for c in range(NCORES):
        d = dict(statH); d["x_i"] = X[c]; insH.append(d)
    outH = run(P["H"], insH)
    X = [np.asarray(o["xout"]) for o in outH]

    statI = {
        "gw": np.asarray(inp["L1_ffn_gw"], np.float32),
        "gb": vec(inp["L1_ffn_gb"]).reshape(1, -1),
        "w1": np.asarray(inp["L1_ffn_w1"], np.float32),
        "b1": np.asarray(inp["L1_ffn_b1"], np.float32),
        "w3": np.asarray(inp["L1_ffn_w3"], np.float32),
        "b3": np.asarray(inp["L1_ffn_b3"], np.float32),
        "w2": np.asarray(inp["L1_ffn_w2"], np.float32),
        "b2": np.asarray(inp["L1_ffn_b2"], np.float32),
        "sw1": np.asarray(inp["L1_ffn_sw1"], np.float32),
        "sb1": vec(inp["L1_ffn_sb1"]).reshape(1, -1),
        "sw3": np.asarray(inp["L1_ffn_sw3"], np.float32),
        "sb3": vec(inp["L1_ffn_sb3"]).reshape(1, -1),
        "sw2": np.asarray(inp["L1_ffn_sw2"], np.float32),
        "sb2": vec(inp["L1_ffn_sb2"]).reshape(1, -1),
        "identf": np.eye(128, dtype=np.float32),
    }
    insI = []
    for c in range(NCORES):
        d = dict(statI); d["x_i"] = X[c]; insI.append(d)
    outI = run(P["I"], insI)
    X = [np.asarray(o["xout"]) for o in outI]

    statJ = {
        "Wk": np.asarray(inp["L1_attn_b_Wk"], np.float32), "bk": vec(inp["L1_attn_b_bk"]).reshape(1, -1),
        "Wv": np.asarray(inp["L1_attn_b_Wv"], np.float32), "bv": vec(inp["L1_attn_b_bv"]).reshape(1, -1),
        "Wq": np.asarray(inp["L1_attn_b_Wq"], np.float32), "bq": vec(inp["L1_attn_b_bq"]).reshape(1, -1),
        "Wkp": np.asarray(inp["L1_attn_b_Wkp"], np.float32), "bkp": vec(inp["L1_attn_b_bkp"]).reshape(1, -1),
        "Wqp": np.asarray(inp["L1_attn_b_Wqp"], np.float32), "bqp": vec(inp["L1_attn_b_bqp"]).reshape(1, -1),
        "Wo": np.asarray(inp["L1_attn_b_Wo"], np.float32), "bo": vec(inp["L1_attn_b_bo"]).reshape(1, -1),
        "phs": np.ascontiguousarray(vec(inp["L1_attn_b_phase_scale"]).reshape(JD, 128).T),
        "lnwb": np.stack([vec(inp["L1_attn_b_ln_g"]).reshape(JD, 128).T,
                          vec(inp["L1_attn_b_ln_b"]).reshape(JD, 128).T]).astype(np.float32),
    }
    insJ = []
    for c in range(NCORES):
        d = dict(statJ); d["x_i"] = X[c]; insJ.append(d)
    outJ = run(P["J"], insJ)
    X = [np.asarray(o["xout"]) for o in outJ]

    insI2 = []
    for c in range(NCORES):
        d = dict(statI); d["x_i"] = X[c]; insI2.append(d)
    outI2 = run(P["I"], insI2)
    X = [np.asarray(o["xout"]) for o in outI2]

    out = np.empty((B, NCORES * TLOC, D), np.float32)
    for c in range(NCORES):
        out[:, c * TLOC:(c + 1) * TLOC, :] = _from_fm(X[c], TLOC)
    return out


# version 03
# speedup vs baseline: 1.1347x; 1.1347x over previous
import math
import numpy as np
import ml_dtypes
import concourse.bass as bass
import concourse.mybir as mybir
from concourse.bass_utils import run_bass_kernel_spmd

F32 = mybir.dt.float32
BF16 = mybir.dt.bfloat16
I32 = mybir.dt.int32
AF = mybir.ActivationFunctionType
ALU = mybir.AluOpType
AX = mybir.AxisListType
ts = bass.ts
BF = ml_dtypes.bfloat16

NCORES = 8
D = 1024
B = 4
JD = D // 128


class Sem:
    def __init__(self, nc, name):
        self.h = nc.semaphore(name).__enter__()
        self.val = 0

    def inc(self, inst, n=1):
        inst.then_inc(self.h, n)
        self.val += n
        return self.val

    def wait(self, eng, val=None):
        eng.wait_ge(self.h, self.val if val is None else val)


def sb(nc, name, shape, dtype):
    return nc.sbuf_tensor(name, shape, dtype).__enter__()


def ps(nc, name, shape, dtype=F32):
    return nc.psum_tensor(name, shape, dtype).__enter__()


def din(nc, name, shape, dtype=F32):
    return nc.dram_tensor(name, shape, dtype, kind="ExternalInput").ap()


def dout(nc, name, shape, dtype=F32):
    return nc.dram_tensor(name, shape, dtype, kind="ExternalOutput").ap()


def dint(nc, name, shape, dtype=F32):
    return nc.dram_tensor(name, shape, dtype, kind="Internal").ap()



def build_prog_A(TLOC, phases=4):
    TL = B * TLOC
    TLB = TL + 4
    NT = (TLB + 127) // 128
    NTC = TL // 512 if TL >= 512 else 1
    TCW = min(512, TL)

    nc = bass.Bass("TRN2", target_bir_lowering=False, debug=False)
    xa = din(nc, "xa", [TLB, D])
    mixes = din(nc, "mixes", [6, 128, JD])
    wts = din(nc, "wts", [6, D, D])
    identb = din(nc, "identb", [128, 128], BF16)
    identf = din(nc, "identf", [128, 128], F32)
    xfm_o = dout(nc, "xfm", [JD, 128, TL])
    rg_o = dout(nc, "rg", [2, JD, 128, TL], BF16)
    sp_o = dout(nc, "sp", [2, JD, 128, TL])
    kva_s = dint(nc, "kva_s", [3, JD, 128, TL], BF16)
    dec_s = dint(nc, "dec_s", [JD, 128, TL])

    xt = sb(nc, "xt", [128, D], F32)
    xnt = sb(nc, "xnt", [128, D], F32)
    xnb = sb(nc, "xnb", [128, D], BF16)
    lnm1 = sb(nc, "lnm1", [128, 1], F32)
    lnm2 = sb(nc, "lnm2", [128, 1], F32)
    lnsq = sb(nc, "lnsq", [128, D], F32)
    idb = sb(nc, "idb", [128, 128], BF16)
    idf = sb(nc, "idf", [128, 128], F32)
    mix_sb = sb(nc, "mix_sb", [128, 6 * JD], F32)
    xne = sb(nc, "xne", [128, JD * TLB], BF16)
    dbf = sb(nc, "dbf", [128, JD * TL], BF16)
    zbf = sb(nc, "zbf", [128, JD * TL], BF16)
    wtf = sb(nc, "wtf", [128, D], F32)
    wtb = sb(nc, "wtb", [128, 8 * D], BF16)
    acts = sb(nc, "acts", [128, TL], F32)
    actb = sb(nc, "actb", [128, TL], BF16)
    t_a = sb(nc, "t_a", [128, TCW], F32)
    t_r = sb(nc, "t_r", [128, TCW], F32)
    xev = sb(nc, "xev", [128, 128], F32)
    kj = sb(nc, "kj", [128, TL], BF16)
    vj = sb(nc, "vj", [128, TL], BF16)
    aj = sb(nc, "aj", [128, TL], BF16)
    dj = sb(nc, "dj", [128, TL], F32)
    uj = sb(nc, "uj", [128, TL], F32)
    slj = sb(nc, "slj", [128, TL], F32)
    pcj = sb(nc, "pcj", [128, TL], F32)
    ptr = ps(nc, "ptr", [128, 128], BF16)
    ptrf = ps(nc, "ptrf", [128, 128], F32)
    pp = ps(nc, "pp", [128, TCW], F32)
    if True:
        Ld = Sem(nc, "Ld"); St = Sem(nc, "St")
        T = Sem(nc, "T"); V = Sem(nc, "V"); S = Sem(nc, "S")
        v, s, t, g, sy = nc.vector, nc.scalar, nc.tensor, nc.gpsimd, nc.sync

        sy.dma_start(out=idb[:], in_=identb).then_inc(Ld.h, 16)
        sy.dma_start(out=idf[:], in_=identf).then_inc(Ld.h, 16)
        for m in range(6):
            sy.dma_start(out=mix_sb[:, m * JD:(m + 1) * JD], in_=mixes[m]).then_inc(Ld.h, 16)
        Ld.val += 16 * 8
        onesf = acts

        for i in range(NT):
            r0 = 128 * i
            rows = min(128, TLB - r0)
            if rows < 128:
                ei = V.inc(v.memset(xt[:], 0.0))
                V.wait(sy, ei)
            T.wait(sy)
            V.wait(sy)
            sy.dma_start(out=xt[0:rows, :],
                         in_=xa[r0:r0 + rows, :]).then_inc(Ld.h, 16)
            Ld.val += 16
            Ld.wait(v)
            v.tensor_reduce(lnm1[:], xt[:], AX.X, ALU.add)
            v.drain()
            v.tensor_scalar_mul(lnm1[:], lnm1[:], 1.0 / D)
            v.drain()
            v.tensor_scalar(xnt[:], xt[:], lnm1[:], None, ALU.subtract)
            v.drain()
            v.tensor_mul(lnsq[:], xnt[:], xnt[:])
            v.drain()
            v.tensor_reduce(lnm2[:], lnsq[:], AX.X, ALU.add)
            v.drain()
            ei = V.inc(v.tensor_scalar(lnm2[:], lnm2[:], 1.0 / D, 1e-5, ALU.mult, ALU.add))
            V.wait(s, ei)
            ei = S.inc(s.activation(lnm2[:], lnm2[:], AF.Sqrt))
            S.wait(v, ei)
            v.reciprocal(lnm2[:], lnm2[:])
            v.drain()
            v.tensor_scalar(xnt[:], xnt[:], lnm2[:], None, ALU.mult)
            v.drain()
            ei = V.inc(v.tensor_copy(xnb[:], xnt[:]))
            V.wait(t, ei)
            for j in range(JD):
                ei = T.inc(t.transpose(ptr[:], xnb[:, 128 * j:128 * (j + 1)], idb[:]))
                T.wait(v, ei)
                ei = V.inc(v.tensor_copy(xne[:, j * TLB + r0: j * TLB + r0 + rows],
                                         ptr[0:128, 0:rows]))
                V.wait(t, ei)
                c0 = max(r0, 4)
                n_real = r0 + rows - c0
                if n_real <= 0:
                    continue
                ei = T.inc(t.transpose(ptrf[:], xt[:, 128 * j:128 * (j + 1)], idf[:]))
                T.wait(v, ei)
                ei = V.inc(v.tensor_copy(xev[:, 0:rows], ptrf[:, 0:rows]))
                V.wait(g, ei)
                g.dma_start(out=xfm_o[j, :, c0 - 4: c0 - 4 + n_real],
                            in_=xev[:, c0 - r0: c0 - r0 + n_real]).then_inc(St.h, 16)
                St.val += 16
                V.wait(t, ei)
                St.wait(v)

        if phases < 2:
            St.wait(g)
            return nc
        St.wait(v)
        v.drain()
        for j in range(JD):
            v.tensor_tensor(dbf[:, j * TL:(j + 1) * TL],
                            xne[:, j * TLB: j * TLB + TL],
                            xne[:, j * TLB + 4: j * TLB + 4 + TL], ALU.subtract)
        ei = V.inc(v.drain())

        import os as _os
        WNAMES = ["r", "w", "k", "v", "a", "g"][:int(_os.environ.get("NMIX", "6"))]
        for m, wn in enumerate(WNAMES):
            T.wait(sy)
            for k in range(JD):
                if k > 0:
                    V.wait(sy)
                sy.dma_start(out=wtf[:],
                             in_=wts[m, 128 * k:128 * (k + 1), :]).then_inc(Ld.h, 16)
                Ld.val += 16
                Ld.wait(v)
                ei = V.inc(v.tensor_copy(wtb[:, k * D:(k + 1) * D], wtf[:]))
            T.wait(v)
            for j in range(JD):
                v.scalar_tensor_tensor(zbf[:, j * TL:(j + 1) * TL],
                                       dbf[:, j * TL:(j + 1) * TL],
                                       mix_sb[:, m * JD + j: m * JD + j + 1],
                                       xne[:, j * TLB + 4: j * TLB + 4 + TL],
                                       ALU.mult, ALU.add)
            ei = V.inc(v.drain())
            V.wait(t, ei)
            for m8 in range(JD):
                for tc in range(NTC):
                    c0 = tc * TCW
                    for k in range(JD):
                        mm = t.matmul(pp[:, 0:TCW],
                                      wtb[:, k * D + 128 * m8: k * D + 128 * (m8 + 1)],
                                      zbf[:, k * TL + c0: k * TL + c0 + TCW],
                                      start=(k == 0), stop=(k == JD - 1))
                    ei = T.inc(mm)
                    if wn == "r":
                        T.wait(s, ei)
                        ei = S.inc(s.activation(actb[:, c0:c0 + TCW], pp[:, 0:TCW], AF.Sigmoid))
                        S.wait(t, ei)
                        S.wait(g, ei)
                    elif wn == "k":
                        T.wait(s, ei)
                        ei = S.inc(s.activation(actb[:, c0:c0 + TCW], pp[:, 0:TCW], AF.Tanh))
                        S.wait(t, ei)
                        S.wait(g, ei)
                    elif wn in ("a", "g"):
                        T.wait(s, ei)
                        ei = S.inc(s.activation(actb[:, c0:c0 + TCW], pp[:, 0:TCW], AF.Sigmoid))
                        S.wait(t, ei)
                        S.wait(g, ei)
                    elif wn == "v":
                        T.wait(v, ei)
                        ei = V.inc(v.tensor_copy(actb[:, c0:c0 + TCW], pp[:, 0:TCW]))
                        V.wait(t, ei)
                        V.wait(g, ei)
                    elif wn == "w":
                        T.wait(s, ei)
                        ei_abs = S.inc(s.activation(t_a[:], pp[:, 0:TCW], AF.Abs))
                        s.drain()
                        s.activation(t_a[:], t_a[:], AF.Exp, scale=-1.0)
                        s.drain()
                        ei_ln = S.inc(s.activation(t_a[:], t_a[:], AF.Ln, bias=1.0))
                        S.wait(v, ei_abs)
                        v.tensor_relu(t_r[:], pp[:, 0:TCW])
                        v.drain()
                        S.wait(v, ei_ln)
                        ei_add = V.inc(v.tensor_add(t_a[:], t_a[:], t_r[:]))
                        V.wait(s, ei_add)
                        V.wait(t, ei_add)
                        ei = S.inc(s.activation(acts[:, c0:c0 + TCW], t_a[:], AF.Exp, scale=-1.0))
                        S.wait(t, ei)
                        S.wait(g, ei)
                if wn == "r":
                    g.dma_start(out=rg_o[0, m8], in_=actb[:, 0:TL]).then_inc(St.h, 16)
                elif wn == "g":
                    g.dma_start(out=rg_o[1, m8], in_=actb[:, 0:TL]).then_inc(St.h, 16)
                elif wn == "k":
                    g.dma_start(out=kva_s[0, m8], in_=actb[:, 0:TL]).then_inc(St.h, 16)
                elif wn == "v":
                    g.dma_start(out=kva_s[1, m8], in_=actb[:, 0:TL]).then_inc(St.h, 16)
                elif wn == "a":
                    g.dma_start(out=kva_s[2, m8], in_=actb[:, 0:TL]).then_inc(St.h, 16)
                elif wn == "w":
                    g.dma_start(out=dec_s[m8], in_=acts[:, 0:TL]).then_inc(St.h, 16)
                St.val += 16
                St.wait(s)
                St.wait(v)

        if phases < 4:
            St.wait(g)
            return nc
        St.wait(v)
        v.memset(onesf[:], 1.0)
        v.drain()
        for j in range(JD):
            St.wait(sy)
            V.wait(sy)
            sy.dma_start(out=kj[:], in_=kva_s[0, j]).then_inc(Ld.h, 16)
            sy.dma_start(out=vj[:], in_=kva_s[1, j]).then_inc(Ld.h, 16)
            sy.dma_start(out=aj[:], in_=kva_s[2, j]).then_inc(Ld.h, 16)
            sy.dma_start(out=dj[:], in_=dec_s[j]).then_inc(Ld.h, 16)
            Ld.val += 64
            Ld.wait(v)
            v.tensor_mul(uj[:], aj[:], kj[:])
            v.drain()
            v.tensor_mul(uj[:], uj[:], vj[:])
            v.drain()
            for b in range(B):
                v.tensor_tensor_scan(slj[:, b:TL:B], dj[:, b:TL:B], uj[:, b:TL:B],
                                     0.0, ALU.mult, ALU.add)
                v.tensor_tensor_scan(pcj[:, b:TL:B], dj[:, b:TL:B], onesf[:, b:TL:B],
                                     1.0, ALU.mult, ALU.mult)
            ei = V.inc(v.drain())
            V.wait(g, ei)
            g.dma_start(out=sp_o[0, j], in_=slj[:]).then_inc(St.h, 16)
            g.dma_start(out=sp_o[1, j], in_=pcj[:]).then_inc(St.h, 16)
            St.val += 32
            St.wait(v)
        St.wait(g)
    return nc



def host_prep_A(x_full, mix_list, w_list, core, TLOC):
    TG_ = x_full.shape[1]
    t0 = TLOC * core
    xs = x_full[:, t0:t0 + TLOC, :]
    xs = np.ascontiguousarray(xs.transpose(1, 0, 2).reshape(B * TLOC, D))
    if t0 == 0:
        bnd = np.zeros((B, D), np.float32)
    else:
        bnd = np.ascontiguousarray(x_full[:, t0 - 1, :])
    xa = np.concatenate([bnd, xs], axis=0)
    mixes = np.stack([m.reshape(JD, 128).T for m in mix_list])
    wts = np.stack([np.ascontiguousarray(w.T) for w in w_list])
    return {
        "xa": xa.astype(np.float32),
        "mixes": mixes.astype(np.float32),
        "wts": wts.astype(np.float32),
        "identb": np.eye(128, dtype=BF),
        "identf": np.eye(128, dtype=np.float32),
    }


def kernel(**inputs):
    raise NotImplementedError("assembling")



def build_prog_B(TLOC):
    TL = B * TLOC
    NTC = max(1, TL // 512)
    TCW = min(512, TL)
    GN_EPS = 0.00064

    nc = bass.Bass("TRN2", target_bir_lowering=False, debug=False)
    xfm_i = din(nc, "xfm_i", [JD, 128, TL])
    rg_i = din(nc, "rg_i", [2, JD, 128, TL], BF16)
    sp_i = din(nc, "sp_i", [2, JD, 128, TL])
    pref = din(nc, "pref", [7, 2, 128, JD * B])
    gnwb = din(nc, "gnwb", [2, 128, JD])
    woutT = din(nc, "woutT", [D, D])
    bout = din(nc, "bout", [1, D])
    gsel = din(nc, "gsel", [128, 2], BF16)
    sel16 = din(nc, "sel16", [JD, 16, 128], BF16)
    onesr = din(nc, "onesr", [1, TL])
    xout_o = dout(nc, "xout", [JD, 128, TL])

    prf = sb(nc, "prf", [128, 7 * 2 * JD * B], F32)
    sin_sb = sb(nc, "sin_sb", [128, JD * B], F32)
    tmp_sb = sb(nc, "tmp_sb", [128, JD * B], F32)
    gnw_sb = sb(nc, "gnw_sb", [128, 2 * JD], F32)
    gsel_sb = sb(nc, "gsel_sb", [128, 2], BF16)
    sel16_sb = sb(nc, "sel16_sb", [16, JD * 128], BF16)
    ones_bf = sb(nc, "ones_bf", [1, TL], BF16)
    onesf_sb = sb(nc, "onesf_sb", [1, TL], F32)
    bout_bf = sb(nc, "bout_bf", [1, D], BF16)
    bout_f = sb(nc, "bout_f", [1, D], F32)
    slj = sb(nc, "slj", [128, TL], F32)
    pcj = sb(nc, "pcj", [128, TL], F32)
    rj = sb(nc, "rj", [128, TL], BF16)
    gj = sb(nc, "gj", [128, TL], BF16)
    mxb = sb(nc, "mxb", [128, JD * TL], BF16)
    sqb = sb(nc, "sqb", [128, TCW], BF16)
    gst = sb(nc, "gst", [16, 2 * TL], F32)
    pg_sb = sb(nc, "pg_sb", [2, 2 * 512], F32)
    g16 = sb(nc, "g16", [16, 2 * TL], BF16)
    wtf = sb(nc, "wtf", [128, D], F32)
    wtb = sb(nc, "wtb", [128, 8 * D], BF16)
    xfm = sb(nc, "xfm", [128, TL], F32)
    t_x = sb(nc, "t_x", [128, TCW], F32)
    t_y = sb(nc, "t_y", [128, TCW], F32)
    pg = ps(nc, "pg", [2, TCW], F32)
    pg2 = ps(nc, "pg2", [2, TCW], F32)
    pbc = ps(nc, "pbc", [128, TCW], F32)
    pbc2 = ps(nc, "pbc2", [128, TCW], F32)
    pp = ps(nc, "pp", [128, TCW], F32)

    Ld = Sem(nc, "Ld"); Ld2 = Sem(nc, "Ld2"); St = Sem(nc, "St"); Sq = Sem(nc, "Sq")
    T = Sem(nc, "T"); V = Sem(nc, "V"); S = Sem(nc, "S")
    v, s, t, g, sy = nc.vector, nc.scalar, nc.tensor, nc.gpsimd, nc.sync

    for k in range(7):
        for q in range(2):
            sy.dma_start(out=prf[:, (k * 2 + q) * JD * B:(k * 2 + q + 1) * JD * B],
                         in_=pref[k, q]).then_inc(Ld.h, 16)
    sy.dma_start(out=gnw_sb[:, 0:JD], in_=gnwb[0]).then_inc(Ld.h, 16)
    sy.dma_start(out=gnw_sb[:, JD:2 * JD], in_=gnwb[1]).then_inc(Ld.h, 16)
    sy.dma_start(out=gsel_sb[:], in_=gsel).then_inc(Ld.h, 16)
    for j in range(JD):
        sy.dma_start(out=sel16_sb[0:16, j * 128:(j + 1) * 128],
                     in_=sel16[j]).then_inc(Ld.h, 16)
    sy.dma_start(out=onesf_sb[:], in_=onesr).then_inc(Ld.h, 16)
    sy.dma_start(out=bout_f[:], in_=bout).then_inc(Ld.h, 16)
    Ld.val += 16 * (14 + 2 + 1 + JD + 1 + 1)
    Ld.wait(v)
    v.tensor_copy(ones_bf[:], onesf_sb[:])
    v.drain()
    v.tensor_copy(bout_bf[:], bout_f[:])
    v.drain()
    v.memset(sin_sb[:], 0.0)
    v.drain()
    for k in range(7):
        v.tensor_mul(tmp_sb[:], sin_sb[:], prf[:, (k * 2) * JD * B:(k * 2 + 1) * JD * B])
        v.drain()
        v.tensor_add(sin_sb[:], tmp_sb[:], prf[:, (k * 2 + 1) * JD * B:(k * 2 + 2) * JD * B])
        v.drain()

    for j in range(JD):
        St.wait(sy); V.wait(sy); S.wait(sy)
        Ld2.wait(sy)
        sy.dma_start(out=slj[:], in_=sp_i[0, j]).then_inc(Ld2.h, 16)
        sy.dma_start(out=pcj[:], in_=sp_i[1, j]).then_inc(Ld2.h, 16)
        sy.dma_start(out=rj[:], in_=rg_i[0, j]).then_inc(Ld2.h, 16)
        Ld2.val += 48
        Ld2.wait(v)
        for b in range(B):
            v.scalar_tensor_tensor(slj[:, b:TL:B], pcj[:, b:TL:B],
                                   sin_sb[:, B * j + b: B * j + b + 1],
                                   slj[:, b:TL:B], ALU.mult, ALU.add)
        v.drain()
        ei = V.inc(v.tensor_mul(mxb[:, j * TL:(j + 1) * TL], rj[:], slj[:]))
        v.drain()
        for tc in range(NTC):
            c0 = tc * TCW
            V.wait(s, ei)
            eis = S.inc(s.activation(sqb[:], mxb[:, j * TL + c0: j * TL + c0 + TCW],
                                     AF.Square))
            V.wait(t, ei)
            S.wait(t, eis)
            mm = t.matmul(pg[0:2, 0:TCW], gsel_sb[:], mxb[:, j * TL + c0: j * TL + c0 + TCW],
                          start=True, stop=True)
            mm2 = t.matmul(pg2[0:2, 0:TCW], gsel_sb[:], sqb[:], start=True, stop=True)
            eit = T.inc(mm2)
            T.wait(v, eit)
            v.tensor_copy(pg_sb[0:2, 0:TCW], pg[0:2, 0:TCW])
            v.drain()
            ei2 = V.inc(v.tensor_copy(pg_sb[0:2, TCW:2 * TCW], pg2[0:2, 0:TCW]))
            V.wait(t, ei2)
            V.wait(g, ei2)
            g.dma_start(out=gst[2 * j:2 * j + 2, c0:c0 + TCW], in_=pg_sb[0:2, 0:TCW]).then_inc(Sq.h, 16)
            g.dma_start(out=gst[2 * j:2 * j + 2, TL + c0:TL + c0 + TCW], in_=pg_sb[0:2, TCW:2 * TCW]).then_inc(Sq.h, 16)
            Sq.val += 32
            Sq.wait(v)
            Sq.wait(s)

    Sq.wait(v)
    NG = 64.0
    v.tensor_scalar_mul(gst[0:16, 0:TL], gst[0:16, 0:TL], 1.0 / NG)
    v.drain()
    v.tensor_scalar_mul(gst[0:16, TL:2 * TL], gst[0:16, TL:2 * TL], 1.0 / NG)
    v.drain()
    v.tensor_mul(g16[0:16, 0:TL], gst[0:16, 0:TL], gst[0:16, 0:TL])
    v.drain()
    v.tensor_sub(gst[0:16, TL:2 * TL], gst[0:16, TL:2 * TL], g16[0:16, 0:TL])
    v.drain()
    v.tensor_scalar_add(gst[0:16, TL:2 * TL], gst[0:16, TL:2 * TL], GN_EPS)
    ei = V.inc(v.drain())
    V.wait(s, ei)
    ei = S.inc(s.activation(gst[0:16, TL:2 * TL], gst[0:16, TL:2 * TL], AF.Sqrt))
    S.wait(v, ei)
    v.reciprocal(gst[0:16, TL:2 * TL], gst[0:16, TL:2 * TL])
    v.drain()
    v.tensor_copy(g16[0:16, :], gst[0:16, :])
    ei = V.inc(v.drain())
    V.wait(t, ei)

    for j in range(JD):
        V.wait(sy)
        Ld2.wait(sy)
        sy.dma_start(out=gj[:], in_=rg_i[1, j]).then_inc(Ld2.h, 16)
        Ld2.val += 16
        for tc in range(NTC):
            c0 = tc * TCW
            sl = slice(j * TL + c0, j * TL + c0 + TCW)
            mm = t.matmul(pbc[:, 0:TCW], sel16_sb[0:16, j * 128:(j + 1) * 128],
                          g16[0:16, c0:c0 + TCW], start=True, stop=True)
            mm2 = t.matmul(pbc2[:, 0:TCW], sel16_sb[0:16, j * 128:(j + 1) * 128],
                           g16[0:16, TL + c0:TL + c0 + TCW], start=True, stop=True)
            eit = T.inc(mm2)
            T.wait(v, eit)
            Ld2.wait(v)
            v.tensor_sub(t_x[:], mxb[:, sl], pbc[:, 0:TCW])
            v.drain()
            v.tensor_mul(t_x[:], t_x[:], pbc2[:, 0:TCW])
            v.drain()
            v.tensor_scalar(t_x[:], t_x[:], gnw_sb[:, j:j + 1],
                            gnw_sb[:, JD + j:JD + j + 1], ALU.mult, ALU.add)
            v.drain()
            ei = V.inc(v.tensor_mul(mxb[:, sl], t_x[:], gj[:, c0:c0 + TCW]))
            V.wait(t, ei)

    for k in range(JD):
        if k > 0:
            V.wait(sy)
        T.wait(sy)
        Ld2.wait(sy)
        sy.dma_start(out=wtf[:], in_=woutT[128 * k:128 * (k + 1), :]).then_inc(Ld2.h, 16)
        Ld2.val += 16
        Ld2.wait(v)
        V.inc(v.tensor_copy(wtb[:, k * D:(k + 1) * D], wtf[:]))
    ei = V.inc(v.drain())
    V.wait(t, ei)
    for m8 in range(JD):
        V.wait(sy)
        Ld2.wait(sy)
        sy.dma_start(out=xfm[:], in_=xfm_i[m8]).then_inc(Ld2.h, 16)
        Ld2.val += 16
        for tc in range(NTC):
            c0 = tc * TCW
            for k in range(JD):
                mm = t.matmul(pp[:, 0:TCW],
                              wtb[:, k * D + 128 * m8: k * D + 128 * (m8 + 1)],
                              mxb[:, k * TL + c0: k * TL + c0 + TCW],
                              start=(k == 0), stop=False)
            mm = t.matmul(pp[:, 0:TCW], bout_bf[:, 128 * m8:128 * (m8 + 1)],
                          ones_bf[:, c0:c0 + TCW], start=False, stop=True)
            eit = T.inc(mm)
            T.wait(v, eit)
            Ld2.wait(v)
            ei = V.inc(v.tensor_add(t_y[:], pp[:, 0:TCW],
                                    xfm[:, c0:c0 + TCW]))
            V.wait(t, ei)
            V.wait(g, ei)
            g.wait_ge(St.h, St.val)
            g.dma_start(out=xout_o[m8, :, c0:c0 + TCW], in_=t_y[:]).then_inc(St.h, 16)
            St.val += 16
            St.wait(v)
    St.wait(g)
    return nc


def host_prep_B(outA_list, gn_w, gn_b, W_out, b_out, core, TLOC):
    TL = B * TLOC
    myA = outA_list[core]
    pref = np.zeros((7, 2, 128, JD * B), np.float32)
    pref[:, 0, :, :] = 1.0
    for k in range(core):
        sp = outA_list[k]["sp"]
        pf = sp[1][:, :, TL - B:TL]
        sf = sp[0][:, :, TL - B:TL]
        pref[k, 0] = pf.transpose(1, 0, 2).reshape(128, JD * B)
        pref[k, 1] = sf.transpose(1, 0, 2).reshape(128, JD * B)
    gnwb = np.stack([gn_w.reshape(JD, 128).T, gn_b.reshape(JD, 128).T])
    gsel = np.zeros((128, 2), BF)
    gsel[0:64, 0] = 1; gsel[64:128, 1] = 1
    sel16 = np.zeros((JD, 16, 128), BF)
    for j in range(JD):
        sel16[j, 2 * j, 0:64] = 1
        sel16[j, 2 * j + 1, 64:128] = 1
    return {
        "xfm_i": myA["xfm"], "rg_i": myA["rg"], "sp_i": myA["sp"],
        "pref": pref,
        "gnwb": gnwb.astype(np.float32),
        "woutT": np.ascontiguousarray(W_out.T).astype(np.float32),
        "bout": b_out.reshape(1, D).astype(np.float32),
        "gsel": gsel, "sel16": sel16,
        "onesr": np.ones((1, TL), np.float32),
    }



def build_prog_C(TLOC):
    TL = B * TLOC
    NTC = max(1, TL // 512)
    TCW = min(512, TL)
    E = 8
    H = 4 * D
    HC = H // 128
    HQ = HC // 4

    nc = bass.Bass("TRN2", target_bir_lowering=False, debug=False)
    x_i = din(nc, "x_i", [JD, 128, TL])
    gate_w = din(nc, "gate_w", [D, E])
    gate_b = din(nc, "gate_b", [1, E])
    w1_i = din(nc, "w1", [E, D, H])
    b1_i = din(nc, "b1", [E, 1, H])
    w2_i = din(nc, "w2", [E, H, D])
    b2_i = din(nc, "b2", [E, 1, D])
    identf = din(nc, "identf", [128, 128], F32)
    xout_o = dout(nc, "xout", [JD, 128, TL])

    uni = sb(nc, "uni", [128, JD * TCW], F32)
    xjs = uni
    xnb = sb(nc, "xnb", [128, JD * TL], BF16)
    xb_c = sb(nc, "xb_c", [128, TCW], BF16)
    sq_c = sb(nc, "sq_c", [128, TCW], BF16)
    onescol = sb(nc, "onescol", [128, 1], BF16)
    ones_row = sb(nc, "ones_row", [1, TL], BF16)
    st_f = sb(nc, "st_f", [1, 4 * TCW], F32)
    st_b = sb(nc, "st_b", [1, 2 * TCW], BF16)
    gw_sb = sb(nc, "gw_sb", [128, JD * E], BF16)
    gb_sb = sb(nc, "gb_sb", [1, E], BF16)
    gb_f = sb(nc, "gb_f", [1, E], F32)
    idf = sb(nc, "idf", [128, 128], F32)
    lg = sb(nc, "lg", [8, TCW], F32)
    lgt = sb(nc, "lgt", [128, E], F32)
    gm1 = sb(nc, "gm1", [128, 1], F32)
    gm8 = sb(nc, "gm8", [128, E], F32)
    wmap = sb(nc, "wmap", [128, E], F32)
    wfm = sb(nc, "wfm", [8, NTC * TCW], F32)
    wfmb = sb(nc, "wfmb", [8, NTC * TCW], BF16)
    wrow = sb(nc, "wrow", [1, TCW], F32)
    wrow_b = sb(nc, "wrow_b", [1, TCW], BF16)
    wbc = sb(nc, "wbc", [128, TCW], BF16)
    w1b = sb(nc, "w1b", [128, JD * 1024], BF16)
    w2b = sb(nc, "w2b", [128, HQ * D], BF16)
    wtf = sb(nc, "wtf", [128, 1024], F32)
    b1b = sb(nc, "b1b", [1, H], BF16)
    b2b = sb(nc, "b2b", [8, D], BF16)
    b2f = sb(nc, "b2f", [8, D], F32)
    hbuf = uni.bitcast(BF16)
    htmp = sb(nc, "htmp", [128, TCW], BF16)
    htmp2 = sb(nc, "htmp2", [128, TCW], BF16)
    oacc = sb(nc, "oacc", [128, JD * TL], F32)
    t_y = sb(nc, "t_y", [128, TCW], F32)
    pA = ps(nc, "pA", [128, TCW], F32)
    pB = ps(nc, "pB", [128, TCW], F32)
    pC = ps(nc, "pC", [128, TCW], F32)
    pD = ps(nc, "pD", [128, TCW], F32)
    pb1 = pA; pb2 = pB; pbc = pC; pbc2 = pD
    pl = pA; ptt = pB; pth = pC
    ph = pB; po = pC

    Ld = Sem(nc, "Ld"); Lw = Sem(nc, "Lw"); St = Sem(nc, "St")
    T = Sem(nc, "T"); V = Sem(nc, "V"); S = Sem(nc, "S")
    v, s, t, g, sy = nc.vector, nc.scalar, nc.tensor, nc.gpsimd, nc.sync

    sy.dma_start(out=idf[:], in_=identf).then_inc(Ld.h, 16)
    Ld.val += 16
    for j in range(JD):
        sy.dma_start(out=wtf[:, 0:E], in_=gate_w[128 * j:128 * (j + 1), :]).then_inc(Ld.h, 16)
        Ld.val += 16
        Ld.wait(v)
        V.inc(v.tensor_copy(gw_sb[:, j * E:(j + 1) * E], wtf[:, 0:E]))
        V.wait(sy)
    sy.dma_start(out=gb_f[:], in_=gate_b).then_inc(Ld.h, 16)
    Ld.val += 16
    Ld.wait(v)
    v.memset(onescol[:], 1.0)
    v.memset(ones_row[:], 1.0)
    v.tensor_copy(gb_sb[:], gb_f[:])
    v.memset(oacc[:], 0.0)
    ei = V.inc(v.drain())
    V.wait(t, ei)

    for tc in range(NTC):
        c0 = tc * TCW
        V.wait(sy); S.wait(sy); T.wait(sy)
        Ld.wait(sy)
        for j in range(JD):
            sy.dma_start(out=xjs[:, j * TCW:(j + 1) * TCW],
                         in_=x_i[j][:, c0:c0 + TCW]).then_inc(Ld.h, 16)
        Ld.val += 16 * JD
        Ld.wait(v); Ld.wait(s)
        for j in range(JD):
            xsl = xjs[:, j * TCW:(j + 1) * TCW]
            ei = V.inc(v.tensor_copy(xb_c[:], xsl))
            eis = S.inc(s.activation(sq_c[:], xsl, AF.Square))
            V.wait(t, ei); S.wait(t, eis)
            mm1 = t.matmul(pb1[0:1, 0:TCW], onescol[:], xb_c[:], start=(j == 0), stop=(j == JD - 1))
            mm2 = t.matmul(pb2[0:1, 0:TCW], onescol[:], sq_c[:], start=(j == 0), stop=(j == JD - 1))
            eit = T.inc(mm2)
            T.wait(v, eit)
            T.wait(s, eit)
        v.tensor_scalar_mul(st_f[:, 0:TCW], pb1[0:1, 0:TCW], 1.0 / D)
        v.drain()
        v.tensor_scalar_mul(st_f[:, TCW:2 * TCW], pb2[0:1, 0:TCW], 1.0 / D)
        v.drain()
        v.tensor_mul(st_f[:, 3 * TCW:4 * TCW], st_f[:, 0:TCW], st_f[:, 0:TCW])
        v.drain()
        v.tensor_sub(st_f[:, TCW:2 * TCW], st_f[:, TCW:2 * TCW], st_f[:, 3 * TCW:4 * TCW])
        v.drain()
        v.tensor_scalar_add(st_f[:, TCW:2 * TCW], st_f[:, TCW:2 * TCW], 1e-5)
        ei = V.inc(v.drain())
        V.wait(s, ei)
        ei = S.inc(s.activation(st_f[:, 2 * TCW:3 * TCW], st_f[:, TCW:2 * TCW], AF.Sqrt))
        S.wait(v, ei)
        v.reciprocal(st_f[:, 2 * TCW:3 * TCW], st_f[:, 2 * TCW:3 * TCW])
        v.drain()
        v.tensor_copy(st_b[:, 0:TCW], st_f[:, 0:TCW])
        v.drain()
        ei = V.inc(v.tensor_copy(st_b[:, TCW:2 * TCW], st_f[:, 2 * TCW:3 * TCW]))
        V.wait(t, ei)
        mm1 = t.matmul(pbc[:, 0:TCW], onescol[0:1, 0:1].bitcast(BF16) if False else ones_row[:, 0:128],
                       st_b[:, 0:TCW], start=True, stop=True) if False else None
        mm1 = t.matmul(pbc[:, 0:TCW], ones_row[:, 0:128], st_b[:, 0:TCW], start=True, stop=True)
        mm2 = t.matmul(pbc2[:, 0:TCW], ones_row[:, 0:128], st_b[:, TCW:2 * TCW], start=True, stop=True)
        eit = T.inc(mm2)
        T.wait(v, eit)
        for j in range(JD):
            v.tensor_sub(t_y[:], xjs[:, j * TCW:(j + 1) * TCW], pbc[:, 0:TCW])
            v.drain()
            v.tensor_mul(xnb[:, j * TL + c0: j * TL + c0 + TCW], t_y[:], pbc2[:, 0:TCW])
            v.drain()
        ei = V.inc(v.drain())
        V.wait(t, ei)
        for j in range(JD):
            mm = t.matmul(pl[0:8, 0:TCW], gw_sb[:, j * E:(j + 1) * E],
                          xnb[:, j * TL + c0: j * TL + c0 + TCW],
                          start=(j == 0), stop=False)
        mm = t.matmul(pl[0:8, 0:TCW], gb_sb[:], ones_row[:, 0:TCW], start=False, stop=True)
        eit = T.inc(mm)
        T.wait(v, eit)
        ei = V.inc(v.tensor_copy(lg[0:8, 0:TCW], pl[0:8, 0:TCW]))
        V.wait(t, ei)
        for sbt in range(TCW // 128):
            mm = t.matmul(ptt[:, 0:E], lg[0:8, 128 * sbt:128 * (sbt + 1)], idf[0:8, 0:E],
                          start=True, stop=True)
            eit = T.inc(mm)
            T.wait(v, eit)
            v.tensor_copy(lgt[:], ptt[:, 0:E])
            v.drain()
            v.reduce_max(gm1[:], lgt[:], AX.X)
            v.drain()
            ei = V.inc(v.tensor_scalar(lgt[:], lgt[:], gm1[:], None, ALU.subtract))
            V.wait(s, ei)
            ei = S.inc(s.activation(lgt[:], lgt[:], AF.Exp))
            S.wait(v, ei)
            v.tensor_reduce(gm1[:], lgt[:], AX.X, ALU.add)
            v.drain()
            v.reciprocal(gm1[:], gm1[:])
            v.drain()
            v.tensor_scalar(lgt[:], lgt[:], gm1[:], None, ALU.mult)
            v.drain()
            v.max(out=gm8[:], in_=lgt[:])
            v.drain()
            v.tensor_scalar(wmap[:], lgt[:], gm8[:, 1:2], None, ALU.is_ge)
            v.drain()
            v.tensor_mul(wmap[:], wmap[:], lgt[:])
            v.drain()
            v.tensor_reduce(gm1[:], wmap[:], AX.X, ALU.add)
            v.drain()
            v.reciprocal(gm1[:], gm1[:])
            v.drain()
            ei = V.inc(v.tensor_scalar(wmap[:], wmap[:], gm1[:], None, ALU.mult))
            V.wait(t, ei)
            mm = t.transpose(pth[0:128, 0:8][0:128, 0:8] if False else pth[:, :],
                             wmap[:], idf[:]) if False else None
            mm = t.transpose(pth[:, :], wmap[:, 0:E].broadcast_to([128, E]) if False else wmap[:],
                             idf[:]) if False else None
            mm = t.transpose(pth[0:8, 0:128], wmap[:], idf[:])
            eit = T.inc(mm)
            T.wait(v, eit)
            ei = V.inc(v.tensor_copy(wfm[0:8, c0 + 128 * sbt: c0 + 128 * (sbt + 1)],
                                     pth[0:8, 0:128]))
            V.wait(t, ei)
        v.drain()
    St.wait(v)
    ei = V.inc(v.tensor_copy(wfmb[0:8, :], wfm[0:8, :]))
    V.wait(t, ei); V.wait(g, ei)

    for e in range(E):
        sy.dma_start(out=b2f[e:e + 1, :] if e == 0 else b2f[e:e + 1, :],
                     in_=b2_i[e]).then_inc(Lw.h, 16)
        Lw.val += 16
    Lw.wait(v)
    v.drain()
    ei = V.inc(v.tensor_copy(b2b[0:8, :], b2f[0:8, :]))
    V.wait(t, ei)

    for e in range(E):
        for half in range(4):
            h0 = half * 1024
            T.wait(sy)
            for k in range(JD):
                V.wait(sy)
                Lw.wait(sy)
                sy.dma_start(out=wtf[:], in_=w1_i[e, 128 * k:128 * (k + 1), h0:h0 + 1024]).then_inc(Lw.h, 16)
                Lw.val += 16
                Lw.wait(v)
                V.inc(v.tensor_copy(w1b[:, k * 1024:(k + 1) * 1024], wtf[:]))
            for hk in range(HQ):
                V.wait(sy)
                Lw.wait(sy)
                sy.dma_start(out=wtf[:, 0:D],
                             in_=w2_i[e, 128 * (HQ * half + hk):128 * (HQ * half + hk + 1), :]).then_inc(Lw.h, 16)
                Lw.val += 16
                Lw.wait(v)
                V.inc(v.tensor_copy(w2b[:, hk * D:(hk + 1) * D], wtf[:, 0:D]))
            if half == 0:
                for bh in range(4):
                    V.wait(sy)
                    Lw.wait(sy)
                    sy.dma_start(out=wtf[0:1, 0:1024],
                                 in_=b1_i[e][:, 1024 * bh:1024 * (bh + 1)]).then_inc(Lw.h, 16)
                    Lw.val += 16
                    Lw.wait(v)
                    V.inc(v.tensor_copy(b1b[:, 1024 * bh:1024 * (bh + 1)], wtf[0:1, 0:1024]))
            ei = V.inc(v.drain())
            V.wait(t, ei)
            for tc in range(NTC):
                c0 = tc * TCW
                V.wait(sy); Lw.wait(sy)
                sy.dma_start(out=wrow[:], in_=wfm[e:e + 1, c0:c0 + TCW]).then_inc(Lw.h, 16)
                Lw.val += 16
                Lw.wait(v)
                v.tensor_copy(wrow_b[:], wrow[:])
                ei = V.inc(v.drain())
                V.wait(t, ei)
                mm = t.matmul(pbc[:, 0:TCW], ones_row[:, 0:128], wrow_b[:], start=True, stop=True)
                eit = T.inc(mm)
                T.wait(v, eit)
                ei = V.inc(v.tensor_copy(wbc[:], pbc[:, 0:TCW]))
                V.wait(t, ei)
                for hk in range(HQ):
                    hcol = 128 * hk
                    php = ph if hk % 2 == 0 else pD
                    htp = htmp if hk % 2 == 0 else htmp2
                    if hk >= 2:
                        S.wait(t, S.val - 1)
                    for k in range(JD):
                        mm = t.matmul(php[:, 0:TCW],
                                      w1b[:, k * 1024 + hcol: k * 1024 + hcol + 128],
                                      xnb[:, k * TL + c0: k * TL + c0 + TCW],
                                      start=(k == 0), stop=False)
                    mm = t.matmul(php[:, 0:TCW], b1b[:, h0 + hcol: h0 + hcol + 128],
                                  ones_row[:, 0:TCW], start=False, stop=True)
                    eit = T.inc(mm)
                    T.wait(s, eit)
                    ei = S.inc(s.activation(htp[:], php[:, 0:TCW], AF.Gelu))
                    S.wait(v, ei)
                    v.tensor_mul(hbuf[:, hk * TCW:(hk + 1) * TCW], htp[:], wbc[:])
                    eiv = V.inc(v.drain())
                    V.wait(s, eiv)
                ei = V.inc(v.drain())
                V.wait(t, ei)
                for m8 in range(JD):
                    pop = po if m8 % 2 == 0 else ph
                    for hk in range(HQ):
                        mm = t.matmul(pop[:, 0:TCW],
                                      w2b[:, hk * D + 128 * m8: hk * D + 128 * (m8 + 1)],
                                      hbuf[:, hk * TCW:(hk + 1) * TCW],
                                      start=(hk == 0), stop=(hk == HQ - 1))
                    eit = T.inc(mm)
                    T.wait(v, eit)
                    osl = oacc[:, m8 * TL + c0: m8 * TL + c0 + TCW]
                    v.tensor_add(osl, osl, pop[:, 0:TCW])
                    ei = V.inc(v.drain())
                    if m8 >= 1:
                        V.wait(t, ei - 1)
                if True:
                    V.wait(t, V.val)

    for m8 in range(JD):
        for tc in range(NTC):
            c0 = tc * TCW
            mm = t.matmul(po[:, 0:TCW], b2b[0:8, 128 * m8:128 * (m8 + 1)],
                          wfmb[0:8, c0:c0 + TCW], start=True, stop=True)
            eit = T.inc(mm)
            T.wait(v, eit)
            osl = oacc[:, m8 * TL + c0: m8 * TL + c0 + TCW]
            v.tensor_add(osl, osl, po[:, 0:TCW])
            ei = V.inc(v.drain())
            V.wait(t, ei)

    for m8 in range(JD):
        V.wait(sy)
        Ld.wait(sy)
        sy.dma_start(out=xjs[:, 0:TL] if TL <= JD * TCW else xjs[:, 0:TL],
                     in_=x_i[m8]).then_inc(Ld.h, 16)
        Ld.val += 16
        Ld.wait(v)
        St.wait(v)
        v.tensor_add(oacc[:, m8 * TL:(m8 + 1) * TL], oacc[:, m8 * TL:(m8 + 1) * TL],
                     xjs[:, 0:TL])
        ei = V.inc(v.drain())
        V.wait(g, ei)
        g.wait_ge(St.h, St.val)
        g.dma_start(out=xout_o[m8], in_=oacc[:, m8 * TL:(m8 + 1) * TL]).then_inc(St.h, 16)
        St.val += 16
    St.wait(g)
    return nc


def host_prep_C(x_fm_list_unused, gate_w, gate_b, w1, b1, w2, b2, TLOC):
    return {
        "gate_w": np.ascontiguousarray(gate_w).astype(np.float32),
        "gate_b": gate_b.reshape(1, -1).astype(np.float32),
        "w1": np.ascontiguousarray(w1).astype(np.float32),
        "b1": np.ascontiguousarray(b1.reshape(b1.shape[0], 1, -1)).astype(np.float32),
        "w2": np.ascontiguousarray(w2).astype(np.float32),
        "b2": np.ascontiguousarray(b2.reshape(b2.shape[0], 1, -1)).astype(np.float32),
        "identf": np.eye(128, dtype=np.float32),
    }



def build_prog_D(TLOC):
    TL = B * TLOC
    NTC = max(1, TL // 512)
    TCW = min(512, TL)
    EH = 2 * D

    nc = bass.Bass("TRN2", target_bir_lowering=False, debug=False)
    x_i = din(nc, "x_i", [JD, 128, TL])
    winT = din(nc, "winT", [D, EH])
    b_in = din(nc, "b_in", [1, EH])
    b_h = din(nc, "b_h", [1, D])
    lnwb = din(nc, "lnwb", [2, 128, JD])
    g_o = dout(nc, "g_o", [JD, 128, TL], BF16)
    vb_o = dout(nc, "vb_o", [JD, 128, TL], BF16)
    xn2_o = dout(nc, "xn2_o", [JD, 128, TL])

    xjs = sb(nc, "xjs", [128, JD * TCW], F32)
    yjs = sb(nc, "yjs", [128, JD * TCW], F32)
    xnb = sb(nc, "xnb", [128, JD * TL], BF16)
    xb_c = sb(nc, "xb_c", [128, TCW], BF16)
    sq_c = sb(nc, "sq_c", [128, TCW], BF16)
    onescol = sb(nc, "onescol", [128, 1], BF16)
    ones_row = sb(nc, "ones_row", [1, TL], BF16)
    st_f = sb(nc, "st_f", [1, 4 * TCW], F32)
    st_b = sb(nc, "st_b", [1, 2 * TCW], BF16)
    lnw_sb = sb(nc, "lnw_sb", [128, 2 * JD], F32)
    wtf = sb(nc, "wtf", [128, 1024], F32)
    wtb = sb(nc, "wtb", [128, JD * EH], BF16)
    brow = sb(nc, "brow", [1, EH + D], BF16)
    t_y = sb(nc, "t_y", [128, TCW], F32)
    t_z = sb(nc, "t_z", [128, TCW], F32)
    pA = ps(nc, "pA", [128, TCW], F32)
    pB = ps(nc, "pB", [128, TCW], F32)
    pC = ps(nc, "pC", [128, TCW], F32)
    pD = ps(nc, "pD", [128, TCW], F32)

    Ld = Sem(nc, "Ld"); Lw = Sem(nc, "Lw"); St = Sem(nc, "St")
    T = Sem(nc, "T"); V = Sem(nc, "V"); S = Sem(nc, "S")
    v, s, t, g, sy = nc.vector, nc.scalar, nc.tensor, nc.gpsimd, nc.sync

    sy.dma_start(out=lnw_sb[:, 0:JD], in_=lnwb[0]).then_inc(Ld.h, 16)
    Ld.val += 16
    sy.dma_start(out=lnw_sb[:, JD:2 * JD], in_=lnwb[1]).then_inc(Ld.h, 16)
    Ld.val += 16
    sy.dma_start(out=wtf[0:1, 0:1024], in_=b_in[:, 0:1024]).then_inc(Ld.h, 16)
    Ld.val += 16
    Ld.wait(v)
    V.inc(v.tensor_copy(brow[:, 0:1024], wtf[0:1, 0:1024]))
    V.wait(sy)
    sy.dma_start(out=wtf[0:1, 0:1024], in_=b_in[:, 1024:2048]).then_inc(Ld.h, 16)
    Ld.val += 16
    Ld.wait(v)
    V.inc(v.tensor_copy(brow[:, 1024:2048], wtf[0:1, 0:1024]))
    V.wait(sy)
    sy.dma_start(out=wtf[0:1, 0:1024], in_=b_h).then_inc(Ld.h, 16)
    Ld.val += 16
    Ld.wait(v)
    V.inc(v.tensor_copy(brow[:, 2048:2048 + D], wtf[0:1, 0:1024]))
    v.memset(onescol[:], 1.0)
    v.memset(ones_row[:], 1.0)
    ei = V.inc(v.drain())
    V.wait(t, ei); V.wait(sy)
    for k in range(JD):
        for hh in range(2):
            Lw.wait(sy)
            V.wait(sy)
            sy.dma_start(out=wtf[:], in_=winT[128 * k:128 * (k + 1),
                                            1024 * hh:1024 * (hh + 1)]).then_inc(Lw.h, 16)
            Lw.val += 16
            Lw.wait(v)
            V.inc(v.tensor_copy(wtb[:, k * EH + 1024 * hh: k * EH + 1024 * (hh + 1)], wtf[:]))
    ei = V.inc(v.drain())
    V.wait(t, ei)

    def ln_stats(xsrc_sb, off_fn, eps):
        for j in range(JD):
            xsl = xsrc_sb[:, off_fn(j):off_fn(j) + TCW]
            ei = V.inc(v.tensor_copy(xb_c[:], xsl))
            eis = S.inc(s.activation(sq_c[:], xsl, AF.Square))
            V.wait(t, ei); S.wait(t, eis)
            t.matmul(pA[0:1, 0:TCW], onescol[:], xb_c[:], start=(j == 0), stop=(j == JD - 1))
            mm2 = t.matmul(pB[0:1, 0:TCW], onescol[:], sq_c[:], start=(j == 0), stop=(j == JD - 1))
            eit = T.inc(mm2)
            T.wait(v, eit); T.wait(s, eit)
        v.tensor_scalar_mul(st_f[:, 0:TCW], pA[0:1, 0:TCW], 1.0 / D)
        v.drain()
        v.tensor_scalar_mul(st_f[:, TCW:2 * TCW], pB[0:1, 0:TCW], 1.0 / D)
        v.drain()
        v.tensor_mul(st_f[:, 3 * TCW:4 * TCW], st_f[:, 0:TCW], st_f[:, 0:TCW])
        v.drain()
        v.tensor_sub(st_f[:, TCW:2 * TCW], st_f[:, TCW:2 * TCW], st_f[:, 3 * TCW:4 * TCW])
        v.drain()
        v.tensor_scalar_add(st_f[:, TCW:2 * TCW], st_f[:, TCW:2 * TCW], eps)
        ei = V.inc(v.drain())
        V.wait(s, ei)
        ei = S.inc(s.activation(st_f[:, 2 * TCW:3 * TCW], st_f[:, TCW:2 * TCW], AF.Sqrt))
        S.wait(v, ei)
        v.reciprocal(st_f[:, 2 * TCW:3 * TCW], st_f[:, 2 * TCW:3 * TCW])
        v.drain()
        v.tensor_copy(st_b[:, 0:TCW], st_f[:, 0:TCW])
        v.drain()
        ei = V.inc(v.tensor_copy(st_b[:, TCW:2 * TCW], st_f[:, 2 * TCW:3 * TCW]))
        V.wait(t, ei)
        mm1 = t.matmul(pC[:, 0:TCW], ones_row[:, 0:128], st_b[:, 0:TCW], start=True, stop=True)
        mm2 = t.matmul(pD[:, 0:TCW], ones_row[:, 0:128], st_b[:, TCW:2 * TCW], start=True, stop=True)
        eit = T.inc(mm2)
        T.wait(v, eit)

    for tc in range(NTC):
        c0 = tc * TCW
        V.wait(sy); S.wait(sy); T.wait(sy); Ld.wait(sy)
        for j in range(JD):
            sy.dma_start(out=xjs[:, j * TCW:(j + 1) * TCW],
                         in_=x_i[j][:, c0:c0 + TCW]).then_inc(Ld.h, 16)
        Ld.val += 16 * JD
        Ld.wait(v); Ld.wait(s)
        ln_stats(xjs, lambda j: j * TCW, 1e-5)
        for j in range(JD):
            v.tensor_sub(t_y[:], xjs[:, j * TCW:(j + 1) * TCW], pC[:, 0:TCW])
            v.drain()
            ei = V.inc(v.tensor_mul(yjs[:, j * TCW:(j + 1) * TCW], t_y[:], pD[:, 0:TCW]))
            v.drain()
            V.wait(g, ei)
            g.wait_ge(St.h, St.val)
            g.dma_start(out=xn2_o[j][:, c0:c0 + TCW],
                        in_=yjs[:, j * TCW:(j + 1) * TCW]).then_inc(St.h, 16)
            St.val += 16
        St.wait(v)
        ei = V.inc(v.drain())
        V.wait(t, ei)
        V.wait(s, ei)
        ln_stats(yjs, lambda j: j * TCW, 1e-5)
        for j in range(JD):
            v.tensor_sub(t_y[:], yjs[:, j * TCW:(j + 1) * TCW], pC[:, 0:TCW])
            v.drain()
            v.tensor_mul(t_z[:], t_y[:], pD[:, 0:TCW])
            v.drain()
            v.tensor_scalar(t_z[:], t_z[:], lnw_sb[:, j:j + 1], lnw_sb[:, JD + j:JD + j + 1],
                            ALU.mult, ALU.add)
            v.drain()
            v.tensor_copy(xnb[:, j * TL + c0: j * TL + c0 + TCW], t_z[:])
            v.drain()
        V.wait(t, V.inc(v.drain()))

    for m16 in range(16):
        for tc in range(NTC):
            c0 = tc * TCW
            for k in range(JD):
                mm = t.matmul(pA[:, 0:TCW],
                              wtb[:, k * EH + 128 * m16: k * EH + 128 * (m16 + 1)],
                              xnb[:, k * TL + c0: k * TL + c0 + TCW],
                              start=(k == 0), stop=False)
            mm = t.matmul(pA[:, 0:TCW], brow[:, 128 * m16:128 * (m16 + 1)],
                          ones_row[:, 0:TCW], start=False, stop=(m16 < 8))
            if m16 >= 8:
                mm = t.matmul(pA[:, 0:TCW], brow[:, 2048 + 128 * (m16 - 8):2048 + 128 * (m16 - 7)],
                              ones_row[:, 0:TCW], start=False, stop=True)
            eit = T.inc(mm)
            if m16 < 8:
                T.wait(s, eit)
                ei = S.inc(s.activation(t_y[:].bitcast(BF16)[:, 0:TCW], pA[:, 0:TCW], AF.Sigmoid))
                S.wait(t, ei); S.wait(g, ei)
                g.wait_ge(St.h, St.val)
                g.dma_start(out=g_o[m16][:, c0:c0 + TCW],
                            in_=t_y[:].bitcast(BF16)[:, 0:TCW]).then_inc(St.h, 16)
                St.val += 16
                St.wait(s); St.wait(v)
            else:
                T.wait(v, eit)
                ei = V.inc(v.tensor_copy(t_y[:].bitcast(BF16)[:, 0:TCW], pA[:, 0:TCW]))
                V.wait(t, ei); V.wait(g, ei)
                g.wait_ge(St.h, St.val)
                g.dma_start(out=vb_o[m16 - 8][:, c0:c0 + TCW],
                            in_=t_y[:].bitcast(BF16)[:, 0:TCW]).then_inc(St.h, 16)
                St.val += 16
                St.wait(v); St.wait(s)
    St.wait(g)
    return nc



def build_prog_E(TG_, NCORES_):
    CH = 64
    NCH = TG_ // CH
    COLS = JD * B
    TLL = B * (TG_ // NCORES_)
    CB = CH * B

    nc = bass.Bass("TRN2", target_bir_lowering=False, debug=False)
    g_i = din(nc, "g_i", [NCORES_, JD, 128, TLL], BF16)
    vb_i = din(nc, "vb_i", [NCORES_, JD, 128, TLL], BF16)
    whT = din(nc, "whT", [D, D])
    hs_o = dout(nc, "hs_o", [JD, 128, B * TG_], BF16)

    wtf = sb(nc, "wtf", [128, D], F32)
    wtb = sb(nc, "wtb", [128, JD * D], BF16)
    gb0 = sb(nc, "gb0", [128, JD * CB], BF16)
    gb1 = sb(nc, "gb1", [128, JD * CB], BF16)
    vbb0 = sb(nc, "vbb0", [128, JD * CB], BF16)
    vbb1 = sb(nc, "vbb1", [128, JD * CB], BF16)
    h = sb(nc, "h", [128, COLS], F32)
    hbf = sb(nc, "hbf", [128, COLS], BF16)
    pre = sb(nc, "pre", [128, COLS], F32)
    cc = sb(nc, "cc", [128, COLS], F32)
    hs0 = sb(nc, "hs0", [128, JD * CB], BF16)
    hs1 = sb(nc, "hs1", [128, JD * CB], BF16)
    pz = ps(nc, "pz", [128, COLS], F32)

    Ld = Sem(nc, "Ld"); Lp = Sem(nc, "Lp"); St = Sem(nc, "St"); St1 = Sem(nc, "St1")
    s_mm = Sem(nc, "s_mm"); s_pre = Sem(nc, "s_pre"); s_tanh = Sem(nc, "s_tanh")
    s_bl = Sem(nc, "s_bl"); s_fl = Sem(nc, "s_fl"); V2 = Sem(nc, "V2")
    v, s, t, g, sy = nc.vector, nc.scalar, nc.tensor, nc.gpsimd, nc.sync

    gbuf = [gb0, gb1]; vbuf = [vbb0, vbb1]; hsb = [hs0, hs1]

    for k in range(JD):
        if k > 0:
            Ld.wait(sy)
            V2.wait(sy)
        sy.dma_start(out=wtf[:], in_=whT[128 * k:128 * (k + 1), :]).then_inc(Ld.h, 16)
        Ld.val += 16
        Ld.wait(v)
        V2.inc(v.tensor_copy(wtb[:, k * D:(k + 1) * D], wtf[:]))
    v.drain()
    v.memset(h[:], 0.0)
    v.memset(hbf[:], 0.0).then_inc(s_bl.h, 1)
    s_bl.val += 1

    def chunk_src(c):
        r = c // (NCH // NCORES_)
        tl0 = (c % (NCH // NCORES_)) * CB
        return r, tl0

    r, tl0 = chunk_src(0)
    for j in range(JD):
        sy.dma_start(out=gbuf[0][:, j * CB:(j + 1) * CB],
                     in_=g_i[r, j][:, tl0:tl0 + CB]).then_inc(Lp.h, 16)
        sy.dma_start(out=vbuf[0][:, j * CB:(j + 1) * CB],
                     in_=vb_i[r, j][:, tl0:tl0 + CB]).then_inc(Lp.h, 16)
    Lp.val += 16 * 2 * JD
    for c in range(1, NCH):
        r, tl0 = chunk_src(c)
        sy.wait_ge(s_bl.h, 1 + (c - 1) * CH)
        Lp.wait(sy)
        for j in range(JD):
            sy.dma_start(out=gbuf[c % 2][:, j * CB:(j + 1) * CB],
                         in_=g_i[r, j][:, tl0:tl0 + CB]).then_inc(Lp.h, 16)
            sy.dma_start(out=vbuf[c % 2][:, j * CB:(j + 1) * CB],
                         in_=vb_i[r, j][:, tl0:tl0 + CB]).then_inc(Lp.h, 16)
        Lp.val += 16 * 2 * JD

    t.wait_ge(Ld.h, Ld.val)
    t.wait_ge(V2.h, V2.val)
    with t.Fori(0, TG_) as i:
        t.wait_ge(s_bl.h, i + 1)
        for o in range(JD):
            for k in range(JD):
                mm = t.matmul(pz[:, 4 * o:4 * (o + 1)],
                              wtb[:, k * D + 128 * o: k * D + 128 * (o + 1)],
                              hbf[:, 4 * k:4 * (k + 1)],
                              start=(k == 0), stop=(k == JD - 1))
        mm.then_inc(s_mm.h, 1)
    s_mm.val += TG_

    with s.Fori(0, TG_) as i:
        s.wait_ge(s_pre.h, i + 1)
        s.activation(cc[:], pre[:], AF.Tanh).then_inc(s_tanh.h, 1)
    s_tanh.val += TG_

    PAIRS = NCH // 2
    with v.Fori(0, PAIRS) as oc2:
        for p in (0, 1):
            v.wait_ge(Lp.h, oc2 * (16 * 2 * JD * 2) + (p + 1) * 16 * 2 * JD)
            with v.Fori(0, CH) as i:
                nsy = oc2 * (2 * CH) + p * CH
                v.wait_ge(s_mm.h, nsy + i + 1)
                v.scalar_tensor_tensor(
                    pre[:], pz[:], 1.0,
                    vbuf[p][:, 0:JD * CB].rearrange("q (j t) -> q j t", j=JD)[:, :, ts(i, 4)],
                    ALU.mult, ALU.add)
                v.drain().then_inc(s_pre.h, 1)
                v.wait_ge(s_tanh.h, nsy + i + 1)
                v.tensor_sub(cc[:], cc[:], h[:])
                v.drain()
                v.tensor_mul(cc[:], cc[:],
                             gbuf[p][:, 0:JD * CB].rearrange("q (j t) -> q j t", j=JD)[:, :, ts(i, 4)])
                v.drain()
                v.tensor_add(h[:], h[:], cc[:])
                v.drain()
                v.tensor_copy(hbf[:], h[:])
                v.drain()
                v.tensor_copy(hsb[p][:, 0:JD * CB].rearrange("q (j t) -> q j t", j=JD)[:, :, ts(i, 4)],
                              h[:]).then_inc(s_bl.h, 1)
            v.wait_ge(s_fl.h, oc2 * 2 + p + 1)
    s_pre.val += TG_
    s_bl.val += TG_
    s_fl_total = 0

    for c in range(NCH):
        par = c % 2
        Sc = St if par == 0 else St1
        g.wait_ge(s_bl.h, 1 + (c + 1) * CH)
        g.wait_ge(Sc.h, Sc.val)
        for j in range(JD):
            g.dma_start(out=hs_o[j][:, c * CB:(c + 1) * CB],
                        in_=hsb[par][:, j * CB:(j + 1) * CB]).then_inc(Sc.h, 16)
        Sc.val += 16 * JD
        g.wait_ge(Sc.h, Sc.val)
        g.engine_nop().then_inc(s_fl.h, 1)
        s_fl.val += 1
    g.wait_ge(St.h, St.val)
    g.wait_ge(St1.h, St1.val)
    return nc



def build_prog_F(TLOC):
    TL = B * TLOC
    NTC = max(1, TL // 512)
    TCW = min(512, TL)

    nc = bass.Bass("TRN2", target_bir_lowering=False, debug=False)
    x_i = din(nc, "x_i", [JD, 128, TL])
    xn2_i = din(nc, "xn2_i", [JD, 128, TL])
    hs_i = din(nc, "hs_i", [JD, 128, TL], BF16)
    woutT = din(nc, "woutT", [D, D])
    bout = din(nc, "bout", [1, D])
    xout_o = dout(nc, "xout", [JD, 128, TL])

    hsb = sb(nc, "hsb", [128, JD * TL], BF16)
    wtf = sb(nc, "wtf", [128, D], F32)
    wtb = sb(nc, "wtb", [128, JD * D], BF16)
    brow = sb(nc, "brow", [1, D], BF16)
    ones_row = sb(nc, "ones_row", [1, TL], BF16)
    xs = sb(nc, "xs", [128, TL], F32)
    xn2s = sb(nc, "xn2s", [128, TL], F32)
    t_y = sb(nc, "t_y", [128, TCW], F32)
    pA = ps(nc, "pA", [128, TCW], F32)

    Ld = Sem(nc, "Ld"); Lw = Sem(nc, "Lw"); St = Sem(nc, "St")
    T = Sem(nc, "T"); V = Sem(nc, "V"); S = Sem(nc, "S")
    v, s, t, g, sy = nc.vector, nc.scalar, nc.tensor, nc.gpsimd, nc.sync

    for j in range(JD):
        sy.dma_start(out=hsb[:, j * TL:(j + 1) * TL], in_=hs_i[j]).then_inc(Ld.h, 16)
        Ld.val += 16
    sy.dma_start(out=wtf[0:1, :], in_=bout).then_inc(Ld.h, 16)
    Ld.val += 16
    Ld.wait(v)
    v.tensor_copy(brow[:], wtf[0:1, :])
    v.memset(ones_row[:], 1.0)
    ei = V.inc(v.drain())
    V.wait(sy)
    for k in range(JD):
        Lw.wait(sy)
        V.wait(sy)
        sy.dma_start(out=wtf[:], in_=woutT[128 * k:128 * (k + 1), :]).then_inc(Lw.h, 16)
        Lw.val += 16
        Lw.wait(v)
        V.inc(v.tensor_copy(wtb[:, k * D:(k + 1) * D], wtf[:]))
    ei = V.inc(v.drain())
    V.wait(t, ei)

    for m8 in range(JD):
        V.wait(sy); Lw.wait(sy)
        sy.dma_start(out=xs[:], in_=x_i[m8]).then_inc(Lw.h, 16)
        sy.dma_start(out=xn2s[:], in_=xn2_i[m8]).then_inc(Lw.h, 16)
        Lw.val += 32
        for tc in range(NTC):
            c0 = tc * TCW
            for k in range(JD):
                mm = t.matmul(pA[:, 0:TCW],
                              wtb[:, k * D + 128 * m8: k * D + 128 * (m8 + 1)],
                              hsb[:, k * TL + c0: k * TL + c0 + TCW],
                              start=(k == 0), stop=False)
            mm = t.matmul(pA[:, 0:TCW], brow[:, 128 * m8:128 * (m8 + 1)],
                          ones_row[:, 0:TCW], start=False, stop=True)
            eit = T.inc(mm)
            T.wait(v, eit)
            Lw.wait(v)
            v.tensor_add(t_y[:], pA[:, 0:TCW], xs[:, c0:c0 + TCW])
            v.drain()
            ei = V.inc(v.tensor_add(t_y[:], t_y[:], xn2s[:, c0:c0 + TCW]))
            V.wait(t, ei)
            V.wait(g, ei)
            g.wait_ge(St.h, St.val)
            g.dma_start(out=xout_o[m8][:, c0:c0 + TCW], in_=t_y[:]).then_inc(St.h, 16)
            St.val += 16
            St.wait(v)
    St.wait(g)
    return nc



def build_prog_I(TLOC):
    TL = B * TLOC
    NTC = max(1, TL // 512)
    TCW = min(512, TL)
    E = 32
    HI = 512
    HCE = HI // 128
    HS = 1024
    HCS = HS // 128

    nc = bass.Bass("TRN2", target_bir_lowering=False, debug=False)
    x_i = din(nc, "x_i", [JD, 128, TL])
    gw_i = din(nc, "gw", [D, E])
    gb_i = din(nc, "gb", [1, E])
    w1_i = din(nc, "w1", [E, D, HI])
    b1_i = din(nc, "b1", [E, HI])
    w3_i = din(nc, "w3", [E, D, HI])
    b3_i = din(nc, "b3", [E, HI])
    w2_i = din(nc, "w2", [E, HI, D])
    b2_i = din(nc, "b2", [E, D])
    sw1_i = din(nc, "sw1", [D, HS])
    sb1_i = din(nc, "sb1", [1, HS])
    sw3_i = din(nc, "sw3", [D, HS])
    sb3_i = din(nc, "sb3", [1, HS])
    sw2_i = din(nc, "sw2", [HS, D])
    sb2_i = din(nc, "sb2", [1, D])
    identf = din(nc, "identf", [128, 128], F32)
    xout_o = dout(nc, "xout", [JD, 128, TL])

    uni = sb(nc, "uni", [128, JD * TCW], F32)
    xjs = uni
    xnb = sb(nc, "xnb", [128, JD * TL], BF16)
    xb_c = sb(nc, "xb_c", [128, TCW], BF16)
    sq_c = sb(nc, "sq_c", [128, TCW], BF16)
    onescol = sb(nc, "onescol", [128, 1], BF16)
    ones_row = sb(nc, "ones_row", [1, TL], BF16)
    st_f = sb(nc, "st_f", [1, 4 * TCW], F32)
    st_b = sb(nc, "st_b", [1, 2 * TCW], BF16)
    gw_sb = sb(nc, "gw_sb", [128, JD * E], BF16)
    gb_sb = sb(nc, "gb_sb", [1, E], BF16)
    idf = sb(nc, "idf", [128, 128], F32)
    lg = sb(nc, "lg", [32, TCW], F32)
    lgt = sb(nc, "lgt", [128, E], F32)
    gm1 = sb(nc, "gm1", [128, 1], F32)
    gm8 = sb(nc, "gm8", [128, 8], F32)
    gsc = sb(nc, "gsc", [128, 8], F32)
    gtmp = sb(nc, "gtmp", [128, 8], F32)
    kbc = sb(nc, "kbc", [128, E], F32)
    wmap = sb(nc, "wmap", [128, E], F32)
    wfmb = sb(nc, "wfmb", [32, NTC * TCW], BF16)
    wrow_b = sb(nc, "wrow_b", [1, TCW], BF16)
    wbc = sb(nc, "wbc", [128, TCW], BF16)
    w1b = sb(nc, "w1b", [128, JD * HS], BF16)
    w3b = sb(nc, "w3b", [128, JD * HS], BF16)
    w2b = sb(nc, "w2b", [128, HCS * D], BF16)
    wtf = sb(nc, "wtf", [128, 1024], F32)
    brow = sb(nc, "brow", [1, 2 * HI], BF16)
    b2b = sb(nc, "b2b", [32, D], BF16)
    b2f = sb(nc, "b2f", [32, D], F32)
    sbrow = sb(nc, "sbrow", [1, 2 * HS + D], BF16)
    hbuf = uni.bitcast(BF16)
    htmp = sb(nc, "htmp", [128, TCW], BF16)
    htmp2 = sb(nc, "htmp2", [128, TCW], BF16)
    oacc = sb(nc, "oacc", [128, JD * TL], F32)
    t_y = sb(nc, "t_y", [128, TCW], F32)
    pA = ps(nc, "pA", [128, TCW], F32)
    pB = ps(nc, "pB", [128, TCW], F32)
    pC = ps(nc, "pC", [128, TCW], F32)
    pD = ps(nc, "pD", [128, TCW], F32)
    pb1 = pA; pb2 = pB; pbc = pC; pbc2 = pD
    pl = pA; ptt = pB; pth = pC
    pha = pA; phb = pB; po = pC; pw = pD

    Ld = Sem(nc, "Ld"); Lw = Sem(nc, "Lw"); St = Sem(nc, "St")
    T = Sem(nc, "T"); V = Sem(nc, "V"); S = Sem(nc, "S")
    v, s, t, g, sy = nc.vector, nc.scalar, nc.tensor, nc.gpsimd, nc.sync

    sy.dma_start(out=idf[:], in_=identf).then_inc(Ld.h, 16)
    Ld.val += 16
    for j in range(JD):
        Ld.wait(sy); V.wait(sy)
        sy.dma_start(out=wtf[:, 0:E], in_=gw_i[128 * j:128 * (j + 1), :]).then_inc(Ld.h, 16)
        Ld.val += 16
        Ld.wait(v)
        V.inc(v.tensor_copy(gw_sb[:, j * E:(j + 1) * E], wtf[:, 0:E]))
    Ld.wait(sy); V.wait(sy)
    sy.dma_start(out=wtf[0:1, 0:E], in_=gb_i).then_inc(Ld.h, 16)
    Ld.val += 16
    Ld.wait(v)
    v.tensor_copy(gb_sb[:], wtf[0:1, 0:E])
    v.memset(onescol[:], 1.0)
    v.memset(ones_row[:], 1.0)
    v.memset(oacc[:], 0.0)
    ei = V.inc(v.drain())
    V.wait(t, ei); V.wait(sy)
    for (name, src, off, w) in [("sb1", sb1_i, 0, HS), ("sb3", sb3_i, HS, HS), ("sb2", sb2_i, 2 * HS, D)]:
        Ld.wait(sy); V.wait(sy)
        sy.dma_start(out=wtf[0:1, 0:w], in_=src).then_inc(Ld.h, 16)
        Ld.val += 16
        Ld.wait(v)
        V.inc(v.tensor_copy(sbrow[:, off:off + w], wtf[0:1, 0:w]))
    Ld.wait(sy); V.wait(sy)
    sy.dma_start(out=b2f[0:32, :], in_=b2_i).then_inc(Ld.h, 16)
    Ld.val += 16
    Ld.wait(v)
    ei = V.inc(v.tensor_copy(b2b[0:32, :], b2f[0:32, :]))
    V.wait(t, ei)

    for tc in range(NTC):
        c0 = tc * TCW
        V.wait(sy); S.wait(sy); T.wait(sy); Ld.wait(sy)
        for j in range(JD):
            sy.dma_start(out=xjs[:, j * TCW:(j + 1) * TCW],
                         in_=x_i[j][:, c0:c0 + TCW]).then_inc(Ld.h, 16)
        Ld.val += 16 * JD
        Ld.wait(v); Ld.wait(s)
        for j in range(JD):
            xsl = xjs[:, j * TCW:(j + 1) * TCW]
            ei = V.inc(v.tensor_copy(xb_c[:], xsl))
            eis = S.inc(s.activation(sq_c[:], xsl, AF.Square))
            V.wait(t, ei); S.wait(t, eis)
            t.matmul(pb1[0:1, 0:TCW], onescol[:], xb_c[:], start=(j == 0), stop=(j == JD - 1))
            mm2 = t.matmul(pb2[0:1, 0:TCW], onescol[:], sq_c[:], start=(j == 0), stop=(j == JD - 1))
            eit = T.inc(mm2)
            T.wait(v, eit); T.wait(s, eit)
        v.tensor_scalar_mul(st_f[:, 0:TCW], pb1[0:1, 0:TCW], 1.0 / D)
        v.drain()
        v.tensor_scalar_mul(st_f[:, TCW:2 * TCW], pb2[0:1, 0:TCW], 1.0 / D)
        v.drain()
        v.tensor_mul(st_f[:, 3 * TCW:4 * TCW], st_f[:, 0:TCW], st_f[:, 0:TCW])
        v.drain()
        v.tensor_sub(st_f[:, TCW:2 * TCW], st_f[:, TCW:2 * TCW], st_f[:, 3 * TCW:4 * TCW])
        v.drain()
        v.tensor_scalar_add(st_f[:, TCW:2 * TCW], st_f[:, TCW:2 * TCW], 1e-5)
        ei = V.inc(v.drain())
        V.wait(s, ei)
        ei = S.inc(s.activation(st_f[:, 2 * TCW:3 * TCW], st_f[:, TCW:2 * TCW], AF.Sqrt))
        S.wait(v, ei)
        v.reciprocal(st_f[:, 2 * TCW:3 * TCW], st_f[:, 2 * TCW:3 * TCW])
        v.drain()
        v.tensor_copy(st_b[:, 0:TCW], st_f[:, 0:TCW])
        v.drain()
        ei = V.inc(v.tensor_copy(st_b[:, TCW:2 * TCW], st_f[:, 2 * TCW:3 * TCW]))
        V.wait(t, ei)
        mm1 = t.matmul(pbc[:, 0:TCW], ones_row[:, 0:128], st_b[:, 0:TCW], start=True, stop=True)
        mm2 = t.matmul(pbc2[:, 0:TCW], ones_row[:, 0:128], st_b[:, TCW:2 * TCW], start=True, stop=True)
        eit = T.inc(mm2)
        T.wait(v, eit)
        for j in range(JD):
            v.tensor_sub(t_y[:], xjs[:, j * TCW:(j + 1) * TCW], pbc[:, 0:TCW])
            v.drain()
            v.tensor_mul(xnb[:, j * TL + c0: j * TL + c0 + TCW], t_y[:], pbc2[:, 0:TCW])
            v.drain()
        ei = V.inc(v.drain())
        V.wait(t, ei)
        for j in range(JD):
            t.matmul(pl[0:32, 0:TCW], gw_sb[:, j * E:(j + 1) * E],
                     xnb[:, j * TL + c0: j * TL + c0 + TCW], start=(j == 0), stop=False)
        mm = t.matmul(pl[0:32, 0:TCW], gb_sb[:], ones_row[:, 0:TCW], start=False, stop=True)
        eit = T.inc(mm)
        T.wait(v, eit)
        ei = V.inc(v.tensor_copy(lg[0:32, 0:TCW], pl[0:32, 0:TCW]))
        V.wait(t, ei)
        for sbt in range(TCW // 128):
            mm = t.matmul(ptt[:, 0:E], lg[0:32, 128 * sbt:128 * (sbt + 1)], idf[0:32, 0:E],
                          start=True, stop=True)
            eit = T.inc(mm)
            T.wait(v, eit)
            v.tensor_copy(lgt[:], ptt[:, 0:E])
            v.drain()
            v.reduce_max(gm1[:], lgt[:], AX.X)
            v.drain()
            ei = V.inc(v.tensor_scalar(lgt[:], lgt[:], gm1[:], None, ALU.subtract))
            V.wait(s, ei)
            ei = S.inc(s.activation(lgt[:], lgt[:], AF.Exp))
            S.wait(v, ei)
            v.tensor_reduce(gm1[:], lgt[:], AX.X, ALU.add)
            v.drain()
            v.reciprocal(gm1[:], gm1[:])
            v.drain()
            v.tensor_scalar(lgt[:], lgt[:], gm1[:], None, ALU.mult)
            v.drain()
            pairs = [(0, 1), (0, 2), (0, 3), (1, 2), (1, 3), (2, 3)]
            first = True
            for (a, b_) in pairs:
                v.tensor_add(gtmp[:], lgt[:, a:E:4], lgt[:, b_:E:4])
                v.drain()
                if first:
                    v.tensor_copy(gsc[:], gtmp[:])
                    first = False
                else:
                    v.tensor_max(gsc[:], gsc[:], gtmp[:])
                v.drain()
            v.max(out=gm8[:], in_=gsc[:])
            v.drain()
            v.tensor_scalar(gsc[:], gsc[:], gm8[:, 3:4], None, ALU.is_ge)
            v.drain()
            v.tensor_copy(kbc[:].rearrange("p (g i) -> p g i", g=8),
                          gsc[:].unsqueeze(2).to_broadcast([128, 8, 4]))
            v.drain()
            v.tensor_mul(wmap[:], lgt[:], kbc[:])
            v.drain()
            v.tensor_scalar(kbc[:], kbc[:], -1.0, 1e30, ALU.add, ALU.mult)
            v.drain()
            v.tensor_add(wmap[:], wmap[:], kbc[:])
            v.drain()
            v.max(out=gm8[:], in_=wmap[:])
            v.drain()
            v.tensor_scalar(wmap[:], wmap[:], gm8[:, 3:4], None, ALU.is_ge)
            v.drain()
            ei = V.inc(v.tensor_mul(wmap[:], wmap[:], lgt[:]))
            V.wait(t, ei)
            mm = t.transpose(pth[0:32, 0:128], wmap[:], idf[:])
            eit = T.inc(mm)
            T.wait(v, eit)
            ei = V.inc(v.tensor_copy(wfmb[0:32, c0 + 128 * sbt: c0 + 128 * (sbt + 1)],
                                     pth[0:32, 0:128]))
            V.wait(t, ei)
        v.drain()
    St.wait(v)
    ei = V.inc(v.drain())
    V.wait(t, ei); V.wait(g, ei); V.wait(sy)

    for e in range(-1, E):
        HCX = HCS if e < 0 else HCE
        T.wait(sy)
        for k in range(JD):
            for (wsrc, dstb, hw) in ([(sw1_i, w1b, HS), (sw3_i, w3b, HS)] if e < 0
                                     else [(w1_i[e], w1b, HI), (w3_i[e], w3b, HI)]):
                V.wait(sy); Lw.wait(sy)
                sy.dma_start(out=wtf[:, 0:hw], in_=wsrc[128 * k:128 * (k + 1), :]).then_inc(Lw.h, 16)
                Lw.val += 16
                Lw.wait(v)
                V.inc(v.tensor_copy(dstb[:, k * hw:(k + 1) * hw], wtf[:, 0:hw]))
        for hk in range(HCX):
            V.wait(sy); Lw.wait(sy)
            src = sw2_i[128 * hk:128 * (hk + 1), :] if e < 0 else w2_i[e, 128 * hk:128 * (hk + 1), :]
            sy.dma_start(out=wtf[:, 0:D], in_=src).then_inc(Lw.h, 16)
            Lw.val += 16
            Lw.wait(v)
            V.inc(v.tensor_copy(w2b[:, hk * D:(hk + 1) * D], wtf[:, 0:D]))
        if e >= 0:
            for (bsrc, off) in [(b1_i[e:e + 1, :], 0), (b3_i[e:e + 1, :], HI)]:
                V.wait(sy); Lw.wait(sy)
                sy.dma_start(out=wtf[0:1, 0:HI], in_=bsrc).then_inc(Lw.h, 16)
                Lw.val += 16
                Lw.wait(v)
                V.inc(v.tensor_copy(brow[:, off:off + HI], wtf[0:1, 0:HI]))
        ei = V.inc(v.drain())
        V.wait(t, ei)
        for tc in range(NTC):
            c0 = tc * TCW
            if e >= 0:
                V.wait(sy); Lw.wait(sy)
                T.wait(sy)
                sy.dma_start(out=wrow_b[:], in_=wfmb[e:e + 1, c0:c0 + TCW]).then_inc(Lw.h, 16)
                Lw.val += 16
                Lw.wait(t)
                mm = t.matmul(pw[:, 0:TCW], ones_row[:, 0:128], wrow_b[:], start=True, stop=True)
                eit = T.inc(mm)
                T.wait(v, eit)
                ei = V.inc(v.tensor_copy(wbc[:], pw[:, 0:TCW]))
                V.wait(t, ei)
            for hk in range(HCX):
                hcol = 128 * hk
                HW = HS if e < 0 else HI
                pa_ = pha if hk % 2 == 0 else po
                pb_ = phb if hk % 2 == 0 else pw
                ht_ = htmp if hk % 2 == 0 else htmp2
                if hk >= 2:
                    V.wait(t, V.val - 1)
                for k in range(JD):
                    t.matmul(pa_[:, 0:TCW], w1b[:, k * HW + hcol: k * HW + hcol + 128],
                             xnb[:, k * TL + c0: k * TL + c0 + TCW],
                             start=(k == 0), stop=False)
                brsrc = sbrow[:, hcol:hcol + 128] if e < 0 else brow[:, hcol:hcol + 128]
                t.matmul(pa_[:, 0:TCW], brsrc, ones_row[:, 0:TCW], start=False, stop=True)
                for k in range(JD):
                    t.matmul(pb_[:, 0:TCW], w3b[:, k * HW + hcol: k * HW + hcol + 128],
                             xnb[:, k * TL + c0: k * TL + c0 + TCW],
                             start=(k == 0), stop=False)
                brsrc = sbrow[:, HS + hcol:HS + hcol + 128] if e < 0 else brow[:, HI + hcol:HI + hcol + 128]
                mm = t.matmul(pb_[:, 0:TCW], brsrc, ones_row[:, 0:TCW], start=False, stop=True)
                eit = T.inc(mm)
                T.wait(s, eit)
                ei = S.inc(s.activation(ht_[:], pa_[:, 0:TCW], AF.Silu))
                S.wait(v, ei)
                v.tensor_mul(hbuf[:, hk * TCW:(hk + 1) * TCW], ht_[:], pb_[:, 0:TCW])
                v.drain()
                if e >= 0:
                    v.tensor_mul(hbuf[:, hk * TCW:(hk + 1) * TCW],
                                 hbuf[:, hk * TCW:(hk + 1) * TCW], wbc[:])
                    v.drain()
                eiv = V.inc(v.drain())
                V.wait(s, eiv - 2 if hk >= 1 else 0)
            V.wait(t, V.val)
            for m8 in range(JD):
                pop = po if m8 % 2 == 0 else pw
                for hk in range(HCX):
                    mm = t.matmul(pop[:, 0:TCW],
                                  w2b[:, hk * D + 128 * m8: hk * D + 128 * (m8 + 1)],
                                  hbuf[:, hk * TCW:(hk + 1) * TCW],
                                  start=(hk == 0), stop=(hk == HCX - 1))
                eit = T.inc(mm)
                T.wait(v, eit)
                osl = oacc[:, m8 * TL + c0: m8 * TL + c0 + TCW]
                v.tensor_add(osl, osl, pop[:, 0:TCW])
                ei = V.inc(v.drain())
                if m8 >= 1:
                    V.wait(t, ei - 1)
            V.wait(t, V.val)

    for m8 in range(JD):
        for tc in range(NTC):
            c0 = tc * TCW
            t.matmul(po[:, 0:TCW], b2b[0:32, 128 * m8:128 * (m8 + 1)],
                     wfmb[0:32, c0:c0 + TCW], start=True, stop=False)
            mm = t.matmul(po[:, 0:TCW], sbrow[:, 2 * HS + 128 * m8:2 * HS + 128 * (m8 + 1)],
                          ones_row[:, 0:TCW], start=False, stop=True)
            eit = T.inc(mm)
            T.wait(v, eit)
            osl = oacc[:, m8 * TL + c0: m8 * TL + c0 + TCW]
            v.tensor_add(osl, osl, po[:, 0:TCW])
            ei = V.inc(v.drain())
            V.wait(t, ei)
    for m8 in range(JD):
        V.wait(sy); Ld.wait(sy)
        sy.dma_start(out=xjs[:, 0:TL], in_=x_i[m8]).then_inc(Ld.h, 16)
        Ld.val += 16
        Ld.wait(v)
        St.wait(v)
        v.tensor_add(oacc[:, m8 * TL:(m8 + 1) * TL], oacc[:, m8 * TL:(m8 + 1) * TL],
                     xjs[:, 0:TL])
        ei = V.inc(v.drain())
        V.wait(g, ei)
        g.wait_ge(St.h, St.val)
        g.dma_start(out=xout_o[m8], in_=oacc[:, m8 * TL:(m8 + 1) * TL]).then_inc(St.h, 16)
        St.val += 16
    St.wait(g)
    return nc



def build_prog_J(TLOC):
    TL = B * TLOC
    NTC = max(1, TL // 512)
    TCW = min(512, TL)
    CH = 64
    NCHJ = TLOC // CH
    PI = float(np.pi)
    nc = bass.Bass("TRN2", target_bir_lowering=False, debug=False)
    x_i = din(nc, "x_i", [JD, 128, TL])
    Wk_i = din(nc, "Wk", [D, D]); bk_i = din(nc, "bk", [1, D])
    Wv_i = din(nc, "Wv", [D, D]); bv_i = din(nc, "bv", [1, D])
    Wq_i = din(nc, "Wq", [D, D]); bq_i = din(nc, "bq", [1, D])
    Wkp_i = din(nc, "Wkp", [D, D]); bkp_i = din(nc, "bkp", [1, D])
    Wqp_i = din(nc, "Wqp", [D, D]); bqp_i = din(nc, "bqp", [1, D])
    Wo_i = din(nc, "Wo", [D, D]); bo_i = din(nc, "bo", [1, D])
    ps_i = din(nc, "phs", [128, JD])
    lnwb = din(nc, "lnwb", [2, 128, JD])
    xout_o = dout(nc, "xout", [JD, 128, TL])
    xn1_s = dint(nc, "xn1_s", [JD, 128, TL])

    xnb = sb(nc, "xnb", [128, JD * TL], BF16)
    kqb = sb(nc, "kqb", [128, JD * TL], BF16)
    vbuf = sb(nc, "vbuf", [128, JD * TL], BF16)
    mrb = sb(nc, "mrb", [128, JD * TL], BF16)
    mib = sb(nc, "mib", [128, JD * TL], BF16)
    wtf = sb(nc, "wtf", [128, 512], F32)
    wtb = sb(nc, "wtb", [128, JD * D], BF16)
    brow = sb(nc, "brow", [65, 2 * D], BF16)

    def bro(q, lo, w_):
        p0 = 32 * (q % 3)
        c0_ = D * (q // 3)
        return brow[p0:p0 + 1, c0_ + lo:c0_ + lo + w_]

    def onr(q, w_):
        p0 = 32 * (q % 3)
        return ones_row[p0:p0 + 1, 0:w_]
    lnw_sb = sb(nc, "lnw_sb", [128, 2 * JD], F32)
    ps_sb = sb(nc, "ps_sb", [128, JD], F32)
    onescol = sb(nc, "onescol", [128, 1], BF16)
    ones_row = sb(nc, "ones_row", [65, TCW], BF16)
    st_f = sb(nc, "st_f", [1, 4 * TCW], F32)
    st_b = sb(nc, "st_b", [1, 2 * TCW], BF16)
    xb_c = sb(nc, "xb_c", [128, TCW], BF16)
    sq_c = sb(nc, "sq_c", [128, TCW], BF16)
    t_y = sb(nc, "t_y", [128, TCW], F32)
    t_z = sb(nc, "t_z", [128, TCW], F32)
    t_w = sb(nc, "t_w", [128, TCW], F32)
    pA = ps(nc, "pA", [128, TCW], F32)
    pB = ps(nc, "pB", [128, TCW], F32)
    pC = ps(nc, "pC", [128, TCW], F32)
    pD = ps(nc, "pD", [128, TCW], F32)

    Ld = Sem(nc, "Ld"); Lw = Sem(nc, "Lw"); St = Sem(nc, "St")
    T = Sem(nc, "T"); V = Sem(nc, "V"); S = Sem(nc, "S")
    v, s, t, g, sy = nc.vector, nc.scalar, nc.tensor, nc.gpsimd, nc.sync

    sy.dma_start(out=lnw_sb[:, 0:JD], in_=lnwb[0]).then_inc(Ld.h, 16); Ld.val += 16
    sy.dma_start(out=lnw_sb[:, JD:2 * JD], in_=lnwb[1]).then_inc(Ld.h, 16); Ld.val += 16
    sy.dma_start(out=ps_sb[:], in_=ps_i).then_inc(Ld.h, 16); Ld.val += 16
    for q, src in enumerate([bk_i, bv_i, bq_i, bkp_i, bqp_i, bo_i]):
        for hh in range(2):
            Ld.wait(sy); V.wait(sy)
            sy.dma_start(out=wtf[0:1, :], in_=src[:, 512 * hh:512 * (hh + 1)]).then_inc(Ld.h, 16)
            Ld.val += 16
            Ld.wait(v)
            V.inc(v.tensor_copy(bro(q, 512 * hh, 512), wtf[0:1, :]))
    v.memset(onescol[:], 1.0)
    v.memset(ones_row[:], 1.0)
    ei = V.inc(v.drain())
    V.wait(t, ei); V.wait(sy)

    def load_w(wsrc):
        for k in range(JD):
            for hh in range(2):
                Lw.wait(sy); V.wait(sy); T.wait(sy)
                sy.dma_start(out=wtf[:], in_=wsrc[128 * k:128 * (k + 1),
                                                  512 * hh:512 * (hh + 1)]).then_inc(Lw.h, 16)
                Lw.val += 16
                Lw.wait(v)
                V.inc(v.tensor_copy(wtb[:, k * D + 512 * hh: k * D + 512 * (hh + 1)], wtf[:]))
        ei = V.inc(v.drain())
        V.wait(t, ei)

    def proj(src_sb, dst_sb, brow_off):
        for m8 in range(JD):
            for tc in range(NTC):
                c0 = tc * TCW
                for k in range(JD):
                    t.matmul(pA[:, 0:TCW], wtb[:, k * D + 128 * m8: k * D + 128 * (m8 + 1)],
                             src_sb[:, k * TL + c0: k * TL + c0 + TCW],
                             start=(k == 0), stop=False)
                mm = t.matmul(pA[:, 0:TCW], bro(brow_off, 128 * m8, 128),
                              onr(brow_off, TCW), start=False, stop=True)
                eit = T.inc(mm)
                T.wait(v, eit)
                ei = V.inc(v.tensor_copy(dst_sb[:, m8 * TL + c0: m8 * TL + c0 + TCW],
                                         pA[:, 0:TCW]))
                V.wait(t, ei)

    def emit_trig(phase_src_psum, m8, tc):
        c0 = tc * TCW
        eit = T.inc(phase_src_psum)
        T.wait(s, eit)
        ei = S.inc(s.activation(t_y[:], pA[:, 0:TCW], AF.Tanh))
        S.wait(t, ei)
        S.wait(v, ei)
        v.tensor_scalar(t_y[:], t_y[:], ps_sb[:, m8:m8 + 1], None, ALU.mult)
        v.drain()
        v.tensor_scalar_mul(t_z[:], t_y[:], 1.0 / (2 * PI))
        v.drain()
        v.tensor_copy(t_w[:].bitcast(I32), t_z[:])
        v.drain()
        v.tensor_copy(t_z[:], t_w[:].bitcast(I32))
        v.drain()
        v.scalar_tensor_tensor(t_y[:], t_z[:], -2 * PI, t_y[:], ALU.mult, ALU.add)
        v.drain()
        v.tensor_scalar(t_z[:], t_y[:], PI, -2 * PI, ALU.is_gt, ALU.mult)
        v.drain()
        v.tensor_add(t_y[:], t_y[:], t_z[:])
        v.drain()
        v.tensor_scalar(t_z[:], t_y[:], -PI, 2 * PI, ALU.is_lt, ALU.mult)
        v.drain()
        v.tensor_add(t_y[:], t_y[:], t_z[:])
        v.drain()

    def sin_of(src_ap, dst_ap):
        ei = V.inc(v.drain())
        V.wait(s, ei)
        ei = S.inc(s.activation(dst_ap, src_ap, AF.Sin))
        S.wait(v, ei)

    for tc in range(NTC):
        c0 = tc * TCW
        V.wait(sy); S.wait(sy); T.wait(sy); Ld.wait(sy)
        for j in range(JD):
            sy.dma_start(out=t_w[:], in_=x_i[j][:, c0:c0 + TCW]).then_inc(Ld.h, 16)
            Ld.val += 16
            Ld.wait(v); Ld.wait(s)
            ei = V.inc(v.tensor_copy(xb_c[:], t_w[:]))
            eis = S.inc(s.activation(sq_c[:], t_w[:], AF.Square))
            V.wait(t, ei); S.wait(t, eis)
            t.matmul(pA[0:1, 0:TCW], onescol[:], xb_c[:], start=(j == 0), stop=(j == JD - 1))
            mm2 = t.matmul(pB[0:1, 0:TCW], onescol[:], sq_c[:], start=(j == 0), stop=(j == JD - 1))
            eit = T.inc(mm2)
            T.wait(v, eit); T.wait(s, eit); T.wait(sy, eit)
            V.wait(sy)
        v.tensor_scalar_mul(st_f[:, 0:TCW], pA[0:1, 0:TCW], 1.0 / D)
        v.drain()
        v.tensor_scalar_mul(st_f[:, TCW:2 * TCW], pB[0:1, 0:TCW], 1.0 / D)
        v.drain()
        v.tensor_mul(st_f[:, 3 * TCW:4 * TCW], st_f[:, 0:TCW], st_f[:, 0:TCW])
        v.drain()
        v.tensor_sub(st_f[:, TCW:2 * TCW], st_f[:, TCW:2 * TCW], st_f[:, 3 * TCW:4 * TCW])
        v.drain()
        v.tensor_scalar_add(st_f[:, TCW:2 * TCW], st_f[:, TCW:2 * TCW], 1e-5)
        ei = V.inc(v.drain())
        V.wait(s, ei)
        ei = S.inc(s.activation(st_f[:, 2 * TCW:3 * TCW], st_f[:, TCW:2 * TCW], AF.Sqrt))
        S.wait(v, ei)
        v.reciprocal(st_f[:, 2 * TCW:3 * TCW], st_f[:, 2 * TCW:3 * TCW])
        v.drain()
        v.tensor_copy(st_b[:, 0:TCW], st_f[:, 0:TCW])
        v.drain()
        ei = V.inc(v.tensor_copy(st_b[:, TCW:2 * TCW], st_f[:, 2 * TCW:3 * TCW]))
        V.wait(t, ei)
        mm1 = t.matmul(pC[:, 0:TCW], ones_row[0:1, 0:128], st_b[:, 0:TCW], start=True, stop=True)
        mm2 = t.matmul(pD[:, 0:TCW], ones_row[0:1, 0:128], st_b[:, TCW:2 * TCW], start=True, stop=True)
        eit = T.inc(mm2)
        T.wait(v, eit)
        T.wait(sy, eit)
        for j in range(JD):
            sy.dma_start(out=t_w[:], in_=x_i[j][:, c0:c0 + TCW]).then_inc(Ld.h, 16)
            Ld.val += 16
            Ld.wait(v)
            v.tensor_sub(t_y[:], t_w[:], pC[:, 0:TCW])
            v.drain()
            v.tensor_mul(t_z[:], t_y[:], pD[:, 0:TCW])
            v.drain()
            ei = V.inc(v.tensor_copy(xnb[:, j * TL + c0: j * TL + c0 + TCW], t_z[:]))
            V.wait(g, ei)
            g.wait_ge(St.h, St.val)
            g.dma_start(out=xn1_s[j][:, c0:c0 + TCW], in_=t_z[:]).then_inc(St.h, 16)
            St.val += 16
            St.wait(v)
            V.wait(sy)
        V.wait(t, V.val)

    load_w(Wv_i)
    proj(xnb, vbuf, 1)
    load_w(Wk_i)
    proj(xnb, kqb, 0)
    load_w(Wkp_i)
    for m8 in range(JD):
        for tc in range(NTC):
            c0 = tc * TCW
            for k in range(JD):
                t.matmul(pA[:, 0:TCW], wtb[:, k * D + 128 * m8: k * D + 128 * (m8 + 1)],
                         kqb[:, k * TL + c0: k * TL + c0 + TCW],
                         start=(k == 0), stop=False)
            mm = t.matmul(pA[:, 0:TCW], bro(3, 128 * m8, 128),
                          onr(3, TCW), start=False, stop=True)
            emit_trig(mm, m8, tc)
            vsl = vbuf[:, m8 * TL + c0: m8 * TL + c0 + TCW]
            sin_of(t_y[:], t_w[:])
            v.tensor_mul(mib[:, m8 * TL + c0: m8 * TL + c0 + TCW], vsl, t_w[:])
            v.drain()
            v.tensor_scalar_add(t_y[:], t_y[:], PI / 2)
            v.drain()
            v.tensor_scalar(t_z[:], t_y[:], PI, -2 * PI, ALU.is_gt, ALU.mult)
            v.drain()
            v.tensor_add(t_y[:], t_y[:], t_z[:])
            sin_of(t_y[:], t_w[:])
            ei = V.inc(v.tensor_mul(mrb[:, m8 * TL + c0: m8 * TL + c0 + TCW], vsl, t_w[:]))
            V.wait(s, ei)

    for j in range(JD):
        jo = j * TL
        for b_ in range(B):
            for ch in range(NCHJ):
                off = jo + ch * CH * B + b_
                e_ = off + (CH - 1) * B + 1
                v.tensor_tensor_scan(mrb[:, off:e_:B], mrb[:, off:e_:B],
                                     mrb[:, off:e_:B], 0.0, ALU.add, ALU.bypass)
                v.tensor_tensor_scan(mib[:, off:e_:B], mib[:, off:e_:B],
                                     mib[:, off:e_:B], 0.0, ALU.add, ALU.bypass)
        v.drain()

    load_w(Wq_i)
    proj(xnb, kqb, 2)
    load_w(Wqp_i)
    for m8 in range(JD):
        for tc in range(NTC):
            c0 = tc * TCW
            sl = slice(m8 * TL + c0, m8 * TL + c0 + TCW)
            for k in range(JD):
                t.matmul(pA[:, 0:TCW], wtb[:, k * D + 128 * m8: k * D + 128 * (m8 + 1)],
                         kqb[:, k * TL + c0: k * TL + c0 + TCW],
                         start=(k == 0), stop=False)
            mm = t.matmul(pA[:, 0:TCW], bro(4, 128 * m8, 128),
                          onr(4, TCW), start=False, stop=True)
            emit_trig(mm, m8, tc)
            sin_of(t_y[:], t_w[:])
            v.tensor_mul(t_z[:], mib[:, sl], t_w[:])
            v.drain()
            v.tensor_scalar_add(t_y[:], t_y[:], PI / 2)
            v.drain()
            v.tensor_scalar(t_w[:], t_y[:], PI, -2 * PI, ALU.is_gt, ALU.mult)
            v.drain()
            v.tensor_add(t_y[:], t_y[:], t_w[:])
            sin_of(t_y[:], t_w[:])
            v.tensor_mul(t_y[:], mrb[:, sl], t_w[:])
            v.drain()
            ei = V.inc(v.tensor_add(mrb[:, sl], t_y[:], t_z[:]))
            V.wait(s, ei)

    for tc in range(NTC):
        c0 = tc * TCW
        S.wait(v); T.wait(v)
        for j in range(JD):
            xsl = mrb[:, j * TL + c0: j * TL + c0 + TCW]
            eis = S.inc(s.activation(sq_c[:], xsl, AF.Square))
            S.wait(t, eis)
            t.matmul(pA[0:1, 0:TCW], onescol[:], xsl, start=(j == 0), stop=(j == JD - 1))
            mm2 = t.matmul(pB[0:1, 0:TCW], onescol[:], sq_c[:], start=(j == 0), stop=(j == JD - 1))
            eit = T.inc(mm2)
            T.wait(s, eit)
        T.wait(v)
        v.tensor_scalar_mul(st_f[:, 0:TCW], pA[0:1, 0:TCW], 1.0 / D)
        v.drain()
        v.tensor_scalar_mul(st_f[:, TCW:2 * TCW], pB[0:1, 0:TCW], 1.0 / D)
        v.drain()
        v.tensor_mul(st_f[:, 3 * TCW:4 * TCW], st_f[:, 0:TCW], st_f[:, 0:TCW])
        v.drain()
        v.tensor_sub(st_f[:, TCW:2 * TCW], st_f[:, TCW:2 * TCW], st_f[:, 3 * TCW:4 * TCW])
        v.drain()
        v.tensor_scalar_add(st_f[:, TCW:2 * TCW], st_f[:, TCW:2 * TCW], float(D) * 1e-5)
        ei = V.inc(v.drain())
        V.wait(s, ei)
        ei = S.inc(s.activation(st_f[:, 2 * TCW:3 * TCW], st_f[:, TCW:2 * TCW], AF.Sqrt))
        S.wait(v, ei)
        v.reciprocal(st_f[:, 2 * TCW:3 * TCW], st_f[:, 2 * TCW:3 * TCW])
        v.drain()
        v.tensor_copy(st_b[:, 0:TCW], st_f[:, 0:TCW])
        v.drain()
        ei = V.inc(v.tensor_copy(st_b[:, TCW:2 * TCW], st_f[:, 2 * TCW:3 * TCW]))
        V.wait(t, ei)
        mm1 = t.matmul(pC[:, 0:TCW], ones_row[0:1, 0:128], st_b[:, 0:TCW], start=True, stop=True)
        mm2 = t.matmul(pD[:, 0:TCW], ones_row[0:1, 0:128], st_b[:, TCW:2 * TCW], start=True, stop=True)
        eit = T.inc(mm2)
        T.wait(v, eit)
        for j in range(JD):
            sl = slice(j * TL + c0, j * TL + c0 + TCW)
            v.tensor_sub(t_y[:], mrb[:, sl], pC[:, 0:TCW])
            v.drain()
            v.tensor_mul(t_y[:], t_y[:], pD[:, 0:TCW])
            v.drain()
            v.tensor_scalar(t_y[:], t_y[:], lnw_sb[:, j:j + 1], lnw_sb[:, JD + j:JD + j + 1],
                            ALU.mult, ALU.add)
            v.drain()
            v.tensor_copy(mrb[:, sl], t_y[:])
            v.drain()
        ei = V.inc(v.drain())
        V.wait(t, ei)

    load_w(Wo_i)
    for m8 in range(JD):
        for tc in range(NTC):
            c0 = tc * TCW
            for k in range(JD):
                t.matmul(pA[:, 0:TCW], wtb[:, k * D + 128 * m8: k * D + 128 * (m8 + 1)],
                         mrb[:, k * TL + c0: k * TL + c0 + TCW],
                         start=(k == 0), stop=False)
            mm = t.matmul(pA[:, 0:TCW], bro(5, 128 * m8, 128),
                          onr(5, TCW), start=False, stop=True)
            eit = T.inc(mm)
            T.wait(sy, eit)
            V.wait(sy)
            sy.dma_start(out=t_w[:], in_=x_i[m8][:, c0:c0 + TCW]).then_inc(Ld.h, 16)
            Ld.val += 16
            sy.dma_start(out=t_z[:], in_=xn1_s[m8][:, c0:c0 + TCW]).then_inc(Ld.h, 16)
            Ld.val += 16
            T.wait(v, eit)
            Ld.wait(v)
            v.tensor_add(t_y[:], pA[:, 0:TCW], t_w[:])
            v.drain()
            ei = V.inc(v.tensor_add(t_y[:], t_y[:], t_z[:]))
            V.wait(t, ei)
            V.wait(g, ei)
            g.wait_ge(St.h, St.val)
            g.dma_start(out=xout_o[m8][:, c0:c0 + TCW], in_=t_y[:]).then_inc(St.h, 16)
            St.val += 16
            St.wait(v)
    St.wait(g)
    return nc



def build_prog_H(TLOC):
    import os as _os
    TL = B * TLOC
    NTC = max(1, TL // 512)
    TCW = min(512, TL)
    EH = 2 * D
    NPT = TL // 128
    NG = NPT // 4

    nc = bass.Bass("TRN2", target_bir_lowering=False, debug=False)
    x_i = din(nc, "x_i", [JD, 128, TL])
    lnwb = din(nc, "lnwb", [2, 128, JD])
    wuT = din(nc, "wuT", [D, EH])
    bu_i = din(nc, "bu", [1, EH])
    wvT = din(nc, "wvT", [D, EH])
    bv_i = din(nc, "bv", [1, EH])
    wbT = din(nc, "wbT", [D, 128])
    bb_i = din(nc, "bb", [128, 1])
    qaff = din(nc, "qaff", [128, 4])
    woT = din(nc, "woT", [EH, D])
    bo_i = din(nc, "bo", [1, D])
    wrel = din(nc, "wrel", [1, 1023])
    costab = din(nc, "costab", [128, TL], BF16)
    sintab = din(nc, "sintab", [128, TL], BF16)
    maskt = din(nc, "maskt", [128, 128], BF16)
    aind = din(nc, "aind", [4, 128], BF16)
    identb = din(nc, "identb", [128, 128], BF16)
    xout_o = dout(nc, "xout", [JD, 128, TL])
    xn1_s = dint(nc, "xn1_s", [JD, 128, TL])

    xnb = sb(nc, "xnb", [128, JD * TL], BF16)
    wub = sb(nc, "wub", [128, JD * EH], BF16)
    wvb = sb(nc, "wvb", [128, JD * EH], BF16)
    wob = sb(nc, "wob", [128, 16 * D], BF16)
    wtf = sb(nc, "wtf", [128, 1024], F32)
    base_b = sb(nc, "base_b", [128, TL], BF16)
    qq_b = sb(nc, "qq_b", [128, TL], BF16)
    qk_b = sb(nc, "qk_b", [128, TL], BF16)
    cs_sb = sb(nc, "cs_sb", [128, max(TL, EH)], BF16)
    sn_sb = sb(nc, "sn_sb", [128, max(TL, EH)], BF16)
    u_tm = cs_sb
    v_tm = sn_sb
    prod_fm = sb(nc, "prod_fm", [128, 16 * 512], BF16)
    mask_sb = sb(nc, "mask_sb", [128, 128], BF16)
    aind_sb = sb(nc, "aind_sb", [4, 128], BF16)
    b4_sb = sb(nc, "b4_sb", [4, 128], BF16)
    idb = sb(nc, "idb", [128, 128], BF16)
    w7 = sb(nc, "w7", [1, 16], F32)
    w7b = sb(nc, "w7b", [1, 128], BF16)
    lnw_sb = sb(nc, "lnw_sb", [128, 2 * JD], F32)
    bbp = sb(nc, "bbp", [128, 1], F32)
    qaf = sb(nc, "qaf", [128, 8], F32)
    brow = sb(nc, "brow", [1, 2 * EH + D], BF16)
    onescol = sb(nc, "onescol", [128, 1], BF16)
    ones_row = sb(nc, "ones_row", [1, TCW], BF16)
    st_f = sb(nc, "st_f", [1, 4 * TCW], F32)
    st_b = sb(nc, "st_b", [1, 2 * TCW], BF16)
    xb_c = sb(nc, "xb_c", [128, TCW], BF16)
    sq_c = sb(nc, "sq_c", [128, TCW], BF16)
    t_y = sb(nc, "t_y", [128, TCW], F32)
    t_z = sb(nc, "t_z", [128, TCW], F32)
    t_w = sb(nc, "t_w", [128, TCW], F32)
    sG = sb(nc, "sG", [128, 128], F32)
    sGb = sb(nc, "sGb", [128, 128], BF16)
    sS = sb(nc, "sS", [128, 128], BF16)
    pA = ps(nc, "pA", [128, TCW], F32)
    pB = ps(nc, "pB", [128, TCW], F32)
    pC = ps(nc, "pC", [128, TCW], F32)
    pD = ps(nc, "pD", [128, TCW], F32)
    pT = ps(nc, "pT", [128, 128], BF16)

    Ld = Sem(nc, "Ld"); Lw = Sem(nc, "Lw"); St = Sem(nc, "St")
    T = Sem(nc, "T"); V = Sem(nc, "V"); S = Sem(nc, "S")
    v, s, t, g, sy = nc.vector, nc.scalar, nc.tensor, nc.gpsimd, nc.sync

    for dst, src in [(cs_sb[:, 0:TL], costab), (sn_sb[:, 0:TL], sintab), (mask_sb, maskt),
                     (aind_sb, aind), (idb, identb), (lnw_sb[:, 0:JD], lnwb[0]),
                     (lnw_sb[:, JD:2 * JD], lnwb[1]), (bbp, bb_i), (qaf[:, 0:4], qaff)]:
        sy.dma_start(out=dst if not hasattr(dst, "shape") else dst[:] if hasattr(dst, "ap") else dst,
                     in_=src).then_inc(Ld.h, 16)
        Ld.val += 16
    sy.dma_start(out=w7[0:1, 0:7], in_=wrel[:, 508:515]).then_inc(Ld.h, 16)
    Ld.val += 16
    Ld.wait(v)
    v.tensor_scalar_mul(qaf[:, 4:8], qaf[:, 0:4], 0.125)
    v.drain()
    v.memset(onescol[:], 1.0)
    v.memset(ones_row[:], 1.0)
    for r in range(4):
        for b_ in range(4):
            v.tensor_copy(w7b[0:1, b_:b_ + 125:4],
                          w7[0:1, 3 - r + b_:3 - r + b_ + 1].to_broadcast([1, 32]))
        ei = V.inc(v.drain())
        V.wait(g, ei)
        g.wait_ge(St.h, St.val)
        g.dma_start(out=b4_sb[r:r + 1, :].bitcast(BF16) if False else b4_sb[r:r + 1, :],
                    in_=w7b[0:1, :]).then_inc(St.h, 16)
        St.val += 16
        St.wait(v)
    ei = V.inc(v.drain())
    V.wait(t, ei)

    def load_cast(dst, src, cols, koff=0):
        for k in range(JD if cols != 16 else 16):
            pass
    for k in range(JD):
        for hh in range(2):
            Lw.wait(sy); V.wait(sy)
            sy.dma_start(out=wtf[:], in_=wuT[128 * k:128 * (k + 1),
                                            1024 * hh:1024 * (hh + 1)]).then_inc(Lw.h, 16)
            Lw.val += 16
            Lw.wait(v)
            V.inc(v.tensor_copy(wub[:, k * EH + 1024 * hh: k * EH + 1024 * (hh + 1)], wtf[:]))
    for k in range(JD):
        for hh in range(2):
            Lw.wait(sy); V.wait(sy)
            sy.dma_start(out=wtf[:], in_=wvT[128 * k:128 * (k + 1),
                                            1024 * hh:1024 * (hh + 1)]).then_inc(Lw.h, 16)
            Lw.val += 16
            Lw.wait(v)
            V.inc(v.tensor_copy(wvb[:, k * EH + 1024 * hh: k * EH + 1024 * (hh + 1)], wtf[:]))
    for k in range(16):
        Lw.wait(sy); V.wait(sy)
        sy.dma_start(out=wtf[:], in_=woT[128 * k:128 * (k + 1), :]).then_inc(Lw.h, 16)
        Lw.val += 16
        Lw.wait(v)
        V.inc(v.tensor_copy(wob[:, k * D:(k + 1) * D], wtf[:]))
    wbb = sb(nc, "wbb", [128, JD * 128], BF16)
    for k in range(JD):
        Lw.wait(sy); V.wait(sy)
        sy.dma_start(out=wtf[:, 0:128], in_=wbT[128 * k:128 * (k + 1), :]).then_inc(Lw.h, 16)
        Lw.val += 16
        Lw.wait(v)
        V.inc(v.tensor_copy(wbb[:, k * 128:(k + 1) * 128], wtf[:, 0:128]))
    for q, (src, w_) in enumerate([(bu_i, EH), (bv_i, EH), (bo_i, D)]):
        for hh in range(w_ // 1024):
            Lw.wait(sy); V.wait(sy)
            sy.dma_start(out=wtf[0:1, :], in_=src[:, 1024 * hh:1024 * (hh + 1)]).then_inc(Lw.h, 16)
            Lw.val += 16
            Lw.wait(v)
            V.inc(v.tensor_copy(brow[0:1, q * EH + 1024 * hh: q * EH + 1024 * (hh + 1)],
                                wtf[0:1, :]))
    ei = V.inc(v.drain())
    V.wait(t, ei)

    yjs = prod_fm.bitcast(F32)
    for tc in range(NTC):
        c0 = tc * TCW

        def ln_pass(src_getter, eps):
            for j in range(JD):
                xsl = src_getter(j)
                ei = V.inc(v.tensor_copy(xb_c[:], xsl))
                eis = S.inc(s.activation(sq_c[:], xsl, AF.Square))
                V.wait(t, ei); S.wait(t, eis)
                t.matmul(pA[0:1, 0:TCW], onescol[:], xb_c[:], start=(j == 0), stop=(j == JD - 1))
                mm2 = t.matmul(pB[0:1, 0:TCW], onescol[:], sq_c[:], start=(j == 0), stop=(j == JD - 1))
                eit = T.inc(mm2)
                T.wait(v, eit); T.wait(s, eit)
            v.tensor_scalar_mul(st_f[:, 0:TCW], pA[0:1, 0:TCW], 1.0 / D)
            v.drain()
            v.tensor_scalar_mul(st_f[:, TCW:2 * TCW], pB[0:1, 0:TCW], 1.0 / D)
            v.drain()
            v.tensor_mul(st_f[:, 3 * TCW:4 * TCW], st_f[:, 0:TCW], st_f[:, 0:TCW])
            v.drain()
            v.tensor_sub(st_f[:, TCW:2 * TCW], st_f[:, TCW:2 * TCW], st_f[:, 3 * TCW:4 * TCW])
            v.drain()
            v.tensor_scalar_add(st_f[:, TCW:2 * TCW], st_f[:, TCW:2 * TCW], eps)
            ei = V.inc(v.drain())
            V.wait(s, ei)
            ei = S.inc(s.activation(st_f[:, 2 * TCW:3 * TCW], st_f[:, TCW:2 * TCW], AF.Sqrt))
            S.wait(v, ei)
            v.reciprocal(st_f[:, 2 * TCW:3 * TCW], st_f[:, 2 * TCW:3 * TCW])
            v.drain()
            v.tensor_copy(st_b[:, 0:TCW], st_f[:, 0:TCW])
            v.drain()
            ei = V.inc(v.tensor_copy(st_b[:, TCW:2 * TCW], st_f[:, 2 * TCW:3 * TCW]))
            V.wait(t, ei)
            mm1 = t.matmul(pC[:, 0:TCW], ones_row[0:1, 0:128], st_b[:, 0:TCW], start=True, stop=True)
            mm2 = t.matmul(pD[:, 0:TCW], ones_row[0:1, 0:128], st_b[:, TCW:2 * TCW], start=True, stop=True)
            eit = T.inc(mm2)
            T.wait(v, eit)

        V.wait(sy); S.wait(sy); T.wait(sy); Ld.wait(sy)
        for j in range(JD):
            sy.dma_start(out=yjs[:, j * TCW:(j + 1) * TCW],
                         in_=x_i[j][:, c0:c0 + TCW]).then_inc(Ld.h, 16)
        Ld.val += 16 * JD
        Ld.wait(v); Ld.wait(s)
        S.wait(v)
        ln_pass(lambda j: yjs[:, j * TCW:(j + 1) * TCW], 1e-5)
        for j in range(JD):
            v.tensor_sub(t_y[:], yjs[:, j * TCW:(j + 1) * TCW], pC[:, 0:TCW])
            v.drain()
            ei = V.inc(v.tensor_mul(yjs[:, j * TCW:(j + 1) * TCW], t_y[:], pD[:, 0:TCW]))
            v.drain()
            V.wait(g, ei)
            g.wait_ge(St.h, St.val)
            g.dma_start(out=xn1_s[j][:, c0:c0 + TCW],
                        in_=yjs[:, j * TCW:(j + 1) * TCW]).then_inc(St.h, 16)
            St.val += 16
        St.wait(v)
        v.drain()
        V.wait(s); V.wait(t)
        ln_pass(lambda j: yjs[:, j * TCW:(j + 1) * TCW], 1e-5)
        for j in range(JD):
            v.tensor_sub(t_y[:], yjs[:, j * TCW:(j + 1) * TCW], pC[:, 0:TCW])
            v.drain()
            v.tensor_mul(t_z[:], t_y[:], pD[:, 0:TCW])
            v.drain()
            v.tensor_scalar(t_z[:], t_z[:], lnw_sb[:, j:j + 1], lnw_sb[:, JD + j:JD + j + 1],
                            ALU.mult, ALU.add)
            v.drain()
            v.tensor_copy(xnb[:, j * TL + c0: j * TL + c0 + TCW], t_z[:])
            v.drain()
        ei = V.inc(v.drain())
        V.wait(t, ei)

    if _os.environ.get("HDBG") == "1":
        for j in range(JD):
            V.wait(g, V.inc(v.tensor_copy(t_y[:, 0:TCW], xnb[:, j * TL: j * TL + TCW])))
            g.wait_ge(St.h, St.val)
            g.dma_start(out=xout_o[j][:, 0:TCW], in_=t_y[:, 0:TCW]).then_inc(St.h, 16)
            St.val += 16
            St.wait(v)
        St.wait(g)
        return nc
    for tc in range(NTC):
        c0 = tc * TCW
        for k in range(JD):
            mm = t.matmul(pA[:, 0:TCW], wbb[:, k * 128:(k + 1) * 128],
                          xnb[:, k * TL + c0: k * TL + c0 + TCW],
                          start=(k == 0), stop=(k == JD - 1))
        eit = T.inc(mm)
        T.wait(s, eit)
        ei = S.inc(s.activation(base_b[:, c0:c0 + TCW], pA[:, 0:TCW], AF.Silu, bias=bbp[:]))
        S.wait(t, ei); S.wait(v, ei)
    for (dst, wq, bq2) in [(qq_b, 4, 5), (qk_b, 6, 7)]:
        v.tensor_scalar(t_y[:, 0:128] if False else dst[:], base_b[:],
                        qaf[:, wq:wq + 1], qaf[:, bq2:bq2 + 1], ALU.mult, ALU.add)
        v.drain()
    for (dst,) in [(qq_b,), (qk_b,)]:
        for tc in range(NTC):
            c0 = tc * TCW
            v.tensor_copy(t_w[0:64, 0:TCW], dst[64:128, c0:c0 + TCW])
            v.drain()
            v.tensor_copy(t_w[64:128, 0:TCW], dst[0:64, c0:c0 + TCW])
            v.drain()
            v.tensor_mul(t_y[0:64, 0:TCW], dst[0:64, c0:c0 + TCW], cs_sb[0:64, c0:c0 + TCW])
            v.drain()
            v.tensor_mul(t_z[0:64, 0:TCW], t_w[0:64, 0:TCW], sn_sb[0:64, c0:c0 + TCW])
            v.drain()
            v.tensor_mul(t_y[64:128, 0:TCW], dst[64:128, c0:c0 + TCW], cs_sb[64:128, c0:c0 + TCW])
            v.drain()
            v.tensor_mul(t_z[64:128, 0:TCW], t_w[64:128, 0:TCW], sn_sb[64:128, c0:c0 + TCW])
            v.drain()
            v.tensor_sub(dst[0:64, c0:c0 + TCW], t_y[0:64, 0:TCW], t_z[0:64, 0:TCW])
            v.drain()
            v.tensor_add(dst[64:128, c0:c0 + TCW], t_y[64:128, 0:TCW], t_z[64:128, 0:TCW])
            v.drain()
    ei = V.inc(v.drain())
    V.wait(t, ei)

    if _os.environ.get("HDBG") == "2":
        for (nm, buf) in [(0, base_b), (1, qq_b), (2, qk_b)]:
            V.wait(g, V.inc(v.tensor_copy(t_y[:, 0:TCW], buf[:, 0:TCW])))
            g.wait_ge(St.h, St.val)
            g.dma_start(out=xout_o[nm][:, 0:TCW], in_=t_y[:, 0:TCW]).then_inc(St.h, 16)
            St.val += 16
            St.wait(v)
        St.wait(g)
        return nc
    for tg in range(NG):
        for ti in range(4):
            pt0 = (tg * 4 + ti) * 128
            for (dstb, wsrc, bq) in [(u_tm, wub, 0), (v_tm, wvb, 1)]:
                for ec in range(4):
                    for k in range(JD):
                        t.matmul(pA[:, 0:TCW],
                                 xnb[:, k * TL + pt0: k * TL + pt0 + 128],
                                 wsrc[:, k * EH + 512 * ec: k * EH + 512 * (ec + 1)],
                                 start=(k == 0), stop=False)
                    mm = t.matmul(pA[:, 0:TCW], ones_row[0:1, 0:128],
                                  brow[0:1, bq * EH + 512 * ec: bq * EH + 512 * (ec + 1)],
                                  start=False, stop=True)
                    eit = T.inc(mm)
                    T.wait(s, eit)
                    ei = S.inc(s.activation(dstb[:, 512 * ec:512 * (ec + 1)], pA[:, 0:TCW], AF.Silu))
                    S.wait(t, ei)
            t.matmul(pA[:, 0:128], qq_b[:, pt0:pt0 + 128], qk_b[:, pt0:pt0 + 128],
                     start=True, stop=False)
            mm = t.matmul(pA[:, 0:128], aind_sb[0:4, :], b4_sb[0:4, :], start=False, stop=True)
            eit = T.inc(mm)
            T.wait(v, eit)
            v.tensor_relu(sG[:], pA[:, 0:128])
            ei = V.inc(v.drain())
            V.wait(s, ei); V.wait(t, ei)
            ei = S.inc(s.activation(sG[:], sG[:], AF.Square))
            S.wait(v, ei)
            ei = V.inc(v.tensor_mul(sGb[:], sG[:], mask_sb[:]))
            V.wait(t, ei)
            mm = t.transpose(pT[:, 0:128], sGb[:], idb[:])
            eit = T.inc(mm)
            T.wait(v, eit)
            ei = V.inc(v.tensor_copy(sS[:], pT[:, 0:128]))
            V.wait(t, ei)
            for ec in range(4):
                mm = t.matmul(pB[:, 0:TCW], sS[:], v_tm[:, 512 * ec:512 * (ec + 1)],
                              start=True, stop=True)
                eit = T.inc(mm)
                T.wait(v, eit)
                ei = V.inc(v.tensor_mul(xb_c[:], u_tm[:, 512 * ec:512 * (ec + 1)], pB[:, 0:TCW]))
                V.wait(t, ei)
                for q4 in range(4):
                    mm = t.transpose(pT[:, 0:128], xb_c[:, 128 * q4:128 * (q4 + 1)], idb[:])
                    eit = T.inc(mm)
                    T.wait(v, eit)
                    ei = V.inc(v.tensor_copy(
                        prod_fm[:, (ec * 4 + q4) * 512 + ti * 128:(ec * 4 + q4) * 512 + ti * 128 + 128],
                        pT[:, 0:128]))
                    V.wait(t, ei)
                    V.wait(s, ei) if False else None
        if _os.environ.get("HDBG") == "3" and tg == 0:
            for (nm, buf, wd) in [(0, u_tm, EH), (1, v_tm, EH), (2, sS, 128), (3, prod_fm, 512)]:
                wd2 = min(wd, TCW)
                V.wait(g, V.inc(v.tensor_copy(t_y[:, 0:wd2], buf[:, 0:wd2])))
                g.wait_ge(St.h, St.val)
                g.dma_start(out=xout_o[nm][:, 0:wd2], in_=t_y[:, 0:wd2]).then_inc(St.h, 16)
                St.val += 16
                St.wait(v)
            St.wait(g)
            return nc
        gc0 = tg * 512
        for m8 in range(JD):
            for ke in range(16):
                t.matmul(pC[:, 0:TCW], wob[:, ke * D + 128 * m8: ke * D + 128 * (m8 + 1)],
                         prod_fm[:, ke * 512:(ke + 1) * 512],
                         start=(ke == 0), stop=False)
            mm = t.matmul(pC[:, 0:TCW], ones_row[0:1, 0:TCW] if False else brow[0:1, 2 * EH + 128 * m8: 2 * EH + 128 * (m8 + 1)],
                          ones_row[0:1, 0:TCW], start=False, stop=True)
            eit = T.inc(mm)
            T.wait(sy, eit)
            V.wait(sy)
            sy.dma_start(out=t_w[:], in_=x_i[m8][:, gc0:gc0 + TCW]).then_inc(Ld.h, 16)
            Ld.val += 16
            sy.dma_start(out=t_z[:], in_=xn1_s[m8][:, gc0:gc0 + TCW]).then_inc(Ld.h, 16)
            Ld.val += 16
            T.wait(v, eit)
            Ld.wait(v)
            v.tensor_add(t_y[:], pC[:, 0:TCW], t_w[:])
            v.drain()
            ei = V.inc(v.tensor_add(t_y[:], t_y[:], t_z[:]))
            V.wait(t, ei)
            V.wait(g, ei)
            g.wait_ge(St.h, St.val)
            g.dma_start(out=xout_o[m8][:, gc0:gc0 + TCW], in_=t_y[:]).then_inc(St.h, 16)
            St.val += 16
            St.wait(v)
    St.wait(g)
    return nc



_PROGS = {}


def _get_progs(TLOC):
    if "A" not in _PROGS:
        _PROGS["A"] = build_prog_A(TLOC)
        _PROGS["B"] = build_prog_B(TLOC)
        _PROGS["C"] = build_prog_C(TLOC)
        _PROGS["D"] = build_prog_D(TLOC)
        _PROGS["E"] = build_prog_E(NCORES * TLOC, NCORES)
        _PROGS["F"] = build_prog_F(TLOC)
        _PROGS["H"] = build_prog_H(TLOC)
        _PROGS["I"] = build_prog_I(TLOC)
        _PROGS["J"] = build_prog_J(TLOC)
    return _PROGS


def _from_fm(xo, TLOC):
    return np.asarray(xo).reshape(D, TLOC, B).transpose(2, 1, 0)


def kernel(**inp):
    TLOC = 512
    TL = B * TLOC
    P = _get_progs(TLOC)
    run = lambda nc_, ins: run_bass_kernel_spmd(nc_, ins, core_ids=list(range(NCORES))).results

    vec = lambda a: np.ascontiguousarray(np.asarray(a, np.float32).reshape(-1))
    x0 = np.asarray(inp["L0_attn_a_x"], np.float32)

    mixes = [vec(inp[f"L0_attn_a_x_{m}"]) for m in "rwkvag"]
    wlist = [np.asarray(inp[f"L0_attn_a_W_{m}"], np.float32) for m in "rwkvag"]
    insA = [host_prep_A(x0, mixes, wlist, c, TLOC) for c in range(NCORES)]
    outA = run(P["A"], insA)
    insB = [host_prep_B(outA, vec(inp["L0_attn_a_gn_w"]), vec(inp["L0_attn_a_gn_b"]),
                        np.asarray(inp["L0_attn_a_W_out"], np.float32),
                        vec(inp["L0_attn_a_b_out"]), c, TLOC) for c in range(NCORES)]
    outB = run(P["B"], insB)
    X = [np.asarray(o["xout"]) for o in outB]

    statC = host_prep_C(None, np.asarray(inp["L0_ffn_gate_w"], np.float32),
                        vec(inp["L0_ffn_gate_b"]),
                        np.asarray(inp["L0_ffn_w1"], np.float32),
                        np.asarray(inp["L0_ffn_b1"], np.float32),
                        np.asarray(inp["L0_ffn_w2"], np.float32),
                        np.asarray(inp["L0_ffn_b2"], np.float32), TLOC)
    insC = []
    for c in range(NCORES):
        d = dict(statC); d["x_i"] = X[c]; insC.append(d)
    outC = run(P["C"], insC)
    X = [np.asarray(o["xout"]) for o in outC]

    insD = [{
        "x_i": X[c],
        "winT": np.ascontiguousarray(np.asarray(inp["L0_attn_b_W_in"], np.float32).T),
        "b_in": vec(inp["L0_attn_b_b_in"]).reshape(1, -1),
        "b_h": vec(inp["L0_attn_b_b_h"]).reshape(1, -1),
        "lnwb": np.stack([vec(inp["L0_attn_b_ln_w"]).reshape(JD, 128).T,
                          vec(inp["L0_attn_b_ln_b"]).reshape(JD, 128).T]).astype(np.float32),
    } for c in range(NCORES)]
    outD = run(P["D"], insD)
    g_all = np.stack([np.asarray(o["g_o"]) for o in outD])
    vb_all = np.stack([np.asarray(o["vb_o"]) for o in outD])
    insE = {"g_i": g_all, "vb_i": vb_all,
            "whT": np.ascontiguousarray(np.asarray(inp["L0_attn_b_W_h"], np.float32).T)}
    outE = run(P["E"], [insE] * NCORES)
    hs_full = np.asarray(outE[0]["hs_o"])
    insF = [{
        "x_i": X[c], "xn2_i": np.asarray(outD[c]["xn2_o"]),
        "hs_i": np.ascontiguousarray(hs_full[:, :, c * TL:(c + 1) * TL]),
        "woutT": np.ascontiguousarray(np.asarray(inp["L0_attn_b_W_out"], np.float32).T),
        "bout": vec(inp["L0_attn_b_b_out"]).reshape(1, -1),
    } for c in range(NCORES)]
    outF = run(P["F"], insF)
    X = [np.asarray(o["xout"]) for o in outF]

    insC2 = []
    for c in range(NCORES):
        d = dict(statC); d["x_i"] = X[c]; insC2.append(d)
    outC2 = run(P["C"], insC2)
    X = [np.asarray(o["xout"]) for o in outC2]

    invf = 10000.0 ** (-np.arange(64, dtype=np.float32) / 64)
    pos_col = np.tile(np.arange(B, dtype=np.float32), TLOC)
    sinus = pos_col[None, :] * np.concatenate([invf, invf])[:, None]
    costab = np.cos(sinus).astype(BF)
    sintab = np.sin(sinus).astype(BF)
    maskt = np.zeros((128, 128), BF)
    for p_ in range(128):
        for q_ in range(128):
            if p_ // 4 == q_ // 4 and q_ % 4 <= p_ % 4:
                maskt[p_, q_] = 1
    aind = np.zeros((4, 128), BF)
    for r_ in range(4):
        aind[r_, r_::4] = 1
    statH = {
        "lnwb": np.stack([vec(inp["L1_attn_a_ln_g"]).reshape(JD, 128).T,
                          vec(inp["L1_attn_a_ln_b"]).reshape(JD, 128).T]).astype(np.float32),
        "wuT": np.ascontiguousarray(np.asarray(inp["L1_attn_a_Wu"], np.float32).T),
        "bu": vec(inp["L1_attn_a_bu"]).reshape(1, -1),
        "wvT": np.ascontiguousarray(np.asarray(inp["L1_attn_a_Wv"], np.float32).T),
        "bv": vec(inp["L1_attn_a_bv"]).reshape(1, -1),
        "wbT": np.ascontiguousarray(np.asarray(inp["L1_attn_a_Wb"], np.float32).T),
        "bb": vec(inp["L1_attn_a_bb"]).reshape(128, 1),
        "qaff": np.stack([vec(inp["L1_attn_a_qqw"]), vec(inp["L1_attn_a_qqb"]),
                          vec(inp["L1_attn_a_qkw"]), vec(inp["L1_attn_a_qkb"])],
                         axis=1).astype(np.float32),
        "woT": np.ascontiguousarray(np.asarray(inp["L1_attn_a_Wo"], np.float32).T),
        "bo": vec(inp["L1_attn_a_bo"]).reshape(1, -1),
        "wrel": vec(inp["L1_attn_a_w_rel"]).reshape(1, -1),
        "costab": costab, "sintab": sintab, "maskt": maskt, "aind": aind,
        "identb": np.eye(128, dtype=BF),
    }
    insH = []
    for c in range(NCORES):
        d = dict(statH); d["x_i"] = X[c]; insH.append(d)
    outH = run(P["H"], insH)
    X = [np.asarray(o["xout"]) for o in outH]

    statI = {
        "gw": np.asarray(inp["L1_ffn_gw"], np.float32),
        "gb": vec(inp["L1_ffn_gb"]).reshape(1, -1),
        "w1": np.asarray(inp["L1_ffn_w1"], np.float32),
        "b1": np.asarray(inp["L1_ffn_b1"], np.float32),
        "w3": np.asarray(inp["L1_ffn_w3"], np.float32),
        "b3": np.asarray(inp["L1_ffn_b3"], np.float32),
        "w2": np.asarray(inp["L1_ffn_w2"], np.float32),
        "b2": np.asarray(inp["L1_ffn_b2"], np.float32),
        "sw1": np.asarray(inp["L1_ffn_sw1"], np.float32),
        "sb1": vec(inp["L1_ffn_sb1"]).reshape(1, -1),
        "sw3": np.asarray(inp["L1_ffn_sw3"], np.float32),
        "sb3": vec(inp["L1_ffn_sb3"]).reshape(1, -1),
        "sw2": np.asarray(inp["L1_ffn_sw2"], np.float32),
        "sb2": vec(inp["L1_ffn_sb2"]).reshape(1, -1),
        "identf": np.eye(128, dtype=np.float32),
    }
    insI = []
    for c in range(NCORES):
        d = dict(statI); d["x_i"] = X[c]; insI.append(d)
    outI = run(P["I"], insI)
    X = [np.asarray(o["xout"]) for o in outI]

    statJ = {
        "Wk": np.asarray(inp["L1_attn_b_Wk"], np.float32), "bk": vec(inp["L1_attn_b_bk"]).reshape(1, -1),
        "Wv": np.asarray(inp["L1_attn_b_Wv"], np.float32), "bv": vec(inp["L1_attn_b_bv"]).reshape(1, -1),
        "Wq": np.asarray(inp["L1_attn_b_Wq"], np.float32), "bq": vec(inp["L1_attn_b_bq"]).reshape(1, -1),
        "Wkp": np.asarray(inp["L1_attn_b_Wkp"], np.float32), "bkp": vec(inp["L1_attn_b_bkp"]).reshape(1, -1),
        "Wqp": np.asarray(inp["L1_attn_b_Wqp"], np.float32), "bqp": vec(inp["L1_attn_b_bqp"]).reshape(1, -1),
        "Wo": np.asarray(inp["L1_attn_b_Wo"], np.float32), "bo": vec(inp["L1_attn_b_bo"]).reshape(1, -1),
        "phs": np.ascontiguousarray(vec(inp["L1_attn_b_phase_scale"]).reshape(JD, 128).T),
        "lnwb": np.stack([vec(inp["L1_attn_b_ln_g"]).reshape(JD, 128).T,
                          vec(inp["L1_attn_b_ln_b"]).reshape(JD, 128).T]).astype(np.float32),
    }
    insJ = []
    for c in range(NCORES):
        d = dict(statJ); d["x_i"] = X[c]; insJ.append(d)
    outJ = run(P["J"], insJ)
    X = [np.asarray(o["xout"]) for o in outJ]

    insI2 = []
    for c in range(NCORES):
        d = dict(statI); d["x_i"] = X[c]; insI2.append(d)
    outI2 = run(P["I"], insI2)
    X = [np.asarray(o["xout"]) for o in outI2]

    out = np.empty((B, NCORES * TLOC, D), np.float32)
    for c in range(NCORES):
        out[:, c * TLOC:(c + 1) * TLOC, :] = _from_fm(X[c], TLOC)
    return out
